# version 89
# baseline (speedup 1.0000x reference)
import contextlib
import numpy as np
import ml_dtypes
import concourse.bass as bass
import concourse.mybir as mybir
from concourse.bass_utils import run_bass_kernel_spmd

F32 = mybir.dt.float32
BF16 = mybir.dt.bfloat16
AF = mybir.ActivationFunctionType
OP = mybir.AluOpType
AX = mybir.AxisListType.X

B, T, C = 4, 4096, 1024
H = 16
NT = 32
OWN = 16
OWN_TOK = OWN * 128
E = 8
SCALE = 0.125
EPS = 1e-5
VW = 65
NB = 6
ENGINES = ("pe", "act", "dve", "pool", "sp")


def build_masks(half):
    s = np.arange(128)[:, None]
    t = np.arange(128)[None, :]
    m0 = np.isin(t - s, [0, 1, 2, 4, 8, 16, 32, 64]).astype(np.float32)
    m1 = np.isin(t - s + 128, [1, 2, 4, 8, 16, 32, 64, 128]).astype(np.float32)
    mD = np.eye(128, dtype=np.float32)
    out = np.zeros((OWN, 128, NB * 128), dtype=np.float32)
    for i in range(OWN):
        vi = 16 + i
        for idx, d in enumerate((16, 8, 4, 2, 1, 0)):
            vb = vi - d
            real = vb - 16 + 16 * half
            if real >= 0:
                kind = m0 if d == 0 else (m1 if d == 1 else mD)
                out[i, :, 128 * idx:128 * (idx + 1)] = kind
    return out.astype(ml_dtypes.bfloat16)


class KB:

    def __init__(self, nc):
        self.nc = nc
        self.items = {e: [] for e in ENGINES}
        self.stack = contextlib.ExitStack()
        self.sems = {}
        self.cnt = {}
        self.hwm = {e: {} for e in ENGINES}
        for s in ("pe", "act", "dve", "pool"):
            self.sems[s] = self.stack.enter_context(nc.semaphore(s))
            self.cnt[s] = 0

    def wait(self, eng, sem, val):
        if val <= 0 or self.hwm[eng].get(sem, 0) >= val:
            return
        self.hwm[eng][sem] = val
        s = self.sems[sem]
        self.items[eng].append(lambda EE, s=s, v=val: EE.wait_ge(s, v))

    def emit(self, eng, fn, waits=()):
        if eng in ("dve", "act", "pool"):
            self.wait(eng, eng, self.cnt[eng])
        for w in waits:
            if w:
                self.wait(eng, w[0], w[1])
        self.cnt[eng] += 1
        s = self.sems[eng]
        self.items[eng].append(lambda EE, fn=fn, s=s: fn(EE).then_inc(s, 1))
        return (eng, self.cnt[eng])

    def dma(self, eng, stream, fn, waits=()):
        if stream not in self.sems:
            self.sems[stream] = self.stack.enter_context(self.nc.semaphore(stream))
            self.cnt[stream] = 0
        self.wait(eng, stream, self.cnt[stream])
        for w in waits:
            if w:
                self.wait(eng, w[0], w[1])
        self.cnt[stream] += 16
        s = self.sems[stream]
        self.items[eng].append(lambda EE, fn=fn, s=s: fn(EE).then_inc(s, 16))
        return (stream, self.cnt[stream])

    def barrier(self):
        toks = [(s, self.cnt[s]) for s in self.cnt]
        for e in ENGINES:
            for sem, val in toks:
                self.wait(e, sem, val)

    def replay(self):
        with self.nc.Block() as block:
            @block.tensor
            def _(t):
                for f in self.items["pe"]:
                    f(t)

            @block.scalar
            def _(s):
                for f in self.items["act"]:
                    f(s)

            @block.vector
            def _(v):
                for f in self.items["dve"]:
                    f(v)

            @block.gpsimd
            def _(g):
                for f in self.items["pool"]:
                    f(g)

            @block.sync
            def _(sy):
                for f in self.items["sp"]:
                    f(sy)


def build_program():
    nc = bass.Bass("TRN2", target_bir_lowering=False)
    dt = nc.dram_tensor
    x_d = dt("x", [T, C], F32, kind="ExternalInput")
    wqkv_d = dt("wqkv", [C, 3 * C], F32, kind="ExternalInput")
    bqkv_d = dt("bqkv", [3 * C], F32, kind="ExternalInput")
    wproj_d = dt("wproj", [C, C], F32, kind="ExternalInput")
    bproj_d = dt("bproj", [C], F32, kind="ExternalInput")
    wr_d = dt("wr", [C, E], F32, kind="ExternalInput")
    wi_d = dt("wi", [E, C, C], F32, kind="ExternalInput")
    wo_d = dt("wo", [E, C, C], F32, kind="ExternalInput")
    idn_d = dt("idn", [128, 128], F32, kind="ExternalInput")
    mk_d = dt("mk", [OWN, 128, NB * 128], BF16, kind="ExternalInput")
    out_d = dt("out", [OWN_TOK, C], F32, kind="ExternalOutput")
    m_d = dt("m_sc", [E, C, C], BF16)
    va_d = dt("va_sc", [NT, 128, H * VW], BF16)
    kt_d = dt("kt_sc", [8, 128, T], BF16)
    q_d = dt("q_sc", [8, 128, OWN_TOK], BF16)
    x1_d = dt("x1_sc", [OWN_TOK, C], F32)

    kb = KB(nc)
    st = kb.stack
    T0 = OWN_TOK

    idb = st.enter_context(nc.sbuf_tensor("idb", [128, 128], BF16))
    bqkT = st.enter_context(nc.sbuf_tensor("bqkT", [128, 16], F32))
    wr_sb = st.enter_context(nc.sbuf_tensor("wr_sb", [128, E * E], BF16))
    epsb = st.enter_context(nc.sbuf_tensor("epsb", [128, 1], F32))
    ones_row = st.enter_context(nc.sbuf_tensor("ones_row", [1, 128], F32))
    bp_row = st.enter_context(nc.sbuf_tensor("bp_row", [1, C], F32))

    tok_onr = kb.emit("dve", lambda e: e.memset(ones_row[:, :], 1.0))
    tok_bpr = kb.dma("sp", "ini", lambda e: e.dma_start(
        out=bp_row[0:1, :], in_=bproj_d.ap().rearrange("(a n) -> a n", a=1)))

    def build_bias_row(dst, ps_a, ps_b, row, t_r):
        one, t_o = ones_row, tok_onr
        toks = []
        for n in range(2):
            ps = ps_a if n == 0 else ps_b
            t_mm = kb.emit("pe", lambda e, ps=ps, n=n: e.matmul(
                ps[:, :], one[:, :], row[0:1, 512 * n: 512 * (n + 1)],
                start=True, stop=True),
                waits=[t_r, t_o])
            toks.append(kb.emit("dve", lambda e, ps=ps, n=n: e.tensor_copy(
                dst[:, 512 * n: 512 * (n + 1)], ps[:, :]), waits=[t_mm]))
        return toks[-1]

    tok_eps = kb.emit("dve", lambda e: e.memset(epsb[:, :], EPS))
    tok_idb = kb.dma("pool", "pini", lambda e: e.dma_start(out=idb[:, :], in_=idn_d[:, :]))
    tok_wr = kb.dma("pool", "pini", lambda e: e.dma_start(
        out=wr_sb[:, :].rearrange("p (c n) -> p c n", n=E),
        in_=wr_d.ap().rearrange("(c p) n -> p c n", p=128)))
    with nc.sbuf_tensor("idf", [16, 16], F32) as idf, \
         nc.sbuf_tensor("rowbuf", [16, 128], F32) as rowbuf:
        tok_id = kb.dma("sp", "ini", lambda e: e.dma_start(
            out=idf[:, :], in_=idn_d[0:16, 0:16]))
        with nc.psum_tensor("pbias", [128, 512], F32) as pbias:
            tok_rb = kb.dma("sp", "ini", lambda e: e.dma_start(
                out=rowbuf[:, :],
                in_=bqkv_d.ap().rearrange("(c p) -> c p", p=128)[0:16, :]))
            tok_tr = kb.emit("pe", lambda e: e.transpose(
                pbias[:, 0:16], rowbuf[:, :], idf[:, :]),
                waits=[tok_rb, tok_id])
            tok_bq = kb.emit("dve", lambda e: e.tensor_copy(bqkT[:, :], pbias[:, 0:16]),
                             waits=[tok_tr])
    kb.barrier()

    with nc.sbuf_tensor("xnT", [128, 8 * T], BF16) as xnT, \
         nc.sbuf_tensor("xld", [128, 2 * C], F32) as xld, \
         nc.sbuf_tensor("xc", [128, 2 * C], F32) as xc, \
         nc.sbuf_tensor("stats", [128, 16], F32) as stats, \
         nc.sbuf_tensor("xnbf", [128, 6 * C], BF16) as xnbf:
        with nc.sbuf_tensor("wiT", [128, 2 * E * C], BF16) as wiT, \
             nc.sbuf_tensor("wof", [128, 2 * 4 * C], F32) as wof, \
             nc.sbuf_tensor("wo_sb", [128, 2 * E * C], BF16) as wo_sb, \
             nc.sbuf_tensor("m_st", [128, 3 * C], BF16) as m_st, \
             nc.psum_tensor("pm0", [128, 512], F32) as pm0, \
             nc.psum_tensor("pm1", [128, 512], F32) as pm1, \
             nc.psum_tensor("pm2", [128, 512], F32) as pm2, \
             nc.psum_tensor("pm3", [128, 512], F32) as pm3, \
             nc.psum_tensor("pm4", [128, 512], F32) as pm4, \
             nc.psum_tensor("pm5", [128, 512], F32) as pm5, \
             nc.psum_tensor("ptr0", [128, 1024], BF16) as ptr0, \
             nc.psum_tensor("ptr1", [128, 1024], BF16) as ptr1:
            pms = [(pm0, pm1), (pm2, pm3), (pm4, pm5)]
            ptrs = [ptr0, ptr1]
            wi_tok = [None] * E
            wi_user = [None, None]
            wo_tok_l = [None] * E
            wof_cast = [None, None]
            last_mm_buf = [None, None]
            mst_free = [None, None, None]
            pm_free = [None, None, None]
            witr_ev = [None, None]
            wisb_user = [None, None]
            wo_cast = [None]

            def emit_wi_load(e_n):
                nbuf = e_n % 2
                wfree = [wi_user[nbuf]] if wi_user[nbuf] else []
                toks = []
                for hf in range(2):
                    toks.append(kb.dma("pool", f"wi{hf}", lambda e, e_n=e_n, nbuf=nbuf, hf=hf: e.dma_start(
                        out=wiT[:, nbuf * E * C + 4096 * hf: nbuf * E * C + 4096 * (hf + 1)].rearrange(
                            "p (c n) -> p c n", n=C),
                        in_=wi_d[e_n, 512 * hf: 512 * (hf + 1), :].rearrange(
                            "(c p) n -> p c n", p=128)),
                        waits=wfree))
                wi_tok[e_n] = toks

            def emit_wo_load(e_n):
                nbuf = e_n % 2
                wW2 = [last_mm_buf[nbuf]] if last_mm_buf[nbuf] else []
                tok_c = None
                for hf in range(2):
                    wfree3 = [wof_cast[hf]] if wof_cast[hf] else []
                    t_l = kb.dma("sp", f"wof{hf}", lambda e, e_n=e_n, hf=hf: e.dma_start(
                        out=wof[:, hf * 4 * C:(hf + 1) * 4 * C].rearrange(
                            "p (c n) -> p c n", n=C),
                        in_=wo_d[e_n, 512 * hf:512 * (hf + 1), :].rearrange(
                            "(c p) n -> p c n", p=128)),
                        waits=wfree3)
                    tok_c = kb.emit("act", lambda e, nbuf=nbuf, hf=hf: e.copy(
                        wo_sb[:, nbuf * E * C + 4096 * hf: nbuf * E * C + 4096 * (hf + 1)],
                        wof[:, hf * 4 * C:(hf + 1) * 4 * C]), waits=[t_l] + wW2)
                    wof_cast[hf] = tok_c
                wo_tok_l[e_n] = tok_c

            def emit_m_expert(e_i, fillers=()):
                buf = e_i % 2
                off = buf * E * C
                offw = (e_i % 2) * E * C
                fillers = list(fillers)
                if e_i + 1 < E and wo_tok_l[e_i + 1] is None:
                    emit_wo_load(e_i + 1)
                wW = [last_mm_buf[buf]] if last_mm_buf[buf] else []
                wo_tok = wo_tok_l[e_i]
                if e_i + 1 < E:
                    emit_wi_load(e_i + 1)
                wiT_tok = wi_tok[e_i]
                for dc in range(8):
                    pi = (e_i * 8 + dc) % 3
                    pp = pms[pi]
                    wfree = [pm_free[pi]] if pm_free[pi] else []
                    evac_toks = []
                    tok = None
                    for n in range(2):
                        for hc in range(8):
                            tok = kb.emit("pe", lambda e, pp=pp, n=n, hc=hc, dc=dc, off=off, offw=offw: e.matmul(
                                pp[n][:, :],
                                wiT[:, offw + 1024 * hc + 128 * dc: offw + 1024 * hc + 128 * (dc + 1)],
                                wo_sb[:, off + 1024 * hc + 512 * n: off + 1024 * hc + 512 * (n + 1)],
                                start=(hc == 0), stop=(hc == 7)),
                                waits=[wo_tok] + wiT_tok + wfree)
                        mb = pi
                        wfree2 = [mst_free[mb]] if mst_free[mb] else []
                        evac_toks.append(kb.emit("dve", lambda e, pp=pp, n=n, mb=mb: e.tensor_copy(
                            m_st[:, mb * C + 512 * n: mb * C + 512 * (n + 1)], pp[n][:, :]),
                            waits=[tok] + wfree2))
                    last_mm_buf[buf] = tok
                    wi_user[e_i % 2] = tok
                    pm_free[pi] = evac_toks[-1]
                    mb = pi
                    mst_free[mb] = kb.dma("sp", f"mst{mb}", lambda e, e_i=e_i, dc=dc, mb=mb: e.dma_start(
                        out=m_d[e_i, 128 * dc: 128 * (dc + 1), :],
                        in_=m_st[:, mb * C: (mb + 1) * C]), waits=evac_toks)
                    if dc % 2 == 1 and fillers:
                        fillers.pop(0)()

            xld_free = [None, None]
            norm_of = {}
            tr_evac = [None, None]
            ln_rstd = [None, None]
            prev_tr = [None] * 6
            xnT_last = [None]

            def emit_ln_ladder(it):
                bufx = it % 2
                bufn = it % 6
                wfree = [xld_free[bufx]] if xld_free[bufx] else []
                tok_x = kb.dma("sp", f"x{bufx}", lambda e, it=it, bufx=bufx: e.dma_start(
                    out=xld[:, bufx * C:(bufx + 1) * C],
                    in_=x_d[128 * it:128 * (it + 1), :]), waits=wfree)
                xv = xld[:, bufx * C:(bufx + 1) * C]
                xcv = xc[:, bufx * C:(bufx + 1) * C]
                s0, s1, s2, s3, s4 = (8 * bufx + k for k in range(5))
                tok_sum = kb.emit("act", lambda e, xv=xv, xcv=xcv, s0=s0: e.activation(
                    xcv, xv, AF.Copy, bias=0.0, scale=1.0,
                    accum_out=stats[:, s0:s0 + 1]),
                    waits=[tok_x] +
                          ([ln_rstd[bufx]] if ln_rstd[bufx] else []) +
                          ([norm_of[it - 2]] if it >= 2 else []))
                kb.emit("dve", lambda e, s0=s0, s1=s1: e.tensor_scalar(
                    stats[:, s1:s1 + 1], stats[:, s0:s0 + 1], -1.0 / C, None, OP.mult),
                    waits=[tok_sum])
                tok_xc = kb.emit("dve", lambda e, xv=xv, xcv=xcv, s1=s1: e.tensor_scalar(
                    xcv, xv, stats[:, s1:s1 + 1], None, OP.add))
                tok_var = kb.emit("dve", lambda e, xv=xv, xcv=xcv, s2=s2: e.scalar_tensor_tensor(
                    xv, xcv, 1.0, xv, OP.mult, OP.mult,
                    accum_out=stats[:, s2:s2 + 1]), waits=[tok_xc])
                xld_free[bufx] = tok_var
                kb.emit("act", lambda e, s2=s2, s3=s3: e.activation(
                    stats[:, s3:s3 + 1], stats[:, s2:s2 + 1], AF.Ln, bias=epsb[:, 0:1],
                    scale=1.0 / C), waits=[tok_var, tok_eps])
                ln_rstd[bufx] = kb.emit("act", lambda e, s3=s3, s4=s4: e.activation(
                    stats[:, s4:s4 + 1], stats[:, s3:s3 + 1], AF.Exp, bias=0.0,
                    scale=-0.5))
                tok_xn = kb.emit("dve", lambda e, bufn=bufn, xcv=xcv, s4=s4: e.tensor_scalar(
                    xnbf[:, bufn * C:(bufn + 1) * C], xcv, stats[:, s4:s4 + 1],
                    None, OP.mult),
                    waits=[ln_rstd[bufx], tok_var] +
                          ([prev_tr[bufn]] if prev_tr[bufn] else []))
                norm_of[it] = tok_xn

            def emit_ln_tr(it):
                bufn = it % 6
                bt = it % 2
                wfree2 = [tr_evac[bt]] if tr_evac[bt] else []
                tok_tr = None
                for c in range(8):
                    tok_tr = kb.emit("pe", lambda e, c=c, bt=bt, bufn=bufn: e.transpose(
                        ptrs[bt][:, 128 * c:128 * (c + 1)],
                        xnbf[:, bufn * C + 128 * c: bufn * C + 128 * (c + 1)],
                        idb[:, :]),
                        waits=([norm_of[it], tok_idb] + wfree2) if c == 0 else ())
                tr_evac[bt] = kb.emit("act", lambda e, it=it, bt=bt: e.copy(
                    xnT[:, :].rearrange("p (a n) -> p a n", n=4096)[
                        :, :, 128 * it:128 * (it + 1)],
                    ptrs[bt][:, :].rearrange("p (a n) -> p a n", n=128)),
                    waits=[tok_tr])
                prev_tr[bufn] = tok_tr
                xnT_last[0] = tr_evac[bt]

            emit_wi_load(0)
            emit_wo_load(0)

            for e_i in range(E):
                for it in range(4 * e_i, 4 * e_i + 4):
                    emit_ln_ladder(it)
                emit_m_expert(e_i)
                for it in range(4 * e_i, 4 * e_i + 4):
                    emit_ln_tr(it)
            xnT_all = xnT_last[0]

        kb.barrier()
        with nc.sbuf_tensor("wq", [128, 2 * 4096], BF16) as wq, \
             nc.sbuf_tensor("vast", [128, 2 * H * VW], BF16) as vast, \
             nc.sbuf_tensor("kq_st", [128, 2 * 512], BF16) as kq_st, \
             nc.sbuf_tensor("vbias", [128, C], F32) as vbias, \
             nc.sbuf_tensor("vb_row", [1, C], F32) as vb_row, \
             nc.psum_tensor("pq0", [128, 512], F32) as pq0, \
             nc.psum_tensor("pq1", [128, 512], F32) as pq1:
            pqs = [pq0, pq1]
            tok_vbr = kb.dma("sp", "ini", lambda e: e.dma_start(
                out=vb_row[0:1, :],
                in_=bqkv_d.ap().rearrange("(a n) -> a n", a=1)[:, 2 * C:3 * C]))
            tok_vb = build_bias_row(vbias, pq0, pq1, vb_row, tok_vbr)
            wq_free = [None, None]
            wq_tok = [None] * 6
            evac_tok = [None, None]
            kq_dma = [None, None]
            for j in range(4):
                bw = j % 2
                wfree = [wq_free[bw]] if wq_free[bw] else []
                wq_tok[j] = kb.dma("pool", f"wq{bw}", lambda e, j=j, bw=bw: e.dma_start(
                    out=wq[:, bw * 4096:(bw + 1) * 4096].rearrange("p (c n) -> p c n", n=512),
                    in_=wqkv_d[:, 512 * j:512 * (j + 1)].rearrange("(c p) n -> p c n", p=128)),
                    waits=wfree)
                nch = 4 if j < 2 else 8
                t0 = T0 if j < 2 else 0
                for m in range(4):
                    pair = 4 * (j % 2) + m
                    bcol = 4 * j + m
                    for n in range(nch):
                        bp = (m * nch + n) % 2
                        wfree2 = [evac_tok[bp]] if evac_tok[bp] else []
                        tok_mm = None
                        for kc in range(8):
                            tok_mm = kb.emit("pe", lambda e, bp=bp, bw=bw, kc=kc, m=m, n=n, t0=t0: e.matmul(
                                pqs[bp][:, :],
                                wq[:, bw * 4096 + 512 * kc + 128 * m: bw * 4096 + 512 * kc + 128 * (m + 1)],
                                xnT[:, 4096 * kc + t0 + 512 * n: 4096 * kc + t0 + 512 * (n + 1)],
                                start=(kc == 0), stop=(kc == 7)),
                                waits=[wq_tok[j], xnT_all, tok_vb] + wfree2)
                        wq_free[bw] = tok_mm
                        wst = [kq_dma[bp]] if kq_dma[bp] else []
                        evac_tok[bp] = kb.emit("dve", lambda e, bp=bp, bcol=bcol: e.tensor_scalar(
                            kq_st[:, bp * 512:(bp + 1) * 512], pqs[bp][:, :],
                            bqkT[:, bcol:bcol + 1], None, OP.add),
                            waits=[tok_mm, tok_bq] + wst)
                        if j < 2:
                            dst = q_d[pair, :, 512 * n: 512 * (n + 1)]
                        else:
                            dst = kt_d[pair, :, 512 * n: 512 * (n + 1)]
                        kq_dma[bp] = kb.dma("sp", f"kq{bp}", lambda e, dst=dst, bp=bp: e.dma_start(
                            out=dst, in_=kq_st[:, bp * 512:(bp + 1) * 512]),
                            waits=[evac_tok[bp]])
            for j in (4, 5):
                bw = j % 2
                wfree = [wq_free[bw]] if wq_free[bw] else []
                wq_tok[j] = kb.dma("pool", f"wq{bw}", lambda e, j=j, bw=bw: e.dma_start(
                    out=wq[:, bw * 4096:(bw + 1) * 4096].rearrange("p (c n) -> p c n", n=512),
                    in_=wqkv_d[:, 512 * j:512 * (j + 1)].rearrange("(c p) n -> p c n", p=128)),
                    waits=wfree)
            tok_vones = kb.emit("dve", lambda e: e.memset(
                vast[:, :].rearrange("p (g c) -> p g c", c=VW)[:, :, 64:65], 1.0))
            va_store = [None, None]
            for it in range(NT):
                bv = it % 2
                ev_toks = []
                for jj in (4, 5):
                    bw = jj % 2
                    bp = (it * 2 + (jj - 4)) % 2
                    wfree2 = [evac_tok[bp]] if evac_tok[bp] else []
                    tok_mm = None
                    for kc in range(8):
                        tok_mm = kb.emit("pe", lambda e, bp=bp, bw=bw, kc=kc, it=it: e.matmul(
                            pqs[bp][:, :],
                            xnT[:, 4096 * kc + 128 * it: 4096 * kc + 128 * (it + 1)],
                            wq[:, bw * 4096 + 512 * kc: bw * 4096 + 512 * (kc + 1)],
                            start=(kc == 0), stop=(kc == 7)),
                            waits=[wq_tok[jj], xnT_all] + wfree2)
                    wq_free[bw] = tok_mm
                    wst = [va_store[bv]] if va_store[bv] else []
                    g0 = 8 * (jj - 4)
                    evac_tok[bp] = kb.emit("dve", lambda e, bp=bp, bv=bv, g0=g0, jj=jj: e.tensor_tensor(
                        vast[:, bv * H * VW:(bv + 1) * H * VW].rearrange(
                            "p (g c) -> p g c", c=VW)[:, g0:g0 + 8, 0:64],
                        pqs[bp][:, :].rearrange("p (g c) -> p g c", c=64),
                        vbias[:, 512 * (jj - 4):512 * (jj - 3)].rearrange(
                            "p (g c) -> p g c", c=64),
                        OP.add),
                        waits=[tok_mm, tok_vones, tok_vb] + wst)
                    ev_toks.append(evac_tok[bp])
                va_store[bv] = kb.dma("sp", f"vas{bv}", lambda e, it=it, bv=bv: e.dma_start(
                    out=va_d[it, :, :], in_=vast[:, bv * H * VW:(bv + 1) * H * VW]),
                    waits=ev_toks)
    kb.barrier()

    msb_stack = contextlib.ExitStack()
    msb = msb_stack.enter_context(nc.sbuf_tensor("msb", [128, 4 * E * C], BF16))
    m_tok = [None] * E
    with contextlib.ExitStack() as p2s:
        _e = p2s.enter_context
        wproj_sb = _e(nc.sbuf_tensor("wproj_sb", [128, E * C], BF16))
        bprojB = _e(nc.sbuf_tensor("bprojB", [128, C], F32))
        var = _e(nc.sbuf_tensor("var", [128, 17 * H * VW], BF16))
        ktr = _e(nc.sbuf_tensor("ktr", [128, 17 * 1024], BF16))
        qtr = _e(nc.sbuf_tensor("qtr", [128, 2 * 1024], BF16))
        mkld = _e(nc.sbuf_tensor("mkld", [128, 2 * NB * 128], BF16))
        psb = _e(nc.sbuf_tensor("psb", [128, 3 * NB * 128], BF16))
        attnA = _e(nc.sbuf_tensor("attnA", [128, 2 * 64], BF16))
        attnT = _e(nc.sbuf_tensor("attnT", [128, 2 * 1024], BF16))
        zr = _e(nc.sbuf_tensor("zr", [128, 2], F32))
        xre = _e(nc.sbuf_tensor("xre", [128, 2 * C], F32))
        projsb = _e(nc.sbuf_tensor("projsb", [128, C], F32))
        t1sb = _e(nc.sbuf_tensor("t1sb", [128, C], F32))
        x1sb = _e(nc.sbuf_tensor("x1sb", [128, 2 * C], F32))
        tok_wp = kb.dma("pool", "wpj", lambda e: e.dma_start(
            out=wproj_sb[:, :].rearrange("p (c n) -> p c n", n=C),
            in_=wproj_d.ap().rearrange("(c p) n -> p c n", p=128)))
        for e_i in range(4):
            m_tok[e_i] = kb.dma("pool", f"ml{e_i % 4}", lambda e, e_i=e_i: e.dma_start(
                out=msb[:, e_i * E * C:(e_i + 1) * E * C].rearrange(
                    "p (c n) -> p c n", n=C),
                in_=m_d[e_i, :, :].rearrange("(c p) n -> p c n", p=128)))
        va_tok = {}
        kt_tok = {}
        slot_user = {}
        ktslot_user = {}
        qt_tok = [None, None]
        qt_used = [None, None]
        sps0 = _e(nc.psum_tensor("sps0", [128, 1024], F32))
        sps1 = _e(nc.psum_tensor("sps1", [128, 1024], F32))
        pvp0 = _e(nc.psum_tensor("pvp0", [128, 512], F32))
        pvp1 = _e(nc.psum_tensor("pvp1", [128, 512], F32))
        trp2 = _e(nc.psum_tensor("trp2", [64, 1024], BF16))
        prj = _e(nc.psum_tensor("prj", [128, 512], F32))
        if True:
            tok_bpb = build_bias_row(bprojB, pvp0, pvp1, bp_row, tok_bpr)
            sps = [sps0, sps1]
            pvp = [pvp0, pvp1]
            exp_tok = [None, None]
            msk_sl = [None, None, None]
            dve_norm = [None, None]
            attnA_user = [None, None]
            norm_tok = [None, None]
            tr_ev = None
            attnT_done = [None, None]
            proj_evac_last = None
            projsb_free = None
            x1_store = [None, None]
            xre_used = [None, None]
            xre_tok = [None, None]
            mk_tok = [None, None]
            mk_used = [None, None]

            def emit_scores(i, h):
                bi, bh = i % 2, h % 2
                vi = 16 + i
                bs = [vi - d for d in (16, 8, 4, 2, 1, 0)]
                pair, pb = h >> 1, 64 * (h & 1)
                wsc = ([exp_tok[bh]] if exp_tok[bh] else []) + [qt_tok[bi]]
                tok_sc0 = None
                for idx, b in enumerate(bs):
                    slot = b % 17
                    tok_sc0 = kb.emit("pe", lambda e, bh=bh, idx=idx, slot=slot, pair=pair, pb=pb, bi=bi: e.matmul(
                        sps[bh][:, 128 * idx:128 * (idx + 1)],
                        ktr[pb:pb + 64, slot * 1024 + 128 * pair: slot * 1024 + 128 * (pair + 1)],
                        qtr[pb:pb + 64, bi * 1024 + 128 * pair: bi * 1024 + 128 * (pair + 1)],
                        start=(idx in (0, 4)), stop=(idx in (3, 5))),
                        waits=wsc + [kt_tok[b]])
                    ktslot_user[slot] = tok_sc0
                qt_used[bi] = tok_sc0
                sl = h % 3
                exp_tok[bh] = kb.emit("act", lambda e, bh=bh, sl=sl: e.activation(
                    psb[:, sl * 768: sl * 768 + 768], sps[bh][:, 0:768],
                    AF.Exp, bias=0.0, scale=SCALE),
                    waits=[tok_sc0, msk_sl[sl]])

            def emit_mask(i, h):
                bi, bh = i % 2, h % 2
                sl = h % 3
                msk_sl[sl] = kb.emit("dve", lambda e, sl=sl, bi=bi: e.tensor_mul(
                    psb[:, sl * 768: sl * 768 + 768],
                    psb[:, sl * 768: sl * 768 + 768],
                    mkld[:, bi * 768: bi * 768 + 768]),
                    waits=[exp_tok[bh], mk_tok[bi]])
                mk_used[bi] = msk_sl[sl]

            def emit_pv(i, h):
                bh = h % 2
                vi = 16 + i
                bs = [vi - d for d in (16, 8, 4, 2, 1, 0)]
                wpv = ([dve_norm[bh]] if dve_norm[bh] else []) + [msk_sl[h % 3], tok_bpb] + \
                    [va_tok[b] for b in bs]
                tok_pv = None
                sl = h % 3
                for idx, b in enumerate(bs):
                    slot = b % 17
                    tok_pv = kb.emit("pe", lambda e, bh=bh, sl=sl, idx=idx, slot=slot, h=h: e.matmul(
                        pvp[bh][:, 0:VW],
                        psb[:, sl * 768 + 128 * idx: sl * 768 + 128 * (idx + 1)],
                        var[:, slot * H * VW + VW * h: slot * H * VW + VW * (h + 1)],
                        start=(idx == 0), stop=(idx == 5)),
                        waits=wpv)
                    slot_user[slot] = tok_pv
                kb.emit("dve", lambda e, bh=bh: e.reciprocal(
                    zr[:, bh:bh + 1], pvp[bh][:, 64:65]), waits=[tok_pv])
                wn = [attnA_user[bh]] if attnA_user[bh] else []
                norm_tok[bh] = kb.emit("dve", lambda e, bh=bh: e.tensor_scalar(
                    attnA[:, bh * 64:(bh + 1) * 64], pvp[bh][:, 0:64],
                    zr[:, bh:bh + 1], None, OP.mult), waits=wn)
                dve_norm[bh] = norm_tok[bh]

            def emit_tr(i, h):
                nonlocal tr_ev
                bi, bh = i % 2, h % 2
                pair, pb = h >> 1, 64 * (h & 1)
                wtr = [tr_ev] if tr_ev else []
                tok_tr2 = kb.emit("pe", lambda e, bh=bh: e.transpose(
                    trp2[:, 0:128], attnA[:, bh * 64:(bh + 1) * 64], idb[:, :]),
                    waits=[norm_tok[bh]] + wtr)
                attnA_user[bh] = tok_tr2
                tr_ev = kb.emit("act", lambda e, bi=bi, pb=pb, pair=pair: e.copy(
                    attnT[pb:pb + 64, bi * 1024 + 128 * pair: bi * 1024 + 128 * (pair + 1)],
                    trp2[0:64, 0:128]), waits=[tok_tr2])
                attnT_done[bi] = tr_ev

            def emit_proj_half(ip, n):
                nonlocal proj_evac_last
                bip = ip % 2
                tok_pj = None
                for kc in range(8):
                    tok_pj = kb.emit("pe", lambda e, bip=bip, kc=kc, n=n: e.matmul(
                        prj[:, :],
                        attnT[:, bip * 1024 + 128 * kc: bip * 1024 + 128 * (kc + 1)],
                        wproj_sb[:, 1024 * kc + 512 * n: 1024 * kc + 512 * (n + 1)],
                        start=(kc == 0), stop=(kc == 7)),
                        waits=[attnT_done[bip], tok_wp] +
                              ([proj_evac_last] if proj_evac_last else []))
                wev = [projsb_free] if (n == 0 and projsb_free) else []
                proj_evac_last = kb.emit("act", lambda e, n=n: e.copy(
                    projsb[:, 512 * n:512 * (n + 1)], prj[:, :]),
                    waits=[tok_pj] + wev)

            def emit_x1(ip):
                nonlocal projsb_free
                bip = ip % 2
                tok_t1 = kb.emit("pool", lambda e: e.tensor_add(
                    t1sb[:, :], projsb[:, :], bprojB[:, :]),
                    waits=[proj_evac_last, tok_bpb])
                projsb_free = tok_t1
                prev_x1 = [x1_store[bip]] if x1_store[bip] else []
                xre_used[bip] = kb.emit("pool", lambda e, bip=bip: e.tensor_add(
                    x1sb[:, bip * C:(bip + 1) * C], t1sb[:, :],
                    xre[:, bip * C:(bip + 1) * C]),
                    waits=[tok_t1, xre_tok[bip]] + prev_x1)
                x1_store[bip] = kb.dma("sp", f"x1s{bip}", lambda e, ip=ip, bip=bip: e.dma_start(
                    out=x1_d[128 * ip:128 * (ip + 1), :],
                    in_=x1sb[:, bip * C:(bip + 1) * C]), waits=[xre_used[bip]])

            for i in range(OWN):
                vi = 16 + i
                bs = [vi - d for d in (16, 8, 4, 2, 1, 0)]
                bi = i % 2
                for b in bs:
                    if b not in va_tok:
                        slot = b % 17
                        wfree = [slot_user[slot]] if slot in slot_user else []
                        va_tok[b] = kb.dma("sp", f"val{slot % 4}", lambda e, b=b, slot=slot: e.dma_start(
                            out=var[:, slot * H * VW:(slot + 1) * H * VW],
                            in_=va_d[b, :, :]), waits=wfree)
                    if b not in kt_tok:
                        slot = b % 17
                        wfree = [ktslot_user[slot]] if slot in ktslot_user else []
                        kt_tok[b] = kb.dma("sp", f"ktl{slot % 4}", lambda e, b=b, slot=slot: e.dma_start(
                            out=ktr[:, slot * 1024:(slot + 1) * 1024].rearrange(
                                "p (a n) -> p a n", n=128),
                            in_=kt_d[:, :, 128 * b:128 * (b + 1)].rearrange(
                                "a p n -> p a n")), waits=wfree)
                wqt = [qt_used[bi]] if qt_used[bi] else []
                qt_tok[bi] = kb.dma("sp", f"qt{bi}", lambda e, i=i, bi=bi: e.dma_start(
                    out=qtr[:, bi * 1024:(bi + 1) * 1024].rearrange("p (a n) -> p a n", n=128),
                    in_=q_d[:, :, 128 * i:128 * (i + 1)].rearrange("a p n -> p a n")),
                    waits=wqt)
                wmk = [mk_used[bi]] if mk_used[bi] else []
                mk_tok[bi] = kb.dma("sp", f"mk{bi}", lambda e, i=i, bi=bi: e.dma_start(
                    out=mkld[:, bi * NB * 128:(bi + 1) * NB * 128], in_=mk_d[i, :, :]),
                    waits=wmk)
                wxf = [xre_used[bi]] if xre_used[bi] else []
                xre_tok[bi] = kb.dma("sp", f"xre{bi}", lambda e, i=i, bi=bi: e.dma_start(
                    out=xre[:, bi * C:(bi + 1) * C],
                    in_=x_d[T0 + 128 * i: T0 + 128 * (i + 1), :]), waits=wxf)
                for step in range(H + 3):
                    if step == 0 and i > 0:
                        emit_proj_half(i - 1, 0)
                    if step == 8 and i > 0:
                        emit_proj_half(i - 1, 1)
                        emit_x1(i - 1)
                    if step < H:
                        emit_scores(i, step)
                    if 2 <= step <= H + 1:
                        emit_pv(i, step - 2)
                    if step < H:
                        emit_mask(i, step)
                    if step >= 3:
                        emit_tr(i, step - 3)
            emit_proj_half(OWN - 1, 0)
            emit_proj_half(OWN - 1, 1)
            emit_x1(OWN - 1)
    kb.barrier()

    with nc.sbuf_tensor("xn1T", [128, OWN * C], BF16) as xn1T, \
         nc.sbuf_tensor("yacc", [128, OWN * C], F32) as yacc, \
         nc.sbuf_tensor("x1ld", [128, 2 * C], F32) as x1ld, \
         nc.sbuf_tensor("xc2", [128, 2 * C], F32) as xc2, \
         nc.sbuf_tensor("xn1bf", [128, 4 * C], BF16) as xn1bf, \
         nc.sbuf_tensor("st2", [128, 16], F32) as st2, \
         nc.sbuf_tensor("wall", [128, OWN * E], F32) as wall, \
         nc.sbuf_tensor("el", [128, 24], F32) as el, \
         nc.sbuf_tensor("outsb", [128, 2 * C], F32) as outsb:
        x1_free = [None, None]
        with nc.psum_tensor("py0", [128, 1024], F32) as py0, \
             nc.psum_tensor("py1", [128, 1024], F32) as py1:
            ev_tok = [[None, None], [None, None]]
            group_last_mm = [None] * 4
            pys = [(py0, py1), None]

            def emit_y_tile(g, i, bg):
                wmm = [t for t in ev_tok[bg] if t]
                tok_y = [None, None]
                for kc in range(8):
                    for e_loc in range(2):
                        e_g = 2 * g + e_loc
                        ms = (e_g % 4) * E * C
                        for n in range(2):
                            tok_y[e_loc] = kb.emit("pe", lambda e, bg=bg, e_loc=e_loc, i=i, kc=kc, n=n, ms=ms: e.matmul(
                                pys[bg][e_loc][:, 512 * n:512 * (n + 1)],
                                xn1T[:, 1024 * i + 128 * kc: 1024 * i + 128 * (kc + 1)],
                                msb[:, ms + 1024 * kc + 512 * n: ms + 1024 * kc + 512 * (n + 1)],
                                start=(kc == 0), stop=(kc == 7)),
                                waits=[m_tok[2 * g + e_loc]] + (wmm if kc == 0 else []))
                for e_loc in range(2):
                    e_g = 2 * g + e_loc
                    if e_g == 0:
                        ev = kb.emit("dve", lambda e, bg=bg, e_loc=e_loc, i=i, e_g=e_g: e.tensor_scalar(
                            yacc[:, C * i:C * (i + 1)], pys[bg][e_loc][:, :],
                            wall[:, E * i + e_g:E * i + e_g + 1], None, OP.mult),
                            waits=[tok_y[e_loc]])
                    else:
                        ev = kb.emit("dve", lambda e, bg=bg, e_loc=e_loc, i=i, e_g=e_g: e.scalar_tensor_tensor(
                            yacc[:, C * i:C * (i + 1)], pys[bg][e_loc][:, :],
                            wall[:, E * i + e_g:E * i + e_g + 1],
                            yacc[:, C * i:C * (i + 1)], OP.mult, OP.add),
                            waits=[tok_y[e_loc]])
                    ev_tok[bg][e_loc] = ev
                return tok_y[1]

            with nc.psum_tensor("ptr3a", [128, 1024], BF16) as ptr3a, \
                 nc.psum_tensor("ptr3b", [128, 1024], BF16) as ptr3b, \
                 nc.psum_tensor("prt", [128, 512], F32) as prt:
                ptrs3 = [ptr3a, ptr3b]
                tr_ev3 = [None, None]
                last_norm2 = [None]
                norm3_of = {}
                rt_done = None
                prev_rt_w = None
                prev_rstd2 = [None, None]
                prev_last_tr3 = [None, None, None, None]
                def p3_lad(i):
                    bi = i % 2
                    wfree = [x1_free[bi]] if x1_free[bi] else []
                    tok_x1 = kb.dma("sp", f"x1l{bi}", lambda e, i=i, bi=bi: e.dma_start(
                        out=x1ld[:, bi * C:(bi + 1) * C],
                        in_=x1_d[128 * i:128 * (i + 1), :]), waits=wfree)
                    xv = x1ld[:, bi * C:(bi + 1) * C]
                    xcv = xc2[:, bi * C:(bi + 1) * C]
                    s0, s1, s2, s3, s4 = (8 * bi + k for k in range(5))
                    tok_sum = kb.emit("act", lambda e, xv=xv, xcv=xcv, s0=s0: e.activation(
                        xcv, xv, AF.Copy, bias=0.0, scale=1.0,
                        accum_out=st2[:, s0:s0 + 1]),
                        waits=[tok_x1] +
                              ([prev_rstd2[bi]] if prev_rstd2[bi] else []) +
                              ([norm3_of[i - 2]] if i >= 2 else []))
                    kb.emit("dve", lambda e, s0=s0, s1=s1: e.tensor_scalar(
                        st2[:, s1:s1 + 1], st2[:, s0:s0 + 1], -1.0 / C, None, OP.mult),
                        waits=[tok_sum])
                    kb.emit("dve", lambda e, xv=xv, xcv=xcv, s1=s1: e.tensor_scalar(
                        xcv, xv, st2[:, s1:s1 + 1], None, OP.add))
                    tok_var = kb.emit("dve", lambda e, xv=xv, xcv=xcv, s2=s2: e.scalar_tensor_tensor(
                        xv, xcv, 1.0, xv, OP.mult, OP.mult,
                        accum_out=st2[:, s2:s2 + 1]))
                    x1_free[bi] = tok_var
                    kb.emit("act", lambda e, s2=s2, s3=s3: e.activation(
                        st2[:, s3:s3 + 1], st2[:, s2:s2 + 1], AF.Ln, bias=epsb[:, 0:1],
                        scale=1.0 / C), waits=[tok_var, tok_eps])
                    prev_rstd2[bi] = kb.emit("act", lambda e, s3=s3, s4=s4: e.activation(
                        st2[:, s4:s4 + 1], st2[:, s3:s3 + 1], AF.Exp, bias=0.0,
                        scale=-0.5))
                    bn = i % 4
                    tok_xn1 = kb.emit("dve", lambda e, bn=bn, xcv=xcv, s4=s4: e.tensor_scalar(
                        xn1bf[:, bn * C:(bn + 1) * C], xcv, st2[:, s4:s4 + 1],
                        None, OP.mult),
                        waits=[prev_rstd2[bi], tok_var] +
                              ([prev_last_tr3[bn]] if prev_last_tr3[bn] else []))
                    norm3_of[i] = tok_xn1
                def p3_work_a(i):
                    bi = i % 2
                    bn = i % 4
                    bt = i % 2
                    wfree2 = [tr_ev3[bt]] if tr_ev3[bt] else []
                    tok_tr = None
                    for c in range(8):
                        tok_tr = kb.emit("pe", lambda e, c=c, bt=bt, bn=bn: e.transpose(
                            ptrs3[bt][:, 128 * c:128 * (c + 1)],
                            xn1bf[:, bn * C + 128 * c: bn * C + 128 * (c + 1)], idb[:, :]),
                            waits=([norm3_of[i], tok_idb] + wfree2) if c == 0 else ())
                    prev_last_tr3[bn] = tok_tr
                    tr_ev3[bt] = kb.emit("act", lambda e, i=i, bt=bt: e.copy(
                        xn1T[:, 1024 * i: 1024 * (i + 1)], ptrs3[bt][:, :]),
                        waits=[tok_tr])
                    tr_done[i] = tr_ev3[bt]

                def p3_work_b(i):
                    nonlocal rt_done, prev_rt_w
                    bi = i % 2
                    tr_last = tr_done[i]
                    wrt = [rt_done] if rt_done else []
                    tok_rt = None
                    for kc in range(8):
                        tok_rt = kb.emit("pe", lambda e, i=i, kc=kc: e.matmul(
                            prt[:, 0:E],
                            xn1T[:, 1024 * i + 128 * kc: 1024 * i + 128 * (kc + 1)],
                            wr_sb[:, E * kc: E * (kc + 1)],
                            start=(kc == 0), stop=(kc == 7)),
                            waits=[tr_last, tok_wr] + wrt)
                    tok_el = kb.emit("act", lambda e: e.activation(
                        el[:, 0:8], prt[:, 0:E], AF.Exp, bias=0.0, scale=1.0),
                        waits=[tok_rt] + ([prev_rt_w] if prev_rt_w else []))
                    rt_done = tok_el
                    kb.emit("dve", lambda e: e.reduce_sum(st2[:, 5:6], el[:, 0:8], AX),
                            waits=[tok_el])
                    kb.emit("dve", lambda e: e.reduce_max(st2[:, 6:7], el[:, 0:8], AX))
                    kb.emit("dve", lambda e: e.tensor_scalar(
                        el[:, 8:16], el[:, 0:8], st2[:, 6:7], None, OP.is_equal))
                    kb.emit("dve", lambda e: e.tensor_scalar(
                        el[:, 8:16], el[:, 8:16], -1.0, 1.0, OP.mult, OP.add))
                    kb.emit("dve", lambda e: e.tensor_mul(el[:, 16:24], el[:, 0:8], el[:, 8:16]))
                    kb.emit("dve", lambda e: e.reduce_max(st2[:, 7:8], el[:, 16:24], AX))
                    kb.emit("dve", lambda e: e.tensor_scalar(
                        el[:, 8:16], el[:, 0:8], st2[:, 7:8], None, OP.is_ge))
                    kb.emit("dve", lambda e: e.tensor_mul(el[:, 16:24], el[:, 0:8], el[:, 8:16]))
                    kb.emit("dve", lambda e: e.reciprocal(st2[:, 5:6], st2[:, 5:6]))
                    prev_rt_w = kb.emit("dve", lambda e, i=i: e.tensor_scalar(
                        wall[:, E * i:E * (i + 1)], el[:, 16:24], st2[:, 5:6], None, OP.mult))

                    group_last_mm[0] = emit_y_tile(0, i, 0)

                tr_done = {}
                for i in range(OWN + 3):
                    if i < OWN:
                        p3_lad(i)
                    if 2 <= i < OWN + 2:
                        p3_work_a(i - 2)
                    if i >= 3:
                        p3_work_b(i - 3)

            kb.wait("pe", "act", kb.cnt["act"])
            kb.wait("pe", "dve", kb.cnt["dve"])
            with nc.psum_tensor("py2", [128, 1024], F32) as py2, \
                 nc.psum_tensor("py3", [128, 1024], F32) as py3:
                pys[1] = (py2, py3)
                for g in range(1, 4):
                    for e_loc in range(2):
                        e_g = 2 * g + e_loc
                        if m_tok[e_g] is None:
                            m_tok[e_g] = kb.dma("pool", f"ml{e_g % 4}", lambda e, e_g=e_g: e.dma_start(
                                out=msb[:, (e_g % 4) * E * C:((e_g % 4) + 1) * E * C].rearrange(
                                    "p (c n) -> p c n", n=C),
                                in_=m_d[e_g, :, :].rearrange("(c p) n -> p c n", p=128)),
                                waits=[group_last_mm[g - 2]])
                    out_store = [None, None]
                    x1_free2 = list(x1_free)

                    def emit_out_tile(i, last_ev):
                        bi = i % 2
                        wfree4 = [x1_free2[bi]] if x1_free2[bi] else []
                        tok_x1 = kb.dma("sp", f"x1l{bi}", lambda e, i=i, bi=bi: e.dma_start(
                            out=x1ld[:, bi * C:(bi + 1) * C],
                            in_=x1_d[128 * i:128 * (i + 1), :]), waits=wfree4)
                        wos = [out_store[bi]] if out_store[bi] else []
                        x1_free2[bi] = kb.emit("dve", lambda e, i=i, bi=bi: e.tensor_add(
                            outsb[:, bi * C:(bi + 1) * C], yacc[:, C * i:C * (i + 1)],
                            x1ld[:, bi * C:(bi + 1) * C]),
                            waits=[tok_x1, last_ev] + wos)
                        out_store[bi] = kb.dma("sp", f"out{bi}", lambda e, i=i, bi=bi: e.dma_start(
                            out=out_d[128 * i:128 * (i + 1), :],
                            in_=outsb[:, bi * C:(bi + 1) * C]), waits=[x1_free2[bi]])

                    for i in range(OWN):
                        bg = (g * OWN + i) % 2
                        group_last_mm[g] = emit_y_tile(g, i, bg)
                        if g == 3:
                            emit_out_tile(i, ev_tok[bg][1])
        kb.wait("sp", "out0", kb.cnt["out0"])
        kb.wait("sp", "out1", kb.cnt["out1"])

    msb_stack.close()
    kb.replay()
    kb.stack.close()
    return nc


_CACHE = {}


def kernel(L0_attn_x, L0_attn_W_qkv, L0_attn_b_qkv, L0_attn_W_proj, L0_attn_b_proj,
           L0_ffn_Wr, L0_ffn_Wi, L0_ffn_Wo):
    x = np.ascontiguousarray(np.asarray(L0_attn_x, dtype=np.float32))
    wqkv = np.ascontiguousarray(np.asarray(L0_attn_W_qkv, dtype=np.float32))
    bqkv = np.ascontiguousarray(np.asarray(L0_attn_b_qkv, dtype=np.float32))
    wproj = np.ascontiguousarray(np.asarray(L0_attn_W_proj, dtype=np.float32))
    bproj = np.ascontiguousarray(np.asarray(L0_attn_b_proj, dtype=np.float32))
    wr = np.ascontiguousarray(np.asarray(L0_ffn_Wr, dtype=np.float32))
    wi = np.ascontiguousarray(np.swapaxes(np.asarray(L0_ffn_Wi, dtype=np.float32), 1, 2))
    wo = np.ascontiguousarray(np.asarray(L0_ffn_Wo, dtype=np.float32))
    idn = np.eye(128, dtype=np.float32)
    masks = [build_masks(0), build_masks(1)]

    if "nc" not in _CACHE:
        _CACHE["nc"] = build_program()
    nc = _CACHE["nc"]

    in_maps = []
    for core in range(8):
        b, half = core >> 1, core & 1
        if half == 0:
            xv = np.concatenate(
                [np.zeros((OWN_TOK, C), np.float32), x[b, 0:OWN_TOK]], axis=0)
        else:
            xv = np.ascontiguousarray(x[b])
        in_maps.append({
            "x": xv, "wqkv": wqkv, "bqkv": bqkv, "wproj": wproj, "bproj": bproj,
            "wr": wr, "wi": wi, "wo": wo, "idn": idn, "mk": masks[half],
        })
    res = run_bass_kernel_spmd(nc, in_maps, core_ids=list(range(8)))
    out = np.empty((B, T, C), dtype=np.float32)
    for core in range(8):
        b, half = core >> 1, core & 1
        out[b, half * OWN_TOK:(half + 1) * OWN_TOK] = res.results[core]["out"]
    return out


# version 90 (final)
# speedup vs baseline: 1.1609x; 1.1609x over previous
import contextlib
import numpy as np
import ml_dtypes
import concourse.bass as bass
import concourse.mybir as mybir
from concourse.bass_utils import run_bass_kernel_spmd

F32 = mybir.dt.float32
BF16 = mybir.dt.bfloat16
AF = mybir.ActivationFunctionType
OP = mybir.AluOpType
AX = mybir.AxisListType.X

B, T, C = 4, 4096, 1024
H = 16
NT = 32
OWN = 16
OWN_TOK = OWN * 128
E = 8
SCALE = 0.125
EPS = 1e-5
VW = 65
NB = 6
ENGINES = ("pe", "act", "dve", "pool", "sp")


def build_masks(half):
    s = np.arange(128)[:, None]
    t = np.arange(128)[None, :]
    m0 = np.isin(t - s, [0, 1, 2, 4, 8, 16, 32, 64]).astype(np.float32)
    m1 = np.isin(t - s + 128, [1, 2, 4, 8, 16, 32, 64, 128]).astype(np.float32)
    mD = np.eye(128, dtype=np.float32)
    out = np.zeros((OWN, 128, NB * 128), dtype=np.float32)
    for i in range(OWN):
        vi = 16 + i
        for idx, d in enumerate((16, 8, 4, 2, 1, 0)):
            vb = vi - d
            real = vb - 16 + 16 * half
            if real >= 0:
                kind = m0 if d == 0 else (m1 if d == 1 else mD)
                out[i, :, 128 * idx:128 * (idx + 1)] = kind
    return out.astype(ml_dtypes.bfloat16)


class KB:

    def __init__(self, nc):
        self.nc = nc
        self.items = {e: [] for e in ENGINES}
        self.stack = contextlib.ExitStack()
        self.sems = {}
        self.cnt = {}
        self.hwm = {e: {} for e in ENGINES}
        for s in ("pe", "act", "dve", "pool"):
            self.sems[s] = self.stack.enter_context(nc.semaphore(s))
            self.cnt[s] = 0

    def wait(self, eng, sem, val):
        if val <= 0 or self.hwm[eng].get(sem, 0) >= val:
            return
        self.hwm[eng][sem] = val
        s = self.sems[sem]
        self.items[eng].append(lambda EE, s=s, v=val: EE.wait_ge(s, v))

    def emit(self, eng, fn, waits=()):
        if eng in ("dve", "act", "pool"):
            self.wait(eng, eng, self.cnt[eng])
        for w in waits:
            if w:
                self.wait(eng, w[0], w[1])
        self.cnt[eng] += 1
        s = self.sems[eng]
        self.items[eng].append(lambda EE, fn=fn, s=s: fn(EE).then_inc(s, 1))
        return (eng, self.cnt[eng])

    def dma(self, eng, stream, fn, waits=()):
        if stream not in self.sems:
            self.sems[stream] = self.stack.enter_context(self.nc.semaphore(stream))
            self.cnt[stream] = 0
        self.wait(eng, stream, self.cnt[stream])
        for w in waits:
            if w:
                self.wait(eng, w[0], w[1])
        self.cnt[stream] += 16
        s = self.sems[stream]
        self.items[eng].append(lambda EE, fn=fn, s=s: fn(EE).then_inc(s, 16))
        return (stream, self.cnt[stream])

    def barrier(self):
        toks = [(s, self.cnt[s]) for s in self.cnt]
        for e in ENGINES:
            for sem, val in toks:
                self.wait(e, sem, val)

    def replay(self):
        with self.nc.Block() as block:
            @block.tensor
            def _(t):
                for f in self.items["pe"]:
                    f(t)

            @block.scalar
            def _(s):
                for f in self.items["act"]:
                    f(s)

            @block.vector
            def _(v):
                for f in self.items["dve"]:
                    f(v)

            @block.gpsimd
            def _(g):
                for f in self.items["pool"]:
                    f(g)

            @block.sync
            def _(sy):
                for f in self.items["sp"]:
                    f(sy)


def build_program():
    nc = bass.Bass("TRN2", target_bir_lowering=False)
    dt = nc.dram_tensor
    x_d = dt("x", [T, C], F32, kind="ExternalInput")
    wqkv_d = dt("wqkv", [C, 3 * C], F32, kind="ExternalInput")
    bqkv_d = dt("bqkv", [3 * C], F32, kind="ExternalInput")
    wproj_d = dt("wproj", [C, C], F32, kind="ExternalInput")
    bproj_d = dt("bproj", [C], F32, kind="ExternalInput")
    wr_d = dt("wr", [C, E], F32, kind="ExternalInput")
    wi_d = dt("wi", [E, C, C], F32, kind="ExternalInput")
    wo_d = dt("wo", [E, C, C], F32, kind="ExternalInput")
    idn_d = dt("idn", [128, 128], F32, kind="ExternalInput")
    mk_d = dt("mk", [OWN, 128, NB * 128], BF16, kind="ExternalInput")
    out_d = dt("out", [OWN_TOK, C], F32, kind="ExternalOutput")
    m_d = dt("m_sc", [E, C, C], BF16)
    va_d = dt("va_sc", [NT, 128, H * VW], BF16)
    kt_d = dt("kt_sc", [8, 128, T], BF16)
    q_d = dt("q_sc", [8, 128, OWN_TOK], BF16)
    x1_d = dt("x1_sc", [OWN_TOK, C], F32)

    kb = KB(nc)
    st = kb.stack
    T0 = OWN_TOK

    idb = st.enter_context(nc.sbuf_tensor("idb", [128, 128], BF16))
    bqkT = st.enter_context(nc.sbuf_tensor("bqkT", [128, 16], F32))
    wr_sb = st.enter_context(nc.sbuf_tensor("wr_sb", [128, E * E], BF16))
    epsb = st.enter_context(nc.sbuf_tensor("epsb", [128, 1], F32))
    ones_row = st.enter_context(nc.sbuf_tensor("ones_row", [1, 128], F32))
    bp_row = st.enter_context(nc.sbuf_tensor("bp_row", [1, C], F32))

    tok_onr = kb.emit("dve", lambda e: e.memset(ones_row[:, :], 1.0))
    tok_bpr = kb.dma("sp", "ini", lambda e: e.dma_start(
        out=bp_row[0:1, :], in_=bproj_d.ap().rearrange("(a n) -> a n", a=1)))

    def build_bias_row(dst, ps_a, ps_b, row, t_r):
        one, t_o = ones_row, tok_onr
        toks = []
        for n in range(2):
            ps = ps_a if n == 0 else ps_b
            t_mm = kb.emit("pe", lambda e, ps=ps, n=n: e.matmul(
                ps[:, :], one[:, :], row[0:1, 512 * n: 512 * (n + 1)],
                start=True, stop=True),
                waits=[t_r, t_o])
            toks.append(kb.emit("dve", lambda e, ps=ps, n=n: e.tensor_copy(
                dst[:, 512 * n: 512 * (n + 1)], ps[:, :]), waits=[t_mm]))
        return toks[-1]

    tok_eps = kb.emit("dve", lambda e: e.memset(epsb[:, :], EPS))
    tok_idb = kb.dma("pool", "pini", lambda e: e.dma_start(out=idb[:, :], in_=idn_d[:, :]))
    tok_wr = kb.dma("pool", "pini", lambda e: e.dma_start(
        out=wr_sb[:, :].rearrange("p (c n) -> p c n", n=E),
        in_=wr_d.ap().rearrange("(c p) n -> p c n", p=128)))
    with nc.sbuf_tensor("idf", [16, 16], F32) as idf, \
         nc.sbuf_tensor("rowbuf", [16, 128], F32) as rowbuf:
        tok_id = kb.dma("sp", "ini", lambda e: e.dma_start(
            out=idf[:, :], in_=idn_d[0:16, 0:16]))
        with nc.psum_tensor("pbias", [128, 512], F32) as pbias:
            tok_rb = kb.dma("sp", "ini", lambda e: e.dma_start(
                out=rowbuf[:, :],
                in_=bqkv_d.ap().rearrange("(c p) -> c p", p=128)[0:16, :]))
            tok_tr = kb.emit("pe", lambda e: e.transpose(
                pbias[:, 0:16], rowbuf[:, :], idf[:, :]),
                waits=[tok_rb, tok_id])
            tok_bq = kb.emit("dve", lambda e: e.tensor_copy(bqkT[:, :], pbias[:, 0:16]),
                             waits=[tok_tr])
    kb.barrier()

    with nc.sbuf_tensor("xnT", [128, 8 * T], BF16) as xnT, \
         nc.sbuf_tensor("xld", [128, 2 * C], F32) as xld, \
         nc.sbuf_tensor("xc", [128, 2 * C], F32) as xc, \
         nc.sbuf_tensor("stats", [128, 16], F32) as stats, \
         nc.sbuf_tensor("xnbf", [128, 6 * C], BF16) as xnbf:
        with nc.sbuf_tensor("wiT", [128, 2 * E * C], BF16) as wiT, \
             nc.sbuf_tensor("wof", [128, 2 * 4 * C], F32) as wof, \
             nc.sbuf_tensor("wo_sb", [128, 2 * E * C], BF16) as wo_sb, \
             nc.sbuf_tensor("m_st", [128, 3 * C], BF16) as m_st, \
             nc.psum_tensor("pm0", [128, 512], F32) as pm0, \
             nc.psum_tensor("pm1", [128, 512], F32) as pm1, \
             nc.psum_tensor("pm2", [128, 512], F32) as pm2, \
             nc.psum_tensor("pm3", [128, 512], F32) as pm3, \
             nc.psum_tensor("pm4", [128, 512], F32) as pm4, \
             nc.psum_tensor("pm5", [128, 512], F32) as pm5, \
             nc.psum_tensor("ptr0", [128, 1024], BF16) as ptr0, \
             nc.psum_tensor("ptr1", [128, 1024], BF16) as ptr1:
            pms = [(pm0, pm1), (pm2, pm3), (pm4, pm5)]
            ptrs = [ptr0, ptr1]
            wi_tok = [None] * E
            wi_user = [None, None]
            wo_tok_l = [None] * E
            wof_cast = [None, None]
            last_mm_buf = [None, None]
            mst_free = [None, None, None]
            pm_free = [None, None, None]
            witr_ev = [None, None]
            wisb_user = [None, None]
            wo_cast = [None]

            def emit_wi_load(e_n):
                nbuf = e_n % 2
                wfree = [wi_user[nbuf]] if wi_user[nbuf] else []
                toks = []
                for hf in range(4):
                    toks.append(kb.dma("pool", f"wi{hf}", lambda e, e_n=e_n, nbuf=nbuf, hf=hf: e.dma_start(
                        out=wiT[:, nbuf * E * C + 2048 * hf: nbuf * E * C + 2048 * (hf + 1)].rearrange(
                            "p (c n) -> p c n", n=C),
                        in_=wi_d[e_n, 256 * hf: 256 * (hf + 1), :].rearrange(
                            "(c p) n -> p c n", p=128)),
                        waits=wfree))
                wi_tok[e_n] = toks

            def emit_wo_load(e_n):
                nbuf = e_n % 2
                wW2 = [last_mm_buf[nbuf]] if last_mm_buf[nbuf] else []
                tok_c = None
                for hf in range(2):
                    wfree3 = [wof_cast[hf]] if wof_cast[hf] else []
                    t_l = kb.dma("sp", f"wof{hf}", lambda e, e_n=e_n, hf=hf: e.dma_start(
                        out=wof[:, hf * 4 * C:(hf + 1) * 4 * C].rearrange(
                            "p (c n) -> p c n", n=C),
                        in_=wo_d[e_n, 512 * hf:512 * (hf + 1), :].rearrange(
                            "(c p) n -> p c n", p=128)),
                        waits=wfree3)
                    tok_c = kb.emit("act", lambda e, nbuf=nbuf, hf=hf: e.copy(
                        wo_sb[:, nbuf * E * C + 4096 * hf: nbuf * E * C + 4096 * (hf + 1)],
                        wof[:, hf * 4 * C:(hf + 1) * 4 * C]), waits=[t_l] + wW2)
                    wof_cast[hf] = tok_c
                wo_tok_l[e_n] = tok_c

            def emit_m_expert(e_i, fillers=()):
                buf = e_i % 2
                off = buf * E * C
                offw = (e_i % 2) * E * C
                fillers = list(fillers)
                if e_i + 1 < E and wo_tok_l[e_i + 1] is None:
                    emit_wo_load(e_i + 1)
                wW = [last_mm_buf[buf]] if last_mm_buf[buf] else []
                wo_tok = wo_tok_l[e_i]
                if e_i + 1 < E:
                    emit_wi_load(e_i + 1)
                wiT_tok = wi_tok[e_i]
                for dc in range(8):
                    pi = (e_i * 8 + dc) % 3
                    pp = pms[pi]
                    wfree = [pm_free[pi]] if pm_free[pi] else []
                    evac_toks = []
                    tok = None
                    for n in range(2):
                        for hc in range(8):
                            tok = kb.emit("pe", lambda e, pp=pp, n=n, hc=hc, dc=dc, off=off, offw=offw: e.matmul(
                                pp[n][:, :],
                                wiT[:, offw + 1024 * hc + 128 * dc: offw + 1024 * hc + 128 * (dc + 1)],
                                wo_sb[:, off + 1024 * hc + 512 * n: off + 1024 * hc + 512 * (n + 1)],
                                start=(hc == 0), stop=(hc == 7)),
                                waits=[wo_tok] + wiT_tok + wfree)
                        mb = pi
                        wfree2 = [mst_free[mb]] if mst_free[mb] else []
                        evac_toks.append(kb.emit("dve", lambda e, pp=pp, n=n, mb=mb: e.tensor_copy(
                            m_st[:, mb * C + 512 * n: mb * C + 512 * (n + 1)], pp[n][:, :]),
                            waits=[tok] + wfree2))
                    last_mm_buf[buf] = tok
                    wi_user[e_i % 2] = tok
                    pm_free[pi] = evac_toks[-1]
                    mb = pi
                    mst_free[mb] = kb.dma("sp", f"mst{mb}", lambda e, e_i=e_i, dc=dc, mb=mb: e.dma_start(
                        out=m_d[e_i, 128 * dc: 128 * (dc + 1), :],
                        in_=m_st[:, mb * C: (mb + 1) * C]), waits=evac_toks)
                    if dc % 2 == 1 and fillers:
                        fillers.pop(0)()

            xld_free = [None, None]
            norm_of = {}
            tr_evac = [None, None]
            ln_rstd = [None, None]
            prev_tr = [None] * 6
            xnT_last = [None]

            def emit_ln_ladder(it):
                bufx = it % 2
                bufn = it % 6
                wfree = [xld_free[bufx]] if xld_free[bufx] else []
                tok_x = kb.dma("sp", f"x{bufx}", lambda e, it=it, bufx=bufx: e.dma_start(
                    out=xld[:, bufx * C:(bufx + 1) * C],
                    in_=x_d[128 * it:128 * (it + 1), :]), waits=wfree)
                xv = xld[:, bufx * C:(bufx + 1) * C]
                xcv = xc[:, bufx * C:(bufx + 1) * C]
                s0, s1, s2, s3, s4 = (8 * bufx + k for k in range(5))
                tok_sum = kb.emit("act", lambda e, xv=xv, xcv=xcv, s0=s0: e.activation(
                    xcv, xv, AF.Copy, bias=0.0, scale=1.0,
                    accum_out=stats[:, s0:s0 + 1]),
                    waits=[tok_x] +
                          ([ln_rstd[bufx]] if ln_rstd[bufx] else []) +
                          ([norm_of[it - 2]] if it >= 2 else []))
                kb.emit("dve", lambda e, s0=s0, s1=s1: e.tensor_scalar(
                    stats[:, s1:s1 + 1], stats[:, s0:s0 + 1], -1.0 / C, None, OP.mult),
                    waits=[tok_sum])
                tok_xc = kb.emit("dve", lambda e, xv=xv, xcv=xcv, s1=s1: e.tensor_scalar(
                    xcv, xv, stats[:, s1:s1 + 1], None, OP.add))
                tok_var = kb.emit("dve", lambda e, xv=xv, xcv=xcv, s2=s2: e.scalar_tensor_tensor(
                    xv, xcv, 1.0, xv, OP.mult, OP.mult,
                    accum_out=stats[:, s2:s2 + 1]), waits=[tok_xc])
                xld_free[bufx] = tok_var
                kb.emit("act", lambda e, s2=s2, s3=s3: e.activation(
                    stats[:, s3:s3 + 1], stats[:, s2:s2 + 1], AF.Ln, bias=epsb[:, 0:1],
                    scale=1.0 / C), waits=[tok_var, tok_eps])
                ln_rstd[bufx] = kb.emit("act", lambda e, s3=s3, s4=s4: e.activation(
                    stats[:, s4:s4 + 1], stats[:, s3:s3 + 1], AF.Exp, bias=0.0,
                    scale=-0.5))
                tok_xn = kb.emit("dve", lambda e, bufn=bufn, xcv=xcv, s4=s4: e.tensor_scalar(
                    xnbf[:, bufn * C:(bufn + 1) * C], xcv, stats[:, s4:s4 + 1],
                    None, OP.mult),
                    waits=[ln_rstd[bufx], tok_var] +
                          ([prev_tr[bufn]] if prev_tr[bufn] else []))
                norm_of[it] = tok_xn

            def emit_ln_tr(it):
                bufn = it % 6
                bt = it % 2
                wfree2 = [tr_evac[bt]] if tr_evac[bt] else []
                tok_tr = None
                for c in range(8):
                    tok_tr = kb.emit("pe", lambda e, c=c, bt=bt, bufn=bufn: e.transpose(
                        ptrs[bt][:, 128 * c:128 * (c + 1)],
                        xnbf[:, bufn * C + 128 * c: bufn * C + 128 * (c + 1)],
                        idb[:, :]),
                        waits=([norm_of[it], tok_idb] + wfree2) if c == 0 else ())
                tr_evac[bt] = kb.emit("act", lambda e, it=it, bt=bt: e.copy(
                    xnT[:, :].rearrange("p (a n) -> p a n", n=4096)[
                        :, :, 128 * it:128 * (it + 1)],
                    ptrs[bt][:, :].rearrange("p (a n) -> p a n", n=128)),
                    waits=[tok_tr])
                prev_tr[bufn] = tok_tr
                xnT_last[0] = tr_evac[bt]

            emit_wi_load(0)
            emit_wo_load(0)

            for e_i in range(E):
                for it in range(4 * e_i, 4 * e_i + 4):
                    emit_ln_ladder(it)
                emit_m_expert(e_i)
                for it in range(4 * e_i, 4 * e_i + 4):
                    emit_ln_tr(it)
            xnT_all = xnT_last[0]

        kb.barrier()
        with nc.sbuf_tensor("wq", [128, 2 * 4096], BF16) as wq, \
             nc.sbuf_tensor("vast", [128, 2 * H * VW], BF16) as vast, \
             nc.sbuf_tensor("kq_st", [128, 2 * 512], BF16) as kq_st, \
             nc.sbuf_tensor("vbias", [128, C], F32) as vbias, \
             nc.sbuf_tensor("vb_row", [1, C], F32) as vb_row, \
             nc.psum_tensor("pq0", [128, 512], F32) as pq0, \
             nc.psum_tensor("pq1", [128, 512], F32) as pq1:
            pqs = [pq0, pq1]
            tok_vbr = kb.dma("sp", "ini", lambda e: e.dma_start(
                out=vb_row[0:1, :],
                in_=bqkv_d.ap().rearrange("(a n) -> a n", a=1)[:, 2 * C:3 * C]))
            tok_vb = build_bias_row(vbias, pq0, pq1, vb_row, tok_vbr)
            wq_free = [None, None]
            wq_tok = [None] * 6
            evac_tok = [None, None]
            kq_dma = [None, None]
            for j in range(4):
                bw = j % 2
                wfree = [wq_free[bw]] if wq_free[bw] else []
                wq_tok[j] = kb.dma("pool", f"wq{bw}", lambda e, j=j, bw=bw: e.dma_start(
                    out=wq[:, bw * 4096:(bw + 1) * 4096].rearrange("p (c n) -> p c n", n=512),
                    in_=wqkv_d[:, 512 * j:512 * (j + 1)].rearrange("(c p) n -> p c n", p=128)),
                    waits=wfree)
                nch = 4 if j < 2 else 8
                t0 = T0 if j < 2 else 0
                for m in range(4):
                    pair = 4 * (j % 2) + m
                    bcol = 4 * j + m
                    for n in range(nch):
                        bp = (m * nch + n) % 2
                        wfree2 = [evac_tok[bp]] if evac_tok[bp] else []
                        tok_mm = None
                        for kc in range(8):
                            tok_mm = kb.emit("pe", lambda e, bp=bp, bw=bw, kc=kc, m=m, n=n, t0=t0: e.matmul(
                                pqs[bp][:, :],
                                wq[:, bw * 4096 + 512 * kc + 128 * m: bw * 4096 + 512 * kc + 128 * (m + 1)],
                                xnT[:, 4096 * kc + t0 + 512 * n: 4096 * kc + t0 + 512 * (n + 1)],
                                start=(kc == 0), stop=(kc == 7)),
                                waits=[wq_tok[j], xnT_all, tok_vb] + wfree2)
                        wq_free[bw] = tok_mm
                        wst = [kq_dma[bp]] if kq_dma[bp] else []
                        evac_tok[bp] = kb.emit("dve", lambda e, bp=bp, bcol=bcol: e.tensor_scalar(
                            kq_st[:, bp * 512:(bp + 1) * 512], pqs[bp][:, :],
                            bqkT[:, bcol:bcol + 1], None, OP.add),
                            waits=[tok_mm, tok_bq] + wst)
                        if j < 2:
                            dst = q_d[pair, :, 512 * n: 512 * (n + 1)]
                        else:
                            dst = kt_d[pair, :, 512 * n: 512 * (n + 1)]
                        kq_dma[bp] = kb.dma("sp", f"kq{bp}", lambda e, dst=dst, bp=bp: e.dma_start(
                            out=dst, in_=kq_st[:, bp * 512:(bp + 1) * 512]),
                            waits=[evac_tok[bp]])
            for j in (4, 5):
                bw = j % 2
                wfree = [wq_free[bw]] if wq_free[bw] else []
                wq_tok[j] = kb.dma("pool", f"wq{bw}", lambda e, j=j, bw=bw: e.dma_start(
                    out=wq[:, bw * 4096:(bw + 1) * 4096].rearrange("p (c n) -> p c n", n=512),
                    in_=wqkv_d[:, 512 * j:512 * (j + 1)].rearrange("(c p) n -> p c n", p=128)),
                    waits=wfree)
            tok_vones = kb.emit("dve", lambda e: e.memset(
                vast[:, :].rearrange("p (g c) -> p g c", c=VW)[:, :, 64:65], 1.0))
            va_store = [None, None]
            for it in range(NT):
                bv = it % 2
                ev_toks = []
                for jj in (4, 5):
                    bw = jj % 2
                    bp = (it * 2 + (jj - 4)) % 2
                    wfree2 = [evac_tok[bp]] if evac_tok[bp] else []
                    tok_mm = None
                    for kc in range(8):
                        tok_mm = kb.emit("pe", lambda e, bp=bp, bw=bw, kc=kc, it=it: e.matmul(
                            pqs[bp][:, :],
                            xnT[:, 4096 * kc + 128 * it: 4096 * kc + 128 * (it + 1)],
                            wq[:, bw * 4096 + 512 * kc: bw * 4096 + 512 * (kc + 1)],
                            start=(kc == 0), stop=(kc == 7)),
                            waits=[wq_tok[jj], xnT_all] + wfree2)
                    wq_free[bw] = tok_mm
                    wst = [va_store[bv]] if va_store[bv] else []
                    g0 = 8 * (jj - 4)
                    evac_tok[bp] = kb.emit("dve", lambda e, bp=bp, bv=bv, g0=g0, jj=jj: e.tensor_tensor(
                        vast[:, bv * H * VW:(bv + 1) * H * VW].rearrange(
                            "p (g c) -> p g c", c=VW)[:, g0:g0 + 8, 0:64],
                        pqs[bp][:, :].rearrange("p (g c) -> p g c", c=64),
                        vbias[:, 512 * (jj - 4):512 * (jj - 3)].rearrange(
                            "p (g c) -> p g c", c=64),
                        OP.add),
                        waits=[tok_mm, tok_vones, tok_vb] + wst)
                    ev_toks.append(evac_tok[bp])
                va_store[bv] = kb.dma("sp", f"vas{bv}", lambda e, it=it, bv=bv: e.dma_start(
                    out=va_d[it, :, :], in_=vast[:, bv * H * VW:(bv + 1) * H * VW]),
                    waits=ev_toks)
    kb.barrier()

    msb_stack = contextlib.ExitStack()
    msb = msb_stack.enter_context(nc.sbuf_tensor("msb", [128, 4 * E * C], BF16))
    m_tok = [None] * E
    with contextlib.ExitStack() as p2s:
        _e = p2s.enter_context
        wproj_sb = _e(nc.sbuf_tensor("wproj_sb", [128, E * C], BF16))
        bprojB = _e(nc.sbuf_tensor("bprojB", [128, C], F32))
        var = _e(nc.sbuf_tensor("var", [128, 17 * H * VW], BF16))
        ktr = _e(nc.sbuf_tensor("ktr", [128, 17 * 1024], BF16))
        qtr = _e(nc.sbuf_tensor("qtr", [128, 2 * 1024], BF16))
        mkld = _e(nc.sbuf_tensor("mkld", [128, 2 * NB * 128], BF16))
        psb = _e(nc.sbuf_tensor("psb", [128, 3 * NB * 128], BF16))
        attnA = _e(nc.sbuf_tensor("attnA", [128, 2 * 64], BF16))
        attnT = _e(nc.sbuf_tensor("attnT", [128, 2 * 1024], BF16))
        zr = _e(nc.sbuf_tensor("zr", [128, 2], F32))
        xre = _e(nc.sbuf_tensor("xre", [128, 2 * C], F32))
        projsb = _e(nc.sbuf_tensor("projsb", [128, C], F32))
        t1sb = _e(nc.sbuf_tensor("t1sb", [128, C], F32))
        x1sb = _e(nc.sbuf_tensor("x1sb", [128, 2 * C], F32))
        tok_wp = kb.dma("pool", "wpj", lambda e: e.dma_start(
            out=wproj_sb[:, :].rearrange("p (c n) -> p c n", n=C),
            in_=wproj_d.ap().rearrange("(c p) n -> p c n", p=128)))
        for e_i in range(4):
            m_tok[e_i] = kb.dma("sp", f"ml{e_i % 4}", lambda e, e_i=e_i: e.dma_start(
                out=msb[:, e_i * E * C:(e_i + 1) * E * C].rearrange(
                    "p (c n) -> p c n", n=C),
                in_=m_d[e_i, :, :].rearrange("(c p) n -> p c n", p=128)))
        va_tok = {}
        kt_tok = {}
        slot_user = {}
        ktslot_user = {}
        qt_tok = [None, None]
        qt_used = [None, None]
        sps0 = _e(nc.psum_tensor("sps0", [128, 1024], F32))
        sps1 = _e(nc.psum_tensor("sps1", [128, 1024], F32))
        pvp0 = _e(nc.psum_tensor("pvp0", [128, 512], F32))
        pvp1 = _e(nc.psum_tensor("pvp1", [128, 512], F32))
        trp2 = _e(nc.psum_tensor("trp2", [64, 1024], BF16))
        prj = _e(nc.psum_tensor("prj", [128, 512], F32))
        if True:
            tok_bpb = build_bias_row(bprojB, pvp0, pvp1, bp_row, tok_bpr)
            sps = [sps0, sps1]
            pvp = [pvp0, pvp1]
            exp_tok = [None, None]
            msk_sl = [None, None, None]
            dve_norm = [None, None]
            attnA_user = [None, None]
            norm_tok = [None, None]
            tr_ev = None
            attnT_done = [None, None]
            proj_evac_last = None
            projsb_free = None
            x1_store = [None, None]
            xre_used = [None, None]
            xre_tok = [None, None]
            mk_tok = [None, None]
            mk_used = [None, None]

            def emit_scores(i, h):
                bi, bh = i % 2, h % 2
                vi = 16 + i
                bs = [vi - d for d in (16, 8, 4, 2, 1, 0)]
                pair, pb = h >> 1, 64 * (h & 1)
                wsc = ([exp_tok[bh]] if exp_tok[bh] else []) + [qt_tok[bi]]
                tok_sc0 = None
                for idx, b in enumerate(bs):
                    slot = b % 17
                    tok_sc0 = kb.emit("pe", lambda e, bh=bh, idx=idx, slot=slot, pair=pair, pb=pb, bi=bi: e.matmul(
                        sps[bh][:, 128 * idx:128 * (idx + 1)],
                        ktr[pb:pb + 64, slot * 1024 + 128 * pair: slot * 1024 + 128 * (pair + 1)],
                        qtr[pb:pb + 64, bi * 1024 + 128 * pair: bi * 1024 + 128 * (pair + 1)],
                        start=(idx in (0, 4)), stop=(idx in (3, 5))),
                        waits=wsc + [kt_tok[b]])
                    ktslot_user[slot] = tok_sc0
                qt_used[bi] = tok_sc0
                sl = h % 3
                exp_tok[bh] = kb.emit("act", lambda e, bh=bh, sl=sl: e.activation(
                    psb[:, sl * 768: sl * 768 + 768], sps[bh][:, 0:768],
                    AF.Exp, bias=0.0, scale=SCALE),
                    waits=[tok_sc0, msk_sl[sl]])

            def emit_mask(i, h):
                bi, bh = i % 2, h % 2
                sl = h % 3
                msk_sl[sl] = kb.emit("dve", lambda e, sl=sl, bi=bi: e.tensor_mul(
                    psb[:, sl * 768: sl * 768 + 768],
                    psb[:, sl * 768: sl * 768 + 768],
                    mkld[:, bi * 768: bi * 768 + 768]),
                    waits=[exp_tok[bh], mk_tok[bi]])
                mk_used[bi] = msk_sl[sl]

            def emit_pv(i, h):
                bh = h % 2
                vi = 16 + i
                bs = [vi - d for d in (16, 8, 4, 2, 1, 0)]
                wpv = ([dve_norm[bh]] if dve_norm[bh] else []) + [msk_sl[h % 3], tok_bpb] + \
                    [va_tok[b] for b in bs]
                tok_pv = None
                sl = h % 3
                for idx, b in enumerate(bs):
                    slot = b % 17
                    tok_pv = kb.emit("pe", lambda e, bh=bh, sl=sl, idx=idx, slot=slot, h=h: e.matmul(
                        pvp[bh][:, 0:VW],
                        psb[:, sl * 768 + 128 * idx: sl * 768 + 128 * (idx + 1)],
                        var[:, slot * H * VW + VW * h: slot * H * VW + VW * (h + 1)],
                        start=(idx == 0), stop=(idx == 5)),
                        waits=wpv)
                    slot_user[slot] = tok_pv
                kb.emit("dve", lambda e, bh=bh: e.reciprocal(
                    zr[:, bh:bh + 1], pvp[bh][:, 64:65]), waits=[tok_pv])
                wn = [attnA_user[bh]] if attnA_user[bh] else []
                norm_tok[bh] = kb.emit("dve", lambda e, bh=bh: e.tensor_scalar(
                    attnA[:, bh * 64:(bh + 1) * 64], pvp[bh][:, 0:64],
                    zr[:, bh:bh + 1], None, OP.mult), waits=wn)
                dve_norm[bh] = norm_tok[bh]

            def emit_tr(i, h):
                nonlocal tr_ev
                bi, bh = i % 2, h % 2
                pair, pb = h >> 1, 64 * (h & 1)
                wtr = [tr_ev] if tr_ev else []
                tok_tr2 = kb.emit("pe", lambda e, bh=bh: e.transpose(
                    trp2[:, 0:128], attnA[:, bh * 64:(bh + 1) * 64], idb[:, :]),
                    waits=[norm_tok[bh]] + wtr)
                attnA_user[bh] = tok_tr2
                tr_ev = kb.emit("act", lambda e, bi=bi, pb=pb, pair=pair: e.copy(
                    attnT[pb:pb + 64, bi * 1024 + 128 * pair: bi * 1024 + 128 * (pair + 1)],
                    trp2[0:64, 0:128]), waits=[tok_tr2])
                attnT_done[bi] = tr_ev

            def emit_proj_half(ip, n):
                nonlocal proj_evac_last
                bip = ip % 2
                tok_pj = None
                for kc in range(8):
                    tok_pj = kb.emit("pe", lambda e, bip=bip, kc=kc, n=n: e.matmul(
                        prj[:, :],
                        attnT[:, bip * 1024 + 128 * kc: bip * 1024 + 128 * (kc + 1)],
                        wproj_sb[:, 1024 * kc + 512 * n: 1024 * kc + 512 * (n + 1)],
                        start=(kc == 0), stop=(kc == 7)),
                        waits=[attnT_done[bip], tok_wp] +
                              ([proj_evac_last] if proj_evac_last else []))
                wev = [projsb_free] if (n == 0 and projsb_free) else []
                proj_evac_last = kb.emit("act", lambda e, n=n: e.copy(
                    projsb[:, 512 * n:512 * (n + 1)], prj[:, :]),
                    waits=[tok_pj] + wev)

            def emit_x1(ip):
                nonlocal projsb_free
                bip = ip % 2
                tok_t1 = kb.emit("pool", lambda e: e.tensor_add(
                    t1sb[:, :], projsb[:, :], bprojB[:, :]),
                    waits=[proj_evac_last, tok_bpb])
                projsb_free = tok_t1
                prev_x1 = [x1_store[bip]] if x1_store[bip] else []
                xre_used[bip] = kb.emit("pool", lambda e, bip=bip: e.tensor_add(
                    x1sb[:, bip * C:(bip + 1) * C], t1sb[:, :],
                    xre[:, bip * C:(bip + 1) * C]),
                    waits=[tok_t1, xre_tok[bip]] + prev_x1)
                x1_store[bip] = kb.dma("sp", f"x1s{bip}", lambda e, ip=ip, bip=bip: e.dma_start(
                    out=x1_d[128 * ip:128 * (ip + 1), :],
                    in_=x1sb[:, bip * C:(bip + 1) * C]), waits=[xre_used[bip]])

            for i in range(OWN):
                vi = 16 + i
                bs = [vi - d for d in (16, 8, 4, 2, 1, 0)]
                bi = i % 2
                for b in bs:
                    if b not in va_tok:
                        slot = b % 17
                        wfree = [slot_user[slot]] if slot in slot_user else []
                        va_tok[b] = kb.dma("sp", f"val{slot % 4}", lambda e, b=b, slot=slot: e.dma_start(
                            out=var[:, slot * H * VW:(slot + 1) * H * VW],
                            in_=va_d[b, :, :]), waits=wfree)
                    if b not in kt_tok:
                        slot = b % 17
                        wfree = [ktslot_user[slot]] if slot in ktslot_user else []
                        kt_tok[b] = kb.dma("sp", f"ktl{slot % 4}", lambda e, b=b, slot=slot: e.dma_start(
                            out=ktr[:, slot * 1024:(slot + 1) * 1024].rearrange(
                                "p (a n) -> p a n", n=128),
                            in_=kt_d[:, :, 128 * b:128 * (b + 1)].rearrange(
                                "a p n -> p a n")), waits=wfree)
                wqt = [qt_used[bi]] if qt_used[bi] else []
                qt_tok[bi] = kb.dma("sp", f"qt{bi}", lambda e, i=i, bi=bi: e.dma_start(
                    out=qtr[:, bi * 1024:(bi + 1) * 1024].rearrange("p (a n) -> p a n", n=128),
                    in_=q_d[:, :, 128 * i:128 * (i + 1)].rearrange("a p n -> p a n")),
                    waits=wqt)
                wmk = [mk_used[bi]] if mk_used[bi] else []
                mk_tok[bi] = kb.dma("sp", f"mk{bi}", lambda e, i=i, bi=bi: e.dma_start(
                    out=mkld[:, bi * NB * 128:(bi + 1) * NB * 128], in_=mk_d[i, :, :]),
                    waits=wmk)
                wxf = [xre_used[bi]] if xre_used[bi] else []
                xre_tok[bi] = kb.dma("sp", f"xre{bi}", lambda e, i=i, bi=bi: e.dma_start(
                    out=xre[:, bi * C:(bi + 1) * C],
                    in_=x_d[T0 + 128 * i: T0 + 128 * (i + 1), :]), waits=wxf)
                for step in range(H + 3):
                    if step == 0 and i > 0:
                        emit_proj_half(i - 1, 0)
                    if step == 8 and i > 0:
                        emit_proj_half(i - 1, 1)
                        emit_x1(i - 1)
                    if step < H:
                        emit_scores(i, step)
                    if 2 <= step <= H + 1:
                        emit_pv(i, step - 2)
                    if step < H:
                        emit_mask(i, step)
                    if step >= 3:
                        emit_tr(i, step - 3)
            emit_proj_half(OWN - 1, 0)
            emit_proj_half(OWN - 1, 1)
            emit_x1(OWN - 1)
    kb.barrier()

    with nc.sbuf_tensor("xn1T", [128, OWN * C], BF16) as xn1T, \
         nc.sbuf_tensor("yacc", [128, OWN * C], F32) as yacc, \
         nc.sbuf_tensor("x1ld", [128, 2 * C], F32) as x1ld, \
         nc.sbuf_tensor("xc2", [128, 2 * C], F32) as xc2, \
         nc.sbuf_tensor("xn1bf", [128, 4 * C], BF16) as xn1bf, \
         nc.sbuf_tensor("st2", [128, 16], F32) as st2, \
         nc.sbuf_tensor("wall", [128, OWN * E], F32) as wall, \
         nc.sbuf_tensor("el", [128, 24], F32) as el, \
         nc.sbuf_tensor("outsb", [128, 2 * C], F32) as outsb:
        x1_free = [None, None]
        with nc.psum_tensor("py0", [128, 1024], F32) as py0, \
             nc.psum_tensor("py1", [128, 1024], F32) as py1:
            ev_tok = [[None, None], [None, None]]
            group_last_mm = [None] * 4
            pys = [(py0, py1), None]

            def emit_y_tile(g, i, bg):
                wmm = [t for t in ev_tok[bg] if t]
                tok_y = [None, None]
                for kc in range(8):
                    for e_loc in range(2):
                        e_g = 2 * g + e_loc
                        ms = (e_g % 4) * E * C
                        for n in range(2):
                            tok_y[e_loc] = kb.emit("pe", lambda e, bg=bg, e_loc=e_loc, i=i, kc=kc, n=n, ms=ms: e.matmul(
                                pys[bg][e_loc][:, 512 * n:512 * (n + 1)],
                                xn1T[:, 1024 * i + 128 * kc: 1024 * i + 128 * (kc + 1)],
                                msb[:, ms + 1024 * kc + 512 * n: ms + 1024 * kc + 512 * (n + 1)],
                                start=(kc == 0), stop=(kc == 7)),
                                waits=[m_tok[2 * g + e_loc]] + (wmm if kc == 0 else []))
                for e_loc in range(2):
                    e_g = 2 * g + e_loc
                    if e_g == 0:
                        ev = kb.emit("dve", lambda e, bg=bg, e_loc=e_loc, i=i, e_g=e_g: e.tensor_scalar(
                            yacc[:, C * i:C * (i + 1)], pys[bg][e_loc][:, :],
                            wall[:, E * i + e_g:E * i + e_g + 1], None, OP.mult),
                            waits=[tok_y[e_loc]])
                    else:
                        ev = kb.emit("dve", lambda e, bg=bg, e_loc=e_loc, i=i, e_g=e_g: e.scalar_tensor_tensor(
                            yacc[:, C * i:C * (i + 1)], pys[bg][e_loc][:, :],
                            wall[:, E * i + e_g:E * i + e_g + 1],
                            yacc[:, C * i:C * (i + 1)], OP.mult, OP.add),
                            waits=[tok_y[e_loc]])
                    ev_tok[bg][e_loc] = ev
                return tok_y[1]

            with nc.psum_tensor("ptr3a", [128, 1024], BF16) as ptr3a, \
                 nc.psum_tensor("ptr3b", [128, 1024], BF16) as ptr3b, \
                 nc.psum_tensor("prt", [128, 512], F32) as prt:
                ptrs3 = [ptr3a, ptr3b]
                tr_ev3 = [None, None]
                last_norm2 = [None]
                norm3_of = {}
                rt_done = None
                prev_rt_w = None
                prev_rstd2 = [None, None]
                prev_last_tr3 = [None, None, None, None]
                def p3_lad(i):
                    bi = i % 2
                    wfree = [x1_free[bi]] if x1_free[bi] else []
                    tok_x1 = kb.dma("sp", f"x1l{bi}", lambda e, i=i, bi=bi: e.dma_start(
                        out=x1ld[:, bi * C:(bi + 1) * C],
                        in_=x1_d[128 * i:128 * (i + 1), :]), waits=wfree)
                    xv = x1ld[:, bi * C:(bi + 1) * C]
                    xcv = xc2[:, bi * C:(bi + 1) * C]
                    s0, s1, s2, s3, s4 = (8 * bi + k for k in range(5))
                    tok_sum = kb.emit("act", lambda e, xv=xv, xcv=xcv, s0=s0: e.activation(
                        xcv, xv, AF.Copy, bias=0.0, scale=1.0,
                        accum_out=st2[:, s0:s0 + 1]),
                        waits=[tok_x1] +
                              ([prev_rstd2[bi]] if prev_rstd2[bi] else []) +
                              ([norm3_of[i - 2]] if i >= 2 else []))
                    kb.emit("dve", lambda e, s0=s0, s1=s1: e.tensor_scalar(
                        st2[:, s1:s1 + 1], st2[:, s0:s0 + 1], -1.0 / C, None, OP.mult),
                        waits=[tok_sum])
                    kb.emit("dve", lambda e, xv=xv, xcv=xcv, s1=s1: e.tensor_scalar(
                        xcv, xv, st2[:, s1:s1 + 1], None, OP.add))
                    tok_var = kb.emit("dve", lambda e, xv=xv, xcv=xcv, s2=s2: e.scalar_tensor_tensor(
                        xv, xcv, 1.0, xv, OP.mult, OP.mult,
                        accum_out=st2[:, s2:s2 + 1]))
                    x1_free[bi] = tok_var
                    kb.emit("act", lambda e, s2=s2, s3=s3: e.activation(
                        st2[:, s3:s3 + 1], st2[:, s2:s2 + 1], AF.Ln, bias=epsb[:, 0:1],
                        scale=1.0 / C), waits=[tok_var, tok_eps])
                    prev_rstd2[bi] = kb.emit("act", lambda e, s3=s3, s4=s4: e.activation(
                        st2[:, s4:s4 + 1], st2[:, s3:s3 + 1], AF.Exp, bias=0.0,
                        scale=-0.5))
                    bn = i % 4
                    tok_xn1 = kb.emit("dve", lambda e, bn=bn, xcv=xcv, s4=s4: e.tensor_scalar(
                        xn1bf[:, bn * C:(bn + 1) * C], xcv, st2[:, s4:s4 + 1],
                        None, OP.mult),
                        waits=[prev_rstd2[bi], tok_var] +
                              ([prev_last_tr3[bn]] if prev_last_tr3[bn] else []))
                    norm3_of[i] = tok_xn1
                def p3_work_a(i):
                    bi = i % 2
                    bn = i % 4
                    bt = i % 2
                    wfree2 = [tr_ev3[bt]] if tr_ev3[bt] else []
                    tok_tr = None
                    for c in range(8):
                        tok_tr = kb.emit("pe", lambda e, c=c, bt=bt, bn=bn: e.transpose(
                            ptrs3[bt][:, 128 * c:128 * (c + 1)],
                            xn1bf[:, bn * C + 128 * c: bn * C + 128 * (c + 1)], idb[:, :]),
                            waits=([norm3_of[i], tok_idb] + wfree2) if c == 0 else ())
                    prev_last_tr3[bn] = tok_tr
                    tr_ev3[bt] = kb.emit("act", lambda e, i=i, bt=bt: e.copy(
                        xn1T[:, 1024 * i: 1024 * (i + 1)], ptrs3[bt][:, :]),
                        waits=[tok_tr])
                    tr_done[i] = tr_ev3[bt]

                def p3_work_b(i):
                    nonlocal rt_done, prev_rt_w
                    bi = i % 2
                    tr_last = tr_done[i]
                    wrt = [rt_done] if rt_done else []
                    tok_rt = None
                    for kc in range(8):
                        tok_rt = kb.emit("pe", lambda e, i=i, kc=kc: e.matmul(
                            prt[:, 0:E],
                            xn1T[:, 1024 * i + 128 * kc: 1024 * i + 128 * (kc + 1)],
                            wr_sb[:, E * kc: E * (kc + 1)],
                            start=(kc == 0), stop=(kc == 7)),
                            waits=[tr_last, tok_wr] + wrt)
                    tok_el = kb.emit("act", lambda e: e.activation(
                        el[:, 0:8], prt[:, 0:E], AF.Exp, bias=0.0, scale=1.0),
                        waits=[tok_rt] + ([prev_rt_w] if prev_rt_w else []))
                    rt_done = tok_el
                    kb.emit("dve", lambda e: e.reduce_sum(st2[:, 5:6], el[:, 0:8], AX),
                            waits=[tok_el])
                    kb.emit("dve", lambda e: e.reduce_max(st2[:, 6:7], el[:, 0:8], AX))
                    kb.emit("dve", lambda e: e.tensor_scalar(
                        el[:, 8:16], el[:, 0:8], st2[:, 6:7], None, OP.is_equal))
                    kb.emit("dve", lambda e: e.tensor_scalar(
                        el[:, 8:16], el[:, 8:16], -1.0, 1.0, OP.mult, OP.add))
                    kb.emit("dve", lambda e: e.tensor_mul(el[:, 16:24], el[:, 0:8], el[:, 8:16]))
                    kb.emit("dve", lambda e: e.reduce_max(st2[:, 7:8], el[:, 16:24], AX))
                    kb.emit("dve", lambda e: e.tensor_scalar(
                        el[:, 8:16], el[:, 0:8], st2[:, 7:8], None, OP.is_ge))
                    kb.emit("dve", lambda e: e.tensor_mul(el[:, 16:24], el[:, 0:8], el[:, 8:16]))
                    kb.emit("dve", lambda e: e.reciprocal(st2[:, 5:6], st2[:, 5:6]))
                    prev_rt_w = kb.emit("dve", lambda e, i=i: e.tensor_scalar(
                        wall[:, E * i:E * (i + 1)], el[:, 16:24], st2[:, 5:6], None, OP.mult))

                    group_last_mm[0] = emit_y_tile(0, i, 0)

                tr_done = {}
                for i in range(OWN + 3):
                    if i < OWN:
                        p3_lad(i)
                    if 2 <= i < OWN + 2:
                        p3_work_a(i - 2)
                    if i >= 3:
                        p3_work_b(i - 3)

            kb.wait("pe", "act", kb.cnt["act"])
            kb.wait("pe", "dve", kb.cnt["dve"])
            with nc.psum_tensor("py2", [128, 1024], F32) as py2, \
                 nc.psum_tensor("py3", [128, 1024], F32) as py3:
                pys[1] = (py2, py3)
                for g in range(1, 4):
                    for e_loc in range(2):
                        e_g = 2 * g + e_loc
                        if m_tok[e_g] is None:
                            m_tok[e_g] = kb.dma("sp", f"ml{e_g % 4}", lambda e, e_g=e_g: e.dma_start(
                                out=msb[:, (e_g % 4) * E * C:((e_g % 4) + 1) * E * C].rearrange(
                                    "p (c n) -> p c n", n=C),
                                in_=m_d[e_g, :, :].rearrange("(c p) n -> p c n", p=128)),
                                waits=[group_last_mm[g - 2]])
                    out_store = [None, None]
                    x1_free2 = list(x1_free)

                    def emit_out_tile(i, last_ev):
                        bi = i % 2
                        wfree4 = [x1_free2[bi]] if x1_free2[bi] else []
                        tok_x1 = kb.dma("sp", f"x1l{bi}", lambda e, i=i, bi=bi: e.dma_start(
                            out=x1ld[:, bi * C:(bi + 1) * C],
                            in_=x1_d[128 * i:128 * (i + 1), :]), waits=wfree4)
                        wos = [out_store[bi]] if out_store[bi] else []
                        x1_free2[bi] = kb.emit("dve", lambda e, i=i, bi=bi: e.tensor_add(
                            outsb[:, bi * C:(bi + 1) * C], yacc[:, C * i:C * (i + 1)],
                            x1ld[:, bi * C:(bi + 1) * C]),
                            waits=[tok_x1, last_ev] + wos)
                        out_store[bi] = kb.dma("sp", f"out{bi}", lambda e, i=i, bi=bi: e.dma_start(
                            out=out_d[128 * i:128 * (i + 1), :],
                            in_=outsb[:, bi * C:(bi + 1) * C]), waits=[x1_free2[bi]])

                    for i in range(OWN):
                        bg = (g * OWN + i) % 2
                        group_last_mm[g] = emit_y_tile(g, i, bg)
                        if g == 3:
                            emit_out_tile(i, ev_tok[bg][1])
        kb.wait("sp", "out0", kb.cnt["out0"])
        kb.wait("sp", "out1", kb.cnt["out1"])

    msb_stack.close()
    kb.replay()
    kb.stack.close()
    return nc


_CACHE = {}


def kernel(L0_attn_x, L0_attn_W_qkv, L0_attn_b_qkv, L0_attn_W_proj, L0_attn_b_proj,
           L0_ffn_Wr, L0_ffn_Wi, L0_ffn_Wo):
    x = np.ascontiguousarray(np.asarray(L0_attn_x, dtype=np.float32))
    wqkv = np.ascontiguousarray(np.asarray(L0_attn_W_qkv, dtype=np.float32))
    bqkv = np.ascontiguousarray(np.asarray(L0_attn_b_qkv, dtype=np.float32))
    wproj = np.ascontiguousarray(np.asarray(L0_attn_W_proj, dtype=np.float32))
    bproj = np.ascontiguousarray(np.asarray(L0_attn_b_proj, dtype=np.float32))
    wr = np.ascontiguousarray(np.asarray(L0_ffn_Wr, dtype=np.float32))
    wi = np.ascontiguousarray(np.swapaxes(np.asarray(L0_ffn_Wi, dtype=np.float32), 1, 2))
    wo = np.ascontiguousarray(np.asarray(L0_ffn_Wo, dtype=np.float32))
    idn = np.eye(128, dtype=np.float32)
    masks = [build_masks(0), build_masks(1)]

    if "nc" not in _CACHE:
        _CACHE["nc"] = build_program()
    nc = _CACHE["nc"]

    in_maps = []
    for core in range(8):
        b, half = core >> 1, core & 1
        if half == 0:
            xv = np.concatenate(
                [np.zeros((OWN_TOK, C), np.float32), x[b, 0:OWN_TOK]], axis=0)
        else:
            xv = np.ascontiguousarray(x[b])
        in_maps.append({
            "x": xv, "wqkv": wqkv, "bqkv": bqkv, "wproj": wproj, "bproj": bproj,
            "wr": wr, "wi": wi, "wo": wo, "idn": idn, "mk": masks[half],
        })
    res = run_bass_kernel_spmd(nc, in_maps, core_ids=list(range(8)))
    out = np.empty((B, T, C), dtype=np.float32)
    for core in range(8):
        b, half = core >> 1, core & 1
        out[b, half * OWN_TOK:(half + 1) * OWN_TOK] = res.results[core]["out"]
    return out


# version 91
# speedup vs baseline: 1.1616x; 1.0006x over previous
import contextlib
import numpy as np
import ml_dtypes
import concourse.bass as bass
import concourse.mybir as mybir
from concourse.bass_utils import run_bass_kernel_spmd

F32 = mybir.dt.float32
BF16 = mybir.dt.bfloat16
AF = mybir.ActivationFunctionType
OP = mybir.AluOpType
AX = mybir.AxisListType.X

B, T, C = 4, 4096, 1024
H = 16
NT = 32
OWN = 16
OWN_TOK = OWN * 128
E = 8
SCALE = 0.125
EPS = 1e-5
VW = 65
NB = 6
ENGINES = ("pe", "act", "dve", "pool", "sp")


def build_masks(half):
    s = np.arange(128)[:, None]
    t = np.arange(128)[None, :]
    m0 = np.isin(t - s, [0, 1, 2, 4, 8, 16, 32, 64]).astype(np.float32)
    m1 = np.isin(t - s + 128, [1, 2, 4, 8, 16, 32, 64, 128]).astype(np.float32)
    mD = np.eye(128, dtype=np.float32)
    out = np.zeros((OWN, 128, NB * 128), dtype=np.float32)
    for i in range(OWN):
        vi = 16 + i
        for idx, d in enumerate((16, 8, 4, 2, 1, 0)):
            vb = vi - d
            real = vb - 16 + 16 * half
            if real >= 0:
                kind = m0 if d == 0 else (m1 if d == 1 else mD)
                out[i, :, 128 * idx:128 * (idx + 1)] = kind
    return out.astype(ml_dtypes.bfloat16)


class KB:

    def __init__(self, nc):
        self.nc = nc
        self.items = {e: [] for e in ENGINES}
        self.stack = contextlib.ExitStack()
        self.sems = {}
        self.cnt = {}
        self.hwm = {e: {} for e in ENGINES}
        for s in ("pe", "act", "dve", "pool"):
            self.sems[s] = self.stack.enter_context(nc.semaphore(s))
            self.cnt[s] = 0

    def wait(self, eng, sem, val):
        if val <= 0 or self.hwm[eng].get(sem, 0) >= val:
            return
        self.hwm[eng][sem] = val
        s = self.sems[sem]
        self.items[eng].append(lambda EE, s=s, v=val: EE.wait_ge(s, v))

    def emit(self, eng, fn, waits=()):
        if eng in ("dve", "act", "pool"):
            self.wait(eng, eng, self.cnt[eng])
        for w in waits:
            if w:
                self.wait(eng, w[0], w[1])
        self.cnt[eng] += 1
        s = self.sems[eng]
        self.items[eng].append(lambda EE, fn=fn, s=s: fn(EE).then_inc(s, 1))
        return (eng, self.cnt[eng])

    def dma(self, eng, stream, fn, waits=()):
        if stream not in self.sems:
            self.sems[stream] = self.stack.enter_context(self.nc.semaphore(stream))
            self.cnt[stream] = 0
        self.wait(eng, stream, self.cnt[stream])
        for w in waits:
            if w:
                self.wait(eng, w[0], w[1])
        self.cnt[stream] += 16
        s = self.sems[stream]
        self.items[eng].append(lambda EE, fn=fn, s=s: fn(EE).then_inc(s, 16))
        return (stream, self.cnt[stream])

    def barrier(self):
        toks = [(s, self.cnt[s]) for s in self.cnt]
        for e in ENGINES:
            for sem, val in toks:
                self.wait(e, sem, val)

    def replay(self):
        with self.nc.Block() as block:
            @block.tensor
            def _(t):
                for f in self.items["pe"]:
                    f(t)

            @block.scalar
            def _(s):
                for f in self.items["act"]:
                    f(s)

            @block.vector
            def _(v):
                for f in self.items["dve"]:
                    f(v)

            @block.gpsimd
            def _(g):
                for f in self.items["pool"]:
                    f(g)

            @block.sync
            def _(sy):
                for f in self.items["sp"]:
                    f(sy)


def build_program():
    nc = bass.Bass("TRN2", target_bir_lowering=False)
    dt = nc.dram_tensor
    x_d = dt("x", [T, C], F32, kind="ExternalInput")
    wqkv_d = dt("wqkv", [C, 3 * C], F32, kind="ExternalInput")
    bqkv_d = dt("bqkv", [3 * C], F32, kind="ExternalInput")
    wproj_d = dt("wproj", [C, C], F32, kind="ExternalInput")
    bproj_d = dt("bproj", [C], F32, kind="ExternalInput")
    wr_d = dt("wr", [C, E], F32, kind="ExternalInput")
    wi_d = dt("wi", [E, C, C], F32, kind="ExternalInput")
    wo_d = dt("wo", [E, C, C], F32, kind="ExternalInput")
    idn_d = dt("idn", [128, 128], F32, kind="ExternalInput")
    mk_d = dt("mk", [OWN, 128, NB * 128], BF16, kind="ExternalInput")
    out_d = dt("out", [OWN_TOK, C], F32, kind="ExternalOutput")
    m_d = dt("m_sc", [E, C, C], BF16)
    va_d = dt("va_sc", [NT, 128, H * VW], BF16)
    kt_d = dt("kt_sc", [8, 128, T], BF16)
    q_d = dt("q_sc", [8, 128, OWN_TOK], BF16)
    x1_d = dt("x1_sc", [OWN_TOK, C], F32)

    kb = KB(nc)
    st = kb.stack
    T0 = OWN_TOK

    idb = st.enter_context(nc.sbuf_tensor("idb", [128, 128], BF16))
    bqkT = st.enter_context(nc.sbuf_tensor("bqkT", [128, 16], F32))
    wr_sb = st.enter_context(nc.sbuf_tensor("wr_sb", [128, E * E], BF16))
    epsb = st.enter_context(nc.sbuf_tensor("epsb", [128, 1], F32))
    ones_row = st.enter_context(nc.sbuf_tensor("ones_row", [1, 128], F32))
    bp_row = st.enter_context(nc.sbuf_tensor("bp_row", [1, C], F32))

    tok_onr = kb.emit("dve", lambda e: e.memset(ones_row[:, :], 1.0))
    tok_bpr = kb.dma("sp", "ini", lambda e: e.dma_start(
        out=bp_row[0:1, :], in_=bproj_d.ap().rearrange("(a n) -> a n", a=1)))

    def build_bias_row(dst, ps_a, ps_b, row, t_r):
        one, t_o = ones_row, tok_onr
        toks = []
        for n in range(2):
            ps = ps_a if n == 0 else ps_b
            t_mm = kb.emit("pe", lambda e, ps=ps, n=n: e.matmul(
                ps[:, :], one[:, :], row[0:1, 512 * n: 512 * (n + 1)],
                start=True, stop=True),
                waits=[t_r, t_o])
            toks.append(kb.emit("dve", lambda e, ps=ps, n=n: e.tensor_copy(
                dst[:, 512 * n: 512 * (n + 1)], ps[:, :]), waits=[t_mm]))
        return toks[-1]

    tok_eps = kb.emit("dve", lambda e: e.memset(epsb[:, :], EPS))
    tok_idb = kb.dma("pool", "pini", lambda e: e.dma_start(out=idb[:, :], in_=idn_d[:, :]))
    tok_wr = kb.dma("pool", "pini", lambda e: e.dma_start(
        out=wr_sb[:, :].rearrange("p (c n) -> p c n", n=E),
        in_=wr_d.ap().rearrange("(c p) n -> p c n", p=128)))
    with nc.sbuf_tensor("idf", [16, 16], F32) as idf, \
         nc.sbuf_tensor("rowbuf", [16, 128], F32) as rowbuf:
        tok_id = kb.dma("sp", "ini", lambda e: e.dma_start(
            out=idf[:, :], in_=idn_d[0:16, 0:16]))
        with nc.psum_tensor("pbias", [128, 512], F32) as pbias:
            tok_rb = kb.dma("sp", "ini", lambda e: e.dma_start(
                out=rowbuf[:, :],
                in_=bqkv_d.ap().rearrange("(c p) -> c p", p=128)[0:16, :]))
            tok_tr = kb.emit("pe", lambda e: e.transpose(
                pbias[:, 0:16], rowbuf[:, :], idf[:, :]),
                waits=[tok_rb, tok_id])
            tok_bq = kb.emit("dve", lambda e: e.tensor_copy(bqkT[:, :], pbias[:, 0:16]),
                             waits=[tok_tr])
    kb.barrier()

    with nc.sbuf_tensor("xnT", [128, 8 * T], BF16) as xnT, \
         nc.sbuf_tensor("xld", [128, 2 * C], F32) as xld, \
         nc.sbuf_tensor("xc", [128, 2 * C], F32) as xc, \
         nc.sbuf_tensor("stats", [128, 16], F32) as stats, \
         nc.sbuf_tensor("xnbf", [128, 6 * C], BF16) as xnbf:
        with nc.sbuf_tensor("wiT", [128, 2 * E * C], BF16) as wiT, \
             nc.sbuf_tensor("wof", [128, 2 * 4 * C], F32) as wof, \
             nc.sbuf_tensor("wo_sb", [128, 2 * E * C], BF16) as wo_sb, \
             nc.sbuf_tensor("m_st", [128, 3 * C], BF16) as m_st, \
             nc.psum_tensor("pm0", [128, 512], F32) as pm0, \
             nc.psum_tensor("pm1", [128, 512], F32) as pm1, \
             nc.psum_tensor("pm2", [128, 512], F32) as pm2, \
             nc.psum_tensor("pm3", [128, 512], F32) as pm3, \
             nc.psum_tensor("pm4", [128, 512], F32) as pm4, \
             nc.psum_tensor("pm5", [128, 512], F32) as pm5, \
             nc.psum_tensor("ptr0", [128, 1024], BF16) as ptr0, \
             nc.psum_tensor("ptr1", [128, 1024], BF16) as ptr1:
            pms = [(pm0, pm1), (pm2, pm3), (pm4, pm5)]
            ptrs = [ptr0, ptr1]
            wi_tok = [None] * E
            wi_user = [None, None]
            wo_tok_l = [None] * E
            wof_cast = [None, None]
            last_mm_buf = [None, None]
            mst_free = [None, None, None]
            pm_free = [None, None, None]
            witr_ev = [None, None]
            wisb_user = [None, None]
            wo_cast = [None]

            def emit_wi_load(e_n):
                nbuf = e_n % 2
                wfree = [wi_user[nbuf]] if wi_user[nbuf] else []
                toks = []
                for hf in range(4):
                    toks.append(kb.dma("pool", f"wi{hf}", lambda e, e_n=e_n, nbuf=nbuf, hf=hf: e.dma_start(
                        out=wiT[:, nbuf * E * C + 2048 * hf: nbuf * E * C + 2048 * (hf + 1)].rearrange(
                            "p (c n) -> p c n", n=C),
                        in_=wi_d[e_n, 256 * hf: 256 * (hf + 1), :].rearrange(
                            "(c p) n -> p c n", p=128)),
                        waits=wfree))
                wi_tok[e_n] = toks

            def emit_wo_load(e_n):
                nbuf = e_n % 2
                wW2 = [last_mm_buf[nbuf]] if last_mm_buf[nbuf] else []
                tok_c = None
                for hf in range(2):
                    wfree3 = [wof_cast[hf]] if wof_cast[hf] else []
                    t_l = kb.dma("sp", f"wof{hf}", lambda e, e_n=e_n, hf=hf: e.dma_start(
                        out=wof[:, hf * 4 * C:(hf + 1) * 4 * C].rearrange(
                            "p (c n) -> p c n", n=C),
                        in_=wo_d[e_n, 512 * hf:512 * (hf + 1), :].rearrange(
                            "(c p) n -> p c n", p=128)),
                        waits=wfree3)
                    tok_c = kb.emit("act", lambda e, nbuf=nbuf, hf=hf: e.copy(
                        wo_sb[:, nbuf * E * C + 4096 * hf: nbuf * E * C + 4096 * (hf + 1)],
                        wof[:, hf * 4 * C:(hf + 1) * 4 * C]), waits=[t_l] + wW2)
                    wof_cast[hf] = tok_c
                wo_tok_l[e_n] = tok_c

            def emit_m_expert(e_i, fillers=()):
                buf = e_i % 2
                off = buf * E * C
                offw = (e_i % 2) * E * C
                fillers = list(fillers)
                if e_i + 1 < E and wo_tok_l[e_i + 1] is None:
                    emit_wo_load(e_i + 1)
                wW = [last_mm_buf[buf]] if last_mm_buf[buf] else []
                wo_tok = wo_tok_l[e_i]
                if e_i + 1 < E:
                    emit_wi_load(e_i + 1)
                wiT_tok = wi_tok[e_i]
                for dc in range(8):
                    pi = (e_i * 8 + dc) % 3
                    pp = pms[pi]
                    wfree = [pm_free[pi]] if pm_free[pi] else []
                    evac_toks = []
                    tok = None
                    for n in range(2):
                        for hc in range(8):
                            tok = kb.emit("pe", lambda e, pp=pp, n=n, hc=hc, dc=dc, off=off, offw=offw: e.matmul(
                                pp[n][:, :],
                                wiT[:, offw + 1024 * hc + 128 * dc: offw + 1024 * hc + 128 * (dc + 1)],
                                wo_sb[:, off + 1024 * hc + 512 * n: off + 1024 * hc + 512 * (n + 1)],
                                start=(hc == 0), stop=(hc == 7)),
                                waits=[wo_tok] + wiT_tok + wfree)
                        mb = pi
                        wfree2 = [mst_free[mb]] if mst_free[mb] else []
                        evac_toks.append(kb.emit("dve", lambda e, pp=pp, n=n, mb=mb: e.tensor_copy(
                            m_st[:, mb * C + 512 * n: mb * C + 512 * (n + 1)], pp[n][:, :]),
                            waits=[tok] + wfree2))
                    last_mm_buf[buf] = tok
                    wi_user[e_i % 2] = tok
                    pm_free[pi] = evac_toks[-1]
                    mb = pi
                    mst_free[mb] = kb.dma("sp", f"mst{mb}", lambda e, e_i=e_i, dc=dc, mb=mb: e.dma_start(
                        out=m_d[e_i, 128 * dc: 128 * (dc + 1), :],
                        in_=m_st[:, mb * C: (mb + 1) * C]), waits=evac_toks)
                    if dc % 2 == 1 and fillers:
                        fillers.pop(0)()

            xld_free = [None, None]
            norm_of = {}
            tr_evac = [None, None]
            ln_rstd = [None, None]
            prev_tr = [None] * 6
            xnT_last = [None]

            def emit_ln_ladder(it):
                bufx = it % 2
                bufn = it % 6
                wfree = [xld_free[bufx]] if xld_free[bufx] else []
                tok_x = kb.dma("sp", f"x{bufx}", lambda e, it=it, bufx=bufx: e.dma_start(
                    out=xld[:, bufx * C:(bufx + 1) * C],
                    in_=x_d[128 * it:128 * (it + 1), :]), waits=wfree)
                xv = xld[:, bufx * C:(bufx + 1) * C]
                xcv = xc[:, bufx * C:(bufx + 1) * C]
                s0, s1, s2, s3, s4 = (8 * bufx + k for k in range(5))
                tok_sum = kb.emit("act", lambda e, xv=xv, xcv=xcv, s0=s0: e.activation(
                    xcv, xv, AF.Copy, bias=0.0, scale=1.0,
                    accum_out=stats[:, s0:s0 + 1]),
                    waits=[tok_x] +
                          ([ln_rstd[bufx]] if ln_rstd[bufx] else []) +
                          ([norm_of[it - 2]] if it >= 2 else []))
                kb.emit("dve", lambda e, s0=s0, s1=s1: e.tensor_scalar(
                    stats[:, s1:s1 + 1], stats[:, s0:s0 + 1], -1.0 / C, None, OP.mult),
                    waits=[tok_sum])
                tok_xc = kb.emit("dve", lambda e, xv=xv, xcv=xcv, s1=s1: e.tensor_scalar(
                    xcv, xv, stats[:, s1:s1 + 1], None, OP.add))
                tok_var = kb.emit("dve", lambda e, xv=xv, xcv=xcv, s2=s2: e.scalar_tensor_tensor(
                    xv, xcv, 1.0, xv, OP.mult, OP.mult,
                    accum_out=stats[:, s2:s2 + 1]), waits=[tok_xc])
                xld_free[bufx] = tok_var
                kb.emit("act", lambda e, s2=s2, s3=s3: e.activation(
                    stats[:, s3:s3 + 1], stats[:, s2:s2 + 1], AF.Ln, bias=epsb[:, 0:1],
                    scale=1.0 / C), waits=[tok_var, tok_eps])
                ln_rstd[bufx] = kb.emit("act", lambda e, s3=s3, s4=s4: e.activation(
                    stats[:, s4:s4 + 1], stats[:, s3:s3 + 1], AF.Exp, bias=0.0,
                    scale=-0.5))
                tok_xn = kb.emit("dve", lambda e, bufn=bufn, xcv=xcv, s4=s4: e.tensor_scalar(
                    xnbf[:, bufn * C:(bufn + 1) * C], xcv, stats[:, s4:s4 + 1],
                    None, OP.mult),
                    waits=[ln_rstd[bufx], tok_var] +
                          ([prev_tr[bufn]] if prev_tr[bufn] else []))
                norm_of[it] = tok_xn

            def emit_ln_tr(it):
                bufn = it % 6
                bt = it % 2
                wfree2 = [tr_evac[bt]] if tr_evac[bt] else []
                tok_tr = None
                for c in range(8):
                    tok_tr = kb.emit("pe", lambda e, c=c, bt=bt, bufn=bufn: e.transpose(
                        ptrs[bt][:, 128 * c:128 * (c + 1)],
                        xnbf[:, bufn * C + 128 * c: bufn * C + 128 * (c + 1)],
                        idb[:, :]),
                        waits=([norm_of[it], tok_idb] + wfree2) if c == 0 else ())
                tr_evac[bt] = kb.emit("act", lambda e, it=it, bt=bt: e.copy(
                    xnT[:, :].rearrange("p (a n) -> p a n", n=4096)[
                        :, :, 128 * it:128 * (it + 1)],
                    ptrs[bt][:, :].rearrange("p (a n) -> p a n", n=128)),
                    waits=[tok_tr])
                prev_tr[bufn] = tok_tr
                xnT_last[0] = tr_evac[bt]

            emit_wi_load(0)
            emit_wo_load(0)

            for e_i in range(E):
                for it in range(4 * e_i, 4 * e_i + 4):
                    emit_ln_ladder(it)
                emit_m_expert(e_i)
                for it in range(4 * e_i, 4 * e_i + 4):
                    emit_ln_tr(it)
            xnT_all = xnT_last[0]

        kb.barrier()
        with nc.sbuf_tensor("wq", [128, 2 * 4096], BF16) as wq, \
             nc.sbuf_tensor("vast", [128, 2 * H * VW], BF16) as vast, \
             nc.sbuf_tensor("kq_st", [128, 2 * 512], BF16) as kq_st, \
             nc.sbuf_tensor("vbias", [128, C], F32) as vbias, \
             nc.sbuf_tensor("vb_row", [1, C], F32) as vb_row, \
             nc.psum_tensor("pq0", [128, 512], F32) as pq0, \
             nc.psum_tensor("pq1", [128, 512], F32) as pq1:
            pqs = [pq0, pq1]
            tok_vbr = kb.dma("sp", "ini", lambda e: e.dma_start(
                out=vb_row[0:1, :],
                in_=bqkv_d.ap().rearrange("(a n) -> a n", a=1)[:, 2 * C:3 * C]))
            tok_vb = build_bias_row(vbias, pq0, pq1, vb_row, tok_vbr)
            wq_free = [None, None]
            wq_tok = [None] * 6
            evac_tok = [None, None]
            kq_dma = [None, None]
            for j in range(4):
                bw = j % 2
                wfree = [wq_free[bw]] if wq_free[bw] else []
                wq_tok[j] = kb.dma("pool", f"wq{bw}", lambda e, j=j, bw=bw: e.dma_start(
                    out=wq[:, bw * 4096:(bw + 1) * 4096].rearrange("p (c n) -> p c n", n=512),
                    in_=wqkv_d[:, 512 * j:512 * (j + 1)].rearrange("(c p) n -> p c n", p=128)),
                    waits=wfree)
                nch = 4 if j < 2 else 8
                t0 = T0 if j < 2 else 0
                for m in range(4):
                    pair = 4 * (j % 2) + m
                    bcol = 4 * j + m
                    for n in range(nch):
                        bp = (m * nch + n) % 2
                        wfree2 = [evac_tok[bp]] if evac_tok[bp] else []
                        tok_mm = None
                        for kc in range(8):
                            tok_mm = kb.emit("pe", lambda e, bp=bp, bw=bw, kc=kc, m=m, n=n, t0=t0: e.matmul(
                                pqs[bp][:, :],
                                wq[:, bw * 4096 + 512 * kc + 128 * m: bw * 4096 + 512 * kc + 128 * (m + 1)],
                                xnT[:, 4096 * kc + t0 + 512 * n: 4096 * kc + t0 + 512 * (n + 1)],
                                start=(kc == 0), stop=(kc == 7)),
                                waits=[wq_tok[j], xnT_all, tok_vb] + wfree2)
                        wq_free[bw] = tok_mm
                        wst = [kq_dma[bp]] if kq_dma[bp] else []
                        evac_tok[bp] = kb.emit("dve", lambda e, bp=bp, bcol=bcol: e.tensor_scalar(
                            kq_st[:, bp * 512:(bp + 1) * 512], pqs[bp][:, :],
                            bqkT[:, bcol:bcol + 1], None, OP.add),
                            waits=[tok_mm, tok_bq] + wst)
                        if j < 2:
                            dst = q_d[pair, :, 512 * n: 512 * (n + 1)]
                        else:
                            dst = kt_d[pair, :, 512 * n: 512 * (n + 1)]
                        kq_dma[bp] = kb.dma("sp", f"kq{bp}", lambda e, dst=dst, bp=bp: e.dma_start(
                            out=dst, in_=kq_st[:, bp * 512:(bp + 1) * 512]),
                            waits=[evac_tok[bp]])
            for j in (4, 5):
                bw = j % 2
                wfree = [wq_free[bw]] if wq_free[bw] else []
                wq_tok[j] = kb.dma("pool", f"wq{bw}", lambda e, j=j, bw=bw: e.dma_start(
                    out=wq[:, bw * 4096:(bw + 1) * 4096].rearrange("p (c n) -> p c n", n=512),
                    in_=wqkv_d[:, 512 * j:512 * (j + 1)].rearrange("(c p) n -> p c n", p=128)),
                    waits=wfree)
            tok_vones = kb.emit("dve", lambda e: e.memset(
                vast[:, :].rearrange("p (g c) -> p g c", c=VW)[:, :, 64:65], 1.0))
            va_store = [None, None]
            for it in range(NT):
                bv = it % 2
                ev_toks = []
                for jj in (4, 5):
                    bw = jj % 2
                    bp = (it * 2 + (jj - 4)) % 2
                    wfree2 = [evac_tok[bp]] if evac_tok[bp] else []
                    tok_mm = None
                    for kc in range(8):
                        tok_mm = kb.emit("pe", lambda e, bp=bp, bw=bw, kc=kc, it=it: e.matmul(
                            pqs[bp][:, :],
                            xnT[:, 4096 * kc + 128 * it: 4096 * kc + 128 * (it + 1)],
                            wq[:, bw * 4096 + 512 * kc: bw * 4096 + 512 * (kc + 1)],
                            start=(kc == 0), stop=(kc == 7)),
                            waits=[wq_tok[jj], xnT_all] + wfree2)
                    wq_free[bw] = tok_mm
                    wst = [va_store[bv]] if va_store[bv] else []
                    g0 = 8 * (jj - 4)
                    evac_tok[bp] = kb.emit("dve", lambda e, bp=bp, bv=bv, g0=g0, jj=jj: e.tensor_tensor(
                        vast[:, bv * H * VW:(bv + 1) * H * VW].rearrange(
                            "p (g c) -> p g c", c=VW)[:, g0:g0 + 8, 0:64],
                        pqs[bp][:, :].rearrange("p (g c) -> p g c", c=64),
                        vbias[:, 512 * (jj - 4):512 * (jj - 3)].rearrange(
                            "p (g c) -> p g c", c=64),
                        OP.add),
                        waits=[tok_mm, tok_vones, tok_vb] + wst)
                    ev_toks.append(evac_tok[bp])
                va_store[bv] = kb.dma("sp", f"vas{bv}", lambda e, it=it, bv=bv: e.dma_start(
                    out=va_d[it, :, :], in_=vast[:, bv * H * VW:(bv + 1) * H * VW]),
                    waits=ev_toks)
    kb.barrier()

    msb_stack = contextlib.ExitStack()
    msb = msb_stack.enter_context(nc.sbuf_tensor("msb", [128, 4 * E * C], BF16))
    m_tok = [None] * E
    with contextlib.ExitStack() as p2s:
        _e = p2s.enter_context
        wproj_sb = _e(nc.sbuf_tensor("wproj_sb", [128, E * C], BF16))
        bprojB = _e(nc.sbuf_tensor("bprojB", [128, C], F32))
        var = _e(nc.sbuf_tensor("var", [128, 17 * H * VW], BF16))
        ktr = _e(nc.sbuf_tensor("ktr", [128, 17 * 1024], BF16))
        qtr = _e(nc.sbuf_tensor("qtr", [128, 2 * 1024], BF16))
        mkld = _e(nc.sbuf_tensor("mkld", [128, 2 * NB * 128], BF16))
        psb = _e(nc.sbuf_tensor("psb", [128, 3 * NB * 128], BF16))
        attnA = _e(nc.sbuf_tensor("attnA", [128, 2 * 64], BF16))
        attnT = _e(nc.sbuf_tensor("attnT", [128, 2 * 1024], BF16))
        zr = _e(nc.sbuf_tensor("zr", [128, 2], F32))
        xre = _e(nc.sbuf_tensor("xre", [128, 2 * C], F32))
        projsb = _e(nc.sbuf_tensor("projsb", [128, C], F32))
        t1sb = _e(nc.sbuf_tensor("t1sb", [128, C], F32))
        x1sb = _e(nc.sbuf_tensor("x1sb", [128, 2 * C], F32))
        tok_wp = kb.dma("pool", "wpj", lambda e: e.dma_start(
            out=wproj_sb[:, :].rearrange("p (c n) -> p c n", n=C),
            in_=wproj_d.ap().rearrange("(c p) n -> p c n", p=128)))
        for e_i in range(4):
            m_tok[e_i] = kb.dma("sp", f"ml{e_i % 4}", lambda e, e_i=e_i: e.dma_start(
                out=msb[:, e_i * E * C:(e_i + 1) * E * C].rearrange(
                    "p (c n) -> p c n", n=C),
                in_=m_d[e_i, :, :].rearrange("(c p) n -> p c n", p=128)))
        va_tok = {}
        kt_tok = {}
        slot_user = {}
        ktslot_user = {}
        qt_tok = [None, None]
        qt_used = [None, None]
        sps0 = _e(nc.psum_tensor("sps0", [128, 1024], F32))
        sps1 = _e(nc.psum_tensor("sps1", [128, 1024], F32))
        pvp0 = _e(nc.psum_tensor("pvp0", [128, 512], F32))
        pvp1 = _e(nc.psum_tensor("pvp1", [128, 512], F32))
        trp2 = _e(nc.psum_tensor("trp2", [64, 1024], BF16))
        prj = _e(nc.psum_tensor("prj", [128, 512], F32))
        if True:
            tok_bpb = build_bias_row(bprojB, pvp0, pvp1, bp_row, tok_bpr)
            sps = [sps0, sps1]
            pvp = [pvp0, pvp1]
            exp_tok = [None, None]
            msk_sl = [None, None, None]
            dve_norm = [None, None]
            attnA_user = [None, None]
            norm_tok = [None, None]
            tr_ev = None
            attnT_done = [None, None]
            proj_evac_last = None
            projsb_free = None
            x1_store = [None, None]
            xre_used = [None, None]
            xre_tok = [None, None]
            mk_tok = [None, None]
            mk_used = [None, None]

            def emit_scores(i, h):
                bi, bh = i % 2, h % 2
                vi = 16 + i
                bs = [vi - d for d in (16, 8, 4, 2, 1, 0)]
                pair, pb = h >> 1, 64 * (h & 1)
                wsc = ([exp_tok[bh]] if exp_tok[bh] else []) + [qt_tok[bi]]
                tok_sc0 = None
                for idx, b in enumerate(bs):
                    slot = b % 17
                    tok_sc0 = kb.emit("pe", lambda e, bh=bh, idx=idx, slot=slot, pair=pair, pb=pb, bi=bi: e.matmul(
                        sps[bh][:, 128 * idx:128 * (idx + 1)],
                        ktr[pb:pb + 64, slot * 1024 + 128 * pair: slot * 1024 + 128 * (pair + 1)],
                        qtr[pb:pb + 64, bi * 1024 + 128 * pair: bi * 1024 + 128 * (pair + 1)],
                        start=(idx in (0, 4)), stop=(idx in (3, 5))),
                        waits=wsc + [kt_tok[b]])
                    ktslot_user[slot] = tok_sc0
                qt_used[bi] = tok_sc0
                sl = h % 3
                exp_tok[bh] = kb.emit("act", lambda e, bh=bh, sl=sl: e.activation(
                    psb[:, sl * 768: sl * 768 + 768], sps[bh][:, 0:768],
                    AF.Exp, bias=0.0, scale=SCALE),
                    waits=[tok_sc0, msk_sl[sl]])

            def emit_mask(i, h):
                bi, bh = i % 2, h % 2
                sl = h % 3
                msk_sl[sl] = kb.emit("dve", lambda e, sl=sl, bi=bi: e.tensor_mul(
                    psb[:, sl * 768: sl * 768 + 768],
                    psb[:, sl * 768: sl * 768 + 768],
                    mkld[:, bi * 768: bi * 768 + 768]),
                    waits=[exp_tok[bh], mk_tok[bi]])
                mk_used[bi] = msk_sl[sl]

            def emit_pv(i, h):
                bh = h % 2
                vi = 16 + i
                bs = [vi - d for d in (16, 8, 4, 2, 1, 0)]
                wpv = ([dve_norm[bh]] if dve_norm[bh] else []) + [msk_sl[h % 3], tok_bpb] + \
                    [va_tok[b] for b in bs]
                tok_pv = None
                sl = h % 3
                for idx, b in enumerate(bs):
                    slot = b % 17
                    tok_pv = kb.emit("pe", lambda e, bh=bh, sl=sl, idx=idx, slot=slot, h=h: e.matmul(
                        pvp[bh][:, 0:VW],
                        psb[:, sl * 768 + 128 * idx: sl * 768 + 128 * (idx + 1)],
                        var[:, slot * H * VW + VW * h: slot * H * VW + VW * (h + 1)],
                        start=(idx == 0), stop=(idx == 5)),
                        waits=wpv)
                    slot_user[slot] = tok_pv
                kb.emit("dve", lambda e, bh=bh: e.reciprocal(
                    zr[:, bh:bh + 1], pvp[bh][:, 64:65]), waits=[tok_pv])
                wn = [attnA_user[bh]] if attnA_user[bh] else []
                norm_tok[bh] = kb.emit("dve", lambda e, bh=bh: e.tensor_scalar(
                    attnA[:, bh * 64:(bh + 1) * 64], pvp[bh][:, 0:64],
                    zr[:, bh:bh + 1], None, OP.mult), waits=wn)
                dve_norm[bh] = norm_tok[bh]

            def emit_tr(i, h):
                nonlocal tr_ev
                bi, bh = i % 2, h % 2
                pair, pb = h >> 1, 64 * (h & 1)
                wtr = [tr_ev] if tr_ev else []
                tok_tr2 = kb.emit("pe", lambda e, bh=bh: e.transpose(
                    trp2[:, 0:128], attnA[:, bh * 64:(bh + 1) * 64], idb[:, :]),
                    waits=[norm_tok[bh]] + wtr)
                attnA_user[bh] = tok_tr2
                tr_ev = kb.emit("act", lambda e, bi=bi, pb=pb, pair=pair: e.copy(
                    attnT[pb:pb + 64, bi * 1024 + 128 * pair: bi * 1024 + 128 * (pair + 1)],
                    trp2[0:64, 0:128]), waits=[tok_tr2])
                attnT_done[bi] = tr_ev

            def emit_proj_half(ip, n):
                nonlocal proj_evac_last
                bip = ip % 2
                tok_pj = None
                for kc in range(8):
                    tok_pj = kb.emit("pe", lambda e, bip=bip, kc=kc, n=n: e.matmul(
                        prj[:, :],
                        attnT[:, bip * 1024 + 128 * kc: bip * 1024 + 128 * (kc + 1)],
                        wproj_sb[:, 1024 * kc + 512 * n: 1024 * kc + 512 * (n + 1)],
                        start=(kc == 0), stop=(kc == 7)),
                        waits=[attnT_done[bip], tok_wp] +
                              ([proj_evac_last] if proj_evac_last else []))
                wev = [projsb_free] if (n == 0 and projsb_free) else []
                proj_evac_last = kb.emit("act", lambda e, n=n: e.copy(
                    projsb[:, 512 * n:512 * (n + 1)], prj[:, :]),
                    waits=[tok_pj] + wev)

            def emit_x1(ip):
                nonlocal projsb_free
                bip = ip % 2
                tok_t1 = kb.emit("pool", lambda e: e.tensor_add(
                    t1sb[:, :], projsb[:, :], bprojB[:, :]),
                    waits=[proj_evac_last, tok_bpb])
                projsb_free = tok_t1
                prev_x1 = [x1_store[bip]] if x1_store[bip] else []
                xre_used[bip] = kb.emit("pool", lambda e, bip=bip: e.tensor_add(
                    x1sb[:, bip * C:(bip + 1) * C], t1sb[:, :],
                    xre[:, bip * C:(bip + 1) * C]),
                    waits=[tok_t1, xre_tok[bip]] + prev_x1)
                x1_store[bip] = kb.dma("sp", f"x1s{bip}", lambda e, ip=ip, bip=bip: e.dma_start(
                    out=x1_d[128 * ip:128 * (ip + 1), :],
                    in_=x1sb[:, bip * C:(bip + 1) * C]), waits=[xre_used[bip]])

            for i in range(OWN):
                vi = 16 + i
                bs = [vi - d for d in (16, 8, 4, 2, 1, 0)]
                bi = i % 2
                for b in bs:
                    if b not in kt_tok:
                        slot = b % 17
                        wfree = [ktslot_user[slot]] if slot in ktslot_user else []
                        kt_tok[b] = kb.dma("sp", f"ktl{slot % 4}", lambda e, b=b, slot=slot: e.dma_start(
                            out=ktr[:, slot * 1024:(slot + 1) * 1024].rearrange(
                                "p (a n) -> p a n", n=128),
                            in_=kt_d[:, :, 128 * b:128 * (b + 1)].rearrange(
                                "a p n -> p a n")), waits=wfree)
                wqt = [qt_used[bi]] if qt_used[bi] else []
                qt_tok[bi] = kb.dma("sp", f"qt{bi}", lambda e, i=i, bi=bi: e.dma_start(
                    out=qtr[:, bi * 1024:(bi + 1) * 1024].rearrange("p (a n) -> p a n", n=128),
                    in_=q_d[:, :, 128 * i:128 * (i + 1)].rearrange("a p n -> p a n")),
                    waits=wqt)
                wmk = [mk_used[bi]] if mk_used[bi] else []
                mk_tok[bi] = kb.dma("sp", f"mk{bi}", lambda e, i=i, bi=bi: e.dma_start(
                    out=mkld[:, bi * NB * 128:(bi + 1) * NB * 128], in_=mk_d[i, :, :]),
                    waits=wmk)
                for b in bs:
                    if b not in va_tok:
                        slot = b % 17
                        wfree = [slot_user[slot]] if slot in slot_user else []
                        va_tok[b] = kb.dma("sp", f"val{slot % 4}", lambda e, b=b, slot=slot: e.dma_start(
                            out=var[:, slot * H * VW:(slot + 1) * H * VW],
                            in_=va_d[b, :, :]), waits=wfree)
                wxf = [xre_used[bi]] if xre_used[bi] else []
                xre_tok[bi] = kb.dma("sp", f"xre{bi}", lambda e, i=i, bi=bi: e.dma_start(
                    out=xre[:, bi * C:(bi + 1) * C],
                    in_=x_d[T0 + 128 * i: T0 + 128 * (i + 1), :]), waits=wxf)
                for step in range(H + 3):
                    if step == 0 and i > 0:
                        emit_proj_half(i - 1, 0)
                    if step == 8 and i > 0:
                        emit_proj_half(i - 1, 1)
                        emit_x1(i - 1)
                    if step < H:
                        emit_scores(i, step)
                    if 2 <= step <= H + 1:
                        emit_pv(i, step - 2)
                    if step < H:
                        emit_mask(i, step)
                    if step >= 3:
                        emit_tr(i, step - 3)
            emit_proj_half(OWN - 1, 0)
            emit_proj_half(OWN - 1, 1)
            emit_x1(OWN - 1)
    kb.barrier()

    with nc.sbuf_tensor("xn1T", [128, OWN * C], BF16) as xn1T, \
         nc.sbuf_tensor("yacc", [128, OWN * C], F32) as yacc, \
         nc.sbuf_tensor("x1ld", [128, 2 * C], F32) as x1ld, \
         nc.sbuf_tensor("xc2", [128, 2 * C], F32) as xc2, \
         nc.sbuf_tensor("xn1bf", [128, 4 * C], BF16) as xn1bf, \
         nc.sbuf_tensor("st2", [128, 16], F32) as st2, \
         nc.sbuf_tensor("wall", [128, OWN * E], F32) as wall, \
         nc.sbuf_tensor("el", [128, 24], F32) as el, \
         nc.sbuf_tensor("outsb", [128, 2 * C], F32) as outsb:
        x1_free = [None, None]
        with nc.psum_tensor("py0", [128, 1024], F32) as py0, \
             nc.psum_tensor("py1", [128, 1024], F32) as py1:
            ev_tok = [[None, None], [None, None]]
            group_last_mm = [None] * 4
            pys = [(py0, py1), None]

            def emit_y_tile(g, i, bg):
                wmm = [t for t in ev_tok[bg] if t]
                tok_y = [None, None]
                for kc in range(8):
                    for e_loc in range(2):
                        e_g = 2 * g + e_loc
                        ms = (e_g % 4) * E * C
                        for n in range(2):
                            tok_y[e_loc] = kb.emit("pe", lambda e, bg=bg, e_loc=e_loc, i=i, kc=kc, n=n, ms=ms: e.matmul(
                                pys[bg][e_loc][:, 512 * n:512 * (n + 1)],
                                xn1T[:, 1024 * i + 128 * kc: 1024 * i + 128 * (kc + 1)],
                                msb[:, ms + 1024 * kc + 512 * n: ms + 1024 * kc + 512 * (n + 1)],
                                start=(kc == 0), stop=(kc == 7)),
                                waits=[m_tok[2 * g + e_loc]] + (wmm if kc == 0 else []))
                for e_loc in range(2):
                    e_g = 2 * g + e_loc
                    if e_g == 0:
                        ev = kb.emit("dve", lambda e, bg=bg, e_loc=e_loc, i=i, e_g=e_g: e.tensor_scalar(
                            yacc[:, C * i:C * (i + 1)], pys[bg][e_loc][:, :],
                            wall[:, E * i + e_g:E * i + e_g + 1], None, OP.mult),
                            waits=[tok_y[e_loc]])
                    else:
                        ev = kb.emit("dve", lambda e, bg=bg, e_loc=e_loc, i=i, e_g=e_g: e.scalar_tensor_tensor(
                            yacc[:, C * i:C * (i + 1)], pys[bg][e_loc][:, :],
                            wall[:, E * i + e_g:E * i + e_g + 1],
                            yacc[:, C * i:C * (i + 1)], OP.mult, OP.add),
                            waits=[tok_y[e_loc]])
                    ev_tok[bg][e_loc] = ev
                return tok_y[1]

            with nc.psum_tensor("ptr3a", [128, 1024], BF16) as ptr3a, \
                 nc.psum_tensor("ptr3b", [128, 1024], BF16) as ptr3b, \
                 nc.psum_tensor("prt", [128, 512], F32) as prt:
                ptrs3 = [ptr3a, ptr3b]
                tr_ev3 = [None, None]
                last_norm2 = [None]
                norm3_of = {}
                rt_done = None
                prev_rt_w = None
                prev_rstd2 = [None, None]
                prev_last_tr3 = [None, None, None, None]
                def p3_lad(i):
                    bi = i % 2
                    wfree = [x1_free[bi]] if x1_free[bi] else []
                    tok_x1 = kb.dma("sp", f"x1l{bi}", lambda e, i=i, bi=bi: e.dma_start(
                        out=x1ld[:, bi * C:(bi + 1) * C],
                        in_=x1_d[128 * i:128 * (i + 1), :]), waits=wfree)
                    xv = x1ld[:, bi * C:(bi + 1) * C]
                    xcv = xc2[:, bi * C:(bi + 1) * C]
                    s0, s1, s2, s3, s4 = (8 * bi + k for k in range(5))
                    tok_sum = kb.emit("act", lambda e, xv=xv, xcv=xcv, s0=s0: e.activation(
                        xcv, xv, AF.Copy, bias=0.0, scale=1.0,
                        accum_out=st2[:, s0:s0 + 1]),
                        waits=[tok_x1] +
                              ([prev_rstd2[bi]] if prev_rstd2[bi] else []) +
                              ([norm3_of[i - 2]] if i >= 2 else []))
                    kb.emit("dve", lambda e, s0=s0, s1=s1: e.tensor_scalar(
                        st2[:, s1:s1 + 1], st2[:, s0:s0 + 1], -1.0 / C, None, OP.mult),
                        waits=[tok_sum])
                    kb.emit("dve", lambda e, xv=xv, xcv=xcv, s1=s1: e.tensor_scalar(
                        xcv, xv, st2[:, s1:s1 + 1], None, OP.add))
                    tok_var = kb.emit("dve", lambda e, xv=xv, xcv=xcv, s2=s2: e.scalar_tensor_tensor(
                        xv, xcv, 1.0, xv, OP.mult, OP.mult,
                        accum_out=st2[:, s2:s2 + 1]))
                    x1_free[bi] = tok_var
                    kb.emit("act", lambda e, s2=s2, s3=s3: e.activation(
                        st2[:, s3:s3 + 1], st2[:, s2:s2 + 1], AF.Ln, bias=epsb[:, 0:1],
                        scale=1.0 / C), waits=[tok_var, tok_eps])
                    prev_rstd2[bi] = kb.emit("act", lambda e, s3=s3, s4=s4: e.activation(
                        st2[:, s4:s4 + 1], st2[:, s3:s3 + 1], AF.Exp, bias=0.0,
                        scale=-0.5))
                    bn = i % 4
                    tok_xn1 = kb.emit("dve", lambda e, bn=bn, xcv=xcv, s4=s4: e.tensor_scalar(
                        xn1bf[:, bn * C:(bn + 1) * C], xcv, st2[:, s4:s4 + 1],
                        None, OP.mult),
                        waits=[prev_rstd2[bi], tok_var] +
                              ([prev_last_tr3[bn]] if prev_last_tr3[bn] else []))
                    norm3_of[i] = tok_xn1
                def p3_work_a(i):
                    bi = i % 2
                    bn = i % 4
                    bt = i % 2
                    wfree2 = [tr_ev3[bt]] if tr_ev3[bt] else []
                    tok_tr = None
                    for c in range(8):
                        tok_tr = kb.emit("pe", lambda e, c=c, bt=bt, bn=bn: e.transpose(
                            ptrs3[bt][:, 128 * c:128 * (c + 1)],
                            xn1bf[:, bn * C + 128 * c: bn * C + 128 * (c + 1)], idb[:, :]),
                            waits=([norm3_of[i], tok_idb] + wfree2) if c == 0 else ())
                    prev_last_tr3[bn] = tok_tr
                    tr_ev3[bt] = kb.emit("act", lambda e, i=i, bt=bt: e.copy(
                        xn1T[:, 1024 * i: 1024 * (i + 1)], ptrs3[bt][:, :]),
                        waits=[tok_tr])
                    tr_done[i] = tr_ev3[bt]

                def p3_work_b(i):
                    nonlocal rt_done, prev_rt_w
                    bi = i % 2
                    tr_last = tr_done[i]
                    wrt = [rt_done] if rt_done else []
                    tok_rt = None
                    for kc in range(8):
                        tok_rt = kb.emit("pe", lambda e, i=i, kc=kc: e.matmul(
                            prt[:, 0:E],
                            xn1T[:, 1024 * i + 128 * kc: 1024 * i + 128 * (kc + 1)],
                            wr_sb[:, E * kc: E * (kc + 1)],
                            start=(kc == 0), stop=(kc == 7)),
                            waits=[tr_last, tok_wr] + wrt)
                    tok_el = kb.emit("act", lambda e: e.activation(
                        el[:, 0:8], prt[:, 0:E], AF.Exp, bias=0.0, scale=1.0),
                        waits=[tok_rt] + ([prev_rt_w] if prev_rt_w else []))
                    rt_done = tok_el
                    kb.emit("dve", lambda e: e.reduce_sum(st2[:, 5:6], el[:, 0:8], AX),
                            waits=[tok_el])
                    kb.emit("dve", lambda e: e.reduce_max(st2[:, 6:7], el[:, 0:8], AX))
                    kb.emit("dve", lambda e: e.tensor_scalar(
                        el[:, 8:16], el[:, 0:8], st2[:, 6:7], None, OP.is_equal))
                    kb.emit("dve", lambda e: e.tensor_scalar(
                        el[:, 8:16], el[:, 8:16], -1.0, 1.0, OP.mult, OP.add))
                    kb.emit("dve", lambda e: e.tensor_mul(el[:, 16:24], el[:, 0:8], el[:, 8:16]))
                    kb.emit("dve", lambda e: e.reduce_max(st2[:, 7:8], el[:, 16:24], AX))
                    kb.emit("dve", lambda e: e.tensor_scalar(
                        el[:, 8:16], el[:, 0:8], st2[:, 7:8], None, OP.is_ge))
                    kb.emit("dve", lambda e: e.tensor_mul(el[:, 16:24], el[:, 0:8], el[:, 8:16]))
                    kb.emit("dve", lambda e: e.reciprocal(st2[:, 5:6], st2[:, 5:6]))
                    prev_rt_w = kb.emit("dve", lambda e, i=i: e.tensor_scalar(
                        wall[:, E * i:E * (i + 1)], el[:, 16:24], st2[:, 5:6], None, OP.mult))

                    group_last_mm[0] = emit_y_tile(0, i, 0)

                tr_done = {}
                for i in range(OWN + 3):
                    if i < OWN:
                        p3_lad(i)
                    if 2 <= i < OWN + 2:
                        p3_work_a(i - 2)
                    if i >= 3:
                        p3_work_b(i - 3)

            kb.wait("pe", "act", kb.cnt["act"])
            kb.wait("pe", "dve", kb.cnt["dve"])
            with nc.psum_tensor("py2", [128, 1024], F32) as py2, \
                 nc.psum_tensor("py3", [128, 1024], F32) as py3:
                pys[1] = (py2, py3)
                for g in range(1, 4):
                    for e_loc in range(2):
                        e_g = 2 * g + e_loc
                        if m_tok[e_g] is None:
                            m_tok[e_g] = kb.dma("sp", f"ml{e_g % 4}", lambda e, e_g=e_g: e.dma_start(
                                out=msb[:, (e_g % 4) * E * C:((e_g % 4) + 1) * E * C].rearrange(
                                    "p (c n) -> p c n", n=C),
                                in_=m_d[e_g, :, :].rearrange("(c p) n -> p c n", p=128)),
                                waits=[group_last_mm[g - 2]])
                    out_store = [None, None]
                    x1_free2 = list(x1_free)

                    def emit_out_tile(i, last_ev):
                        bi = i % 2
                        wfree4 = [x1_free2[bi]] if x1_free2[bi] else []
                        tok_x1 = kb.dma("sp", f"x1l{bi}", lambda e, i=i, bi=bi: e.dma_start(
                            out=x1ld[:, bi * C:(bi + 1) * C],
                            in_=x1_d[128 * i:128 * (i + 1), :]), waits=wfree4)
                        wos = [out_store[bi]] if out_store[bi] else []
                        x1_free2[bi] = kb.emit("dve", lambda e, i=i, bi=bi: e.tensor_add(
                            outsb[:, bi * C:(bi + 1) * C], yacc[:, C * i:C * (i + 1)],
                            x1ld[:, bi * C:(bi + 1) * C]),
                            waits=[tok_x1, last_ev] + wos)
                        out_store[bi] = kb.dma("sp", f"out{bi}", lambda e, i=i, bi=bi: e.dma_start(
                            out=out_d[128 * i:128 * (i + 1), :],
                            in_=outsb[:, bi * C:(bi + 1) * C]), waits=[x1_free2[bi]])

                    for i in range(OWN):
                        bg = (g * OWN + i) % 2
                        group_last_mm[g] = emit_y_tile(g, i, bg)
                        if g == 3:
                            emit_out_tile(i, ev_tok[bg][1])
        kb.wait("sp", "out0", kb.cnt["out0"])
        kb.wait("sp", "out1", kb.cnt["out1"])

    msb_stack.close()
    kb.replay()
    kb.stack.close()
    return nc


_CACHE = {}


def kernel(L0_attn_x, L0_attn_W_qkv, L0_attn_b_qkv, L0_attn_W_proj, L0_attn_b_proj,
           L0_ffn_Wr, L0_ffn_Wi, L0_ffn_Wo):
    x = np.ascontiguousarray(np.asarray(L0_attn_x, dtype=np.float32))
    wqkv = np.ascontiguousarray(np.asarray(L0_attn_W_qkv, dtype=np.float32))
    bqkv = np.ascontiguousarray(np.asarray(L0_attn_b_qkv, dtype=np.float32))
    wproj = np.ascontiguousarray(np.asarray(L0_attn_W_proj, dtype=np.float32))
    bproj = np.ascontiguousarray(np.asarray(L0_attn_b_proj, dtype=np.float32))
    wr = np.ascontiguousarray(np.asarray(L0_ffn_Wr, dtype=np.float32))
    wi = np.ascontiguousarray(np.swapaxes(np.asarray(L0_ffn_Wi, dtype=np.float32), 1, 2))
    wo = np.ascontiguousarray(np.asarray(L0_ffn_Wo, dtype=np.float32))
    idn = np.eye(128, dtype=np.float32)
    masks = [build_masks(0), build_masks(1)]

    if "nc" not in _CACHE:
        _CACHE["nc"] = build_program()
    nc = _CACHE["nc"]

    in_maps = []
    for core in range(8):
        b, half = core >> 1, core & 1
        if half == 0:
            xv = np.concatenate(
                [np.zeros((OWN_TOK, C), np.float32), x[b, 0:OWN_TOK]], axis=0)
        else:
            xv = np.ascontiguousarray(x[b])
        in_maps.append({
            "x": xv, "wqkv": wqkv, "bqkv": bqkv, "wproj": wproj, "bproj": bproj,
            "wr": wr, "wi": wi, "wo": wo, "idn": idn, "mk": masks[half],
        })
    res = run_bass_kernel_spmd(nc, in_maps, core_ids=list(range(8)))
    out = np.empty((B, T, C), dtype=np.float32)
    for core in range(8):
        b, half = core >> 1, core & 1
        out[b, half * OWN_TOK:(half + 1) * OWN_TOK] = res.results[core]["out"]
    return out


# version 93
# speedup vs baseline: 1.1821x; 1.0176x over previous
import contextlib
import numpy as np
import ml_dtypes
import concourse.bass as bass
import concourse.mybir as mybir
from concourse.bass_utils import run_bass_kernel_spmd

F32 = mybir.dt.float32
BF16 = mybir.dt.bfloat16
AF = mybir.ActivationFunctionType
OP = mybir.AluOpType
AX = mybir.AxisListType.X

B, T, C = 4, 4096, 1024
H = 16
NT = 32
OWN = 16
OWN_TOK = OWN * 128
E = 8
SCALE = 0.125
EPS = 1e-5
VW = 65
NB = 6
ENGINES = ("pe", "act", "dve", "pool", "sp")


def build_masks(half):
    s = np.arange(128)[:, None]
    t = np.arange(128)[None, :]
    m0 = np.isin(t - s, [0, 1, 2, 4, 8, 16, 32, 64]).astype(np.float32)
    m1 = np.isin(t - s + 128, [1, 2, 4, 8, 16, 32, 64, 128]).astype(np.float32)
    mD = np.eye(128, dtype=np.float32)
    out = np.zeros((OWN, 128, NB * 128), dtype=np.float32)
    for i in range(OWN):
        vi = 16 + i
        for idx, d in enumerate((16, 8, 4, 2, 1, 0)):
            vb = vi - d
            real = vb - 16 + 16 * half
            if real >= 0:
                kind = m0 if d == 0 else (m1 if d == 1 else mD)
                out[i, :, 128 * idx:128 * (idx + 1)] = kind
    return out.astype(ml_dtypes.bfloat16)


class KB:

    def __init__(self, nc):
        self.nc = nc
        self.items = {e: [] for e in ENGINES}
        self.stack = contextlib.ExitStack()
        self.sems = {}
        self.cnt = {}
        self.hwm = {e: {} for e in ENGINES}
        for s in ("pe", "act", "dve", "pool"):
            self.sems[s] = self.stack.enter_context(nc.semaphore(s))
            self.cnt[s] = 0

    def wait(self, eng, sem, val):
        if val <= 0 or self.hwm[eng].get(sem, 0) >= val:
            return
        self.hwm[eng][sem] = val
        s = self.sems[sem]
        self.items[eng].append(lambda EE, s=s, v=val: EE.wait_ge(s, v))

    def emit(self, eng, fn, waits=()):
        if eng in ("dve", "act", "pool"):
            self.wait(eng, eng, self.cnt[eng])
        for w in waits:
            if w:
                self.wait(eng, w[0], w[1])
        self.cnt[eng] += 1
        s = self.sems[eng]
        self.items[eng].append(lambda EE, fn=fn, s=s: fn(EE).then_inc(s, 1))
        return (eng, self.cnt[eng])

    def dma(self, eng, stream, fn, waits=()):
        if stream not in self.sems:
            self.sems[stream] = self.stack.enter_context(self.nc.semaphore(stream))
            self.cnt[stream] = 0
        self.wait(eng, stream, self.cnt[stream])
        for w in waits:
            if w:
                self.wait(eng, w[0], w[1])
        self.cnt[stream] += 16
        s = self.sems[stream]
        self.items[eng].append(lambda EE, fn=fn, s=s: fn(EE).then_inc(s, 16))
        return (stream, self.cnt[stream])

    def barrier(self):
        toks = [(s, self.cnt[s]) for s in self.cnt]
        for e in ENGINES:
            for sem, val in toks:
                self.wait(e, sem, val)

    def replay(self):
        with self.nc.Block() as block:
            @block.tensor
            def _(t):
                for f in self.items["pe"]:
                    f(t)

            @block.scalar
            def _(s):
                for f in self.items["act"]:
                    f(s)

            @block.vector
            def _(v):
                for f in self.items["dve"]:
                    f(v)

            @block.gpsimd
            def _(g):
                for f in self.items["pool"]:
                    f(g)

            @block.sync
            def _(sy):
                for f in self.items["sp"]:
                    f(sy)


def build_program():
    nc = bass.Bass("TRN2", target_bir_lowering=False)
    dt = nc.dram_tensor
    x_d = dt("x", [T, C], F32, kind="ExternalInput")
    wqkv_d = dt("wqkv", [C, 3 * C], F32, kind="ExternalInput")
    bqkv_d = dt("bqkv", [3 * C], F32, kind="ExternalInput")
    wproj_d = dt("wproj", [C, C], F32, kind="ExternalInput")
    bproj_d = dt("bproj", [C], F32, kind="ExternalInput")
    wr_d = dt("wr", [C, E], F32, kind="ExternalInput")
    wi_d = dt("wi", [E, C, C], F32, kind="ExternalInput")
    wo_d = dt("wo", [E, C, C], F32, kind="ExternalInput")
    idn_d = dt("idn", [128, 128], F32, kind="ExternalInput")
    mk_d = dt("mk", [OWN, 128, NB * 128], BF16, kind="ExternalInput")
    out_d = dt("out", [OWN_TOK, C], F32, kind="ExternalOutput")
    m_d = dt("m_sc", [E, C, C], BF16)
    va_d = dt("va_sc", [NT, 128, H * VW], BF16)
    kt_d = dt("kt_sc", [8, 128, T], BF16)
    q_d = dt("q_sc", [8, 128, OWN_TOK], BF16)
    x1_d = dt("x1_sc", [OWN_TOK, C], F32)

    kb = KB(nc)
    st = kb.stack
    T0 = OWN_TOK

    idb = st.enter_context(nc.sbuf_tensor("idb", [128, 128], BF16))
    bqkT = st.enter_context(nc.sbuf_tensor("bqkT", [128, 16], F32))
    wr_sb = st.enter_context(nc.sbuf_tensor("wr_sb", [128, E * E], BF16))
    epsb = st.enter_context(nc.sbuf_tensor("epsb", [128, 1], F32))
    ones_row = st.enter_context(nc.sbuf_tensor("ones_row", [1, 128], F32))
    bp_row = st.enter_context(nc.sbuf_tensor("bp_row", [1, C], F32))

    tok_onr = kb.emit("dve", lambda e: e.memset(ones_row[:, :], 1.0))
    tok_bpr = kb.dma("sp", "ini", lambda e: e.dma_start(
        out=bp_row[0:1, :], in_=bproj_d.ap().rearrange("(a n) -> a n", a=1)))

    def build_bias_row(dst, ps_a, ps_b, row, t_r):
        one, t_o = ones_row, tok_onr
        toks = []
        for n in range(2):
            ps = ps_a if n == 0 else ps_b
            t_mm = kb.emit("pe", lambda e, ps=ps, n=n: e.matmul(
                ps[:, :], one[:, :], row[0:1, 512 * n: 512 * (n + 1)],
                start=True, stop=True),
                waits=[t_r, t_o])
            toks.append(kb.emit("dve", lambda e, ps=ps, n=n: e.tensor_copy(
                dst[:, 512 * n: 512 * (n + 1)], ps[:, :]), waits=[t_mm]))
        return toks[-1]

    tok_eps = kb.emit("dve", lambda e: e.memset(epsb[:, :], EPS))
    tok_idb = kb.dma("pool", "pini", lambda e: e.dma_start(out=idb[:, :], in_=idn_d[:, :]))
    tok_wr = kb.dma("pool", "pini", lambda e: e.dma_start(
        out=wr_sb[:, :].rearrange("p (c n) -> p c n", n=E),
        in_=wr_d.ap().rearrange("(c p) n -> p c n", p=128)))
    with nc.sbuf_tensor("idf", [16, 16], F32) as idf, \
         nc.sbuf_tensor("rowbuf", [16, 128], F32) as rowbuf:
        tok_id = kb.dma("sp", "ini", lambda e: e.dma_start(
            out=idf[:, :], in_=idn_d[0:16, 0:16]))
        with nc.psum_tensor("pbias", [128, 512], F32) as pbias:
            tok_rb = kb.dma("sp", "ini", lambda e: e.dma_start(
                out=rowbuf[:, :],
                in_=bqkv_d.ap().rearrange("(c p) -> c p", p=128)[0:16, :]))
            tok_tr = kb.emit("pe", lambda e: e.transpose(
                pbias[:, 0:16], rowbuf[:, :], idf[:, :]),
                waits=[tok_rb, tok_id])
            tok_bq = kb.emit("dve", lambda e: e.tensor_copy(bqkT[:, :], pbias[:, 0:16]),
                             waits=[tok_tr])
    kb.barrier()

    with nc.sbuf_tensor("xnT", [128, 8 * T], BF16) as xnT, \
         nc.sbuf_tensor("xld", [128, 2 * C], F32) as xld, \
         nc.sbuf_tensor("xc", [128, 2 * C], F32) as xc, \
         nc.sbuf_tensor("stats", [128, 16], F32) as stats, \
         nc.sbuf_tensor("xnbf", [128, 6 * C], BF16) as xnbf:
        with nc.sbuf_tensor("wiT", [128, 2 * E * C], BF16) as wiT, \
             nc.sbuf_tensor("wof", [128, 2 * 4 * C], F32) as wof, \
             nc.sbuf_tensor("wo_sb", [128, 2 * E * C], BF16) as wo_sb, \
             nc.sbuf_tensor("m_st", [128, 3 * C], BF16) as m_st, \
             nc.psum_tensor("pm0", [128, 512], F32) as pm0, \
             nc.psum_tensor("pm1", [128, 512], F32) as pm1, \
             nc.psum_tensor("pm2", [128, 512], F32) as pm2, \
             nc.psum_tensor("pm3", [128, 512], F32) as pm3, \
             nc.psum_tensor("pm4", [128, 512], F32) as pm4, \
             nc.psum_tensor("pm5", [128, 512], F32) as pm5, \
             nc.psum_tensor("ptr0", [128, 1024], BF16) as ptr0, \
             nc.psum_tensor("ptr1", [128, 1024], BF16) as ptr1:
            pms = [(pm0, pm1), (pm2, pm3), (pm4, pm5)]
            ptrs = [ptr0, ptr1]
            wi_tok = [None] * E
            wi_user = [None, None]
            wo_tok_l = [None] * E
            wof_cast = [None, None]
            last_mm_buf = [None, None]
            mst_free = [None, None, None]
            pm_free = [None, None, None]
            witr_ev = [None, None]
            wisb_user = [None, None]
            wo_cast = [None]

            def emit_wi_load(e_n):
                nbuf = e_n % 2
                wfree = [wi_user[nbuf]] if wi_user[nbuf] else []
                toks = []
                for hf in range(4):
                    toks.append(kb.dma("pool", f"wi{hf}", lambda e, e_n=e_n, nbuf=nbuf, hf=hf: e.dma_start(
                        out=wiT[:, nbuf * E * C + 2048 * hf: nbuf * E * C + 2048 * (hf + 1)].rearrange(
                            "p (c n) -> p c n", n=C),
                        in_=wi_d[e_n, 256 * hf: 256 * (hf + 1), :].rearrange(
                            "(c p) n -> p c n", p=128)),
                        waits=wfree))
                wi_tok[e_n] = toks

            def emit_wo_load(e_n):
                nbuf = e_n % 2
                wW2 = [last_mm_buf[nbuf]] if last_mm_buf[nbuf] else []
                tok_c = None
                for hf in range(2):
                    wfree3 = [wof_cast[hf]] if wof_cast[hf] else []
                    t_l = kb.dma("sp", f"wof{hf}", lambda e, e_n=e_n, hf=hf: e.dma_start(
                        out=wof[:, hf * 4 * C:(hf + 1) * 4 * C].rearrange(
                            "p (c n) -> p c n", n=C),
                        in_=wo_d[e_n, 512 * hf:512 * (hf + 1), :].rearrange(
                            "(c p) n -> p c n", p=128)),
                        waits=wfree3)
                    tok_c = kb.emit("act", lambda e, nbuf=nbuf, hf=hf: e.copy(
                        wo_sb[:, nbuf * E * C + 4096 * hf: nbuf * E * C + 4096 * (hf + 1)],
                        wof[:, hf * 4 * C:(hf + 1) * 4 * C]), waits=[t_l] + wW2)
                    wof_cast[hf] = tok_c
                wo_tok_l[e_n] = tok_c

            def emit_m_expert(e_i, fillers=()):
                buf = e_i % 2
                off = buf * E * C
                offw = (e_i % 2) * E * C
                fillers = list(fillers)
                if e_i + 1 < E and wo_tok_l[e_i + 1] is None:
                    emit_wo_load(e_i + 1)
                wW = [last_mm_buf[buf]] if last_mm_buf[buf] else []
                wo_tok = wo_tok_l[e_i]
                if e_i + 1 < E:
                    emit_wi_load(e_i + 1)
                wiT_tok = wi_tok[e_i]
                for dc in range(8):
                    pi = (e_i * 8 + dc) % 3
                    pp = pms[pi]
                    wfree = [pm_free[pi]] if pm_free[pi] else []
                    evac_toks = []
                    tok = None
                    for n in range(2):
                        for hc in range(8):
                            tok = kb.emit("pe", lambda e, pp=pp, n=n, hc=hc, dc=dc, off=off, offw=offw: e.matmul(
                                pp[n][:, :],
                                wiT[:, offw + 1024 * hc + 128 * dc: offw + 1024 * hc + 128 * (dc + 1)],
                                wo_sb[:, off + 1024 * hc + 512 * n: off + 1024 * hc + 512 * (n + 1)],
                                start=(hc == 0), stop=(hc == 7)),
                                waits=[wo_tok] + wiT_tok + wfree)
                        mb = pi
                        wfree2 = [mst_free[mb]] if mst_free[mb] else []
                        evac_toks.append(kb.emit("dve", lambda e, pp=pp, n=n, mb=mb: e.tensor_copy(
                            m_st[:, mb * C + 512 * n: mb * C + 512 * (n + 1)], pp[n][:, :]),
                            waits=[tok] + wfree2))
                    last_mm_buf[buf] = tok
                    wi_user[e_i % 2] = tok
                    pm_free[pi] = evac_toks[-1]
                    mb = pi
                    mst_free[mb] = kb.dma("sp", f"mst{mb}", lambda e, e_i=e_i, dc=dc, mb=mb: e.dma_start(
                        out=m_d[e_i, 128 * dc: 128 * (dc + 1), :],
                        in_=m_st[:, mb * C: (mb + 1) * C]), waits=evac_toks)
                    if dc % 2 == 1 and fillers:
                        fillers.pop(0)()

            xld_free = [None, None]
            norm_of = {}
            tr_evac = [None, None]
            ln_rstd = [None, None]
            prev_tr = [None] * 6
            xnT_last = [None]

            def emit_ln_ladder(it):
                bufx = it % 2
                bufn = it % 6
                wfree = [xld_free[bufx]] if xld_free[bufx] else []
                tok_x = kb.dma("sp", f"x{bufx}", lambda e, it=it, bufx=bufx: e.dma_start(
                    out=xld[:, bufx * C:(bufx + 1) * C],
                    in_=x_d[128 * it:128 * (it + 1), :]), waits=wfree)
                xv = xld[:, bufx * C:(bufx + 1) * C]
                xcv = xc[:, bufx * C:(bufx + 1) * C]
                s0, s1, s2, s3, s4 = (8 * bufx + k for k in range(5))
                tok_sum = kb.emit("act", lambda e, xv=xv, xcv=xcv, s0=s0: e.activation(
                    xcv, xv, AF.Copy, bias=0.0, scale=1.0,
                    accum_out=stats[:, s0:s0 + 1]),
                    waits=[tok_x] +
                          ([ln_rstd[bufx]] if ln_rstd[bufx] else []) +
                          ([norm_of[it - 2]] if it >= 2 else []))
                kb.emit("dve", lambda e, s0=s0, s1=s1: e.tensor_scalar(
                    stats[:, s1:s1 + 1], stats[:, s0:s0 + 1], -1.0 / C, None, OP.mult),
                    waits=[tok_sum])
                tok_xc = kb.emit("dve", lambda e, xv=xv, xcv=xcv, s1=s1: e.tensor_scalar(
                    xcv, xv, stats[:, s1:s1 + 1], None, OP.add))
                tok_var = kb.emit("dve", lambda e, xv=xv, xcv=xcv, s2=s2: e.scalar_tensor_tensor(
                    xv, xcv, 1.0, xv, OP.mult, OP.mult,
                    accum_out=stats[:, s2:s2 + 1]), waits=[tok_xc])
                xld_free[bufx] = tok_var
                kb.emit("act", lambda e, s2=s2, s3=s3: e.activation(
                    stats[:, s3:s3 + 1], stats[:, s2:s2 + 1], AF.Ln, bias=epsb[:, 0:1],
                    scale=1.0 / C), waits=[tok_var, tok_eps])
                ln_rstd[bufx] = kb.emit("act", lambda e, s3=s3, s4=s4: e.activation(
                    stats[:, s4:s4 + 1], stats[:, s3:s3 + 1], AF.Exp, bias=0.0,
                    scale=-0.5))
                tok_xn = kb.emit("dve", lambda e, bufn=bufn, xcv=xcv, s4=s4: e.tensor_scalar(
                    xnbf[:, bufn * C:(bufn + 1) * C], xcv, stats[:, s4:s4 + 1],
                    None, OP.mult),
                    waits=[ln_rstd[bufx], tok_var] +
                          ([prev_tr[bufn]] if prev_tr[bufn] else []))
                norm_of[it] = tok_xn

            def emit_ln_tr(it):
                bufn = it % 6
                bt = it % 2
                wfree2 = [tr_evac[bt]] if tr_evac[bt] else []
                tok_tr = None
                for c in range(8):
                    tok_tr = kb.emit("pe", lambda e, c=c, bt=bt, bufn=bufn: e.transpose(
                        ptrs[bt][:, 128 * c:128 * (c + 1)],
                        xnbf[:, bufn * C + 128 * c: bufn * C + 128 * (c + 1)],
                        idb[:, :]),
                        waits=([norm_of[it], tok_idb] + wfree2) if c == 0 else ())
                tr_evac[bt] = kb.emit("act", lambda e, it=it, bt=bt: e.copy(
                    xnT[:, :].rearrange("p (a n) -> p a n", n=4096)[
                        :, :, 128 * it:128 * (it + 1)],
                    ptrs[bt][:, :].rearrange("p (a n) -> p a n", n=128)),
                    waits=[tok_tr])
                prev_tr[bufn] = tok_tr
                xnT_last[0] = tr_evac[bt]

            emit_wi_load(0)
            emit_wo_load(0)

            for e_i in range(E):
                for it in range(4 * e_i, 4 * e_i + 4):
                    emit_ln_ladder(it)
                emit_m_expert(e_i)
                for it in range(4 * e_i, 4 * e_i + 4):
                    emit_ln_tr(it)
            xnT_all = xnT_last[0]

        kb.barrier()
        with nc.sbuf_tensor("wq", [128, 2 * 4096], BF16) as wq, \
             nc.sbuf_tensor("vast", [128, 2 * H * VW], BF16) as vast, \
             nc.sbuf_tensor("kq_st", [128, 2 * 512], BF16) as kq_st, \
             nc.sbuf_tensor("vbias", [128, C], F32) as vbias, \
             nc.sbuf_tensor("vb_row", [1, C], F32) as vb_row, \
             nc.psum_tensor("pq0", [128, 512], F32) as pq0, \
             nc.psum_tensor("pq1", [128, 512], F32) as pq1:
            pqs = [pq0, pq1]
            tok_vbr = kb.dma("sp", "ini", lambda e: e.dma_start(
                out=vb_row[0:1, :],
                in_=bqkv_d.ap().rearrange("(a n) -> a n", a=1)[:, 2 * C:3 * C]))
            tok_vb = build_bias_row(vbias, pq0, pq1, vb_row, tok_vbr)
            wq_free = [None, None]
            wq_tok = [None] * 6
            evac_tok = [None, None]
            kq_dma = [None, None]
            for j in range(4):
                bw = j % 2
                wfree = [wq_free[bw]] if wq_free[bw] else []
                wq_tok[j] = kb.dma("pool", f"wq{bw}", lambda e, j=j, bw=bw: e.dma_start(
                    out=wq[:, bw * 4096:(bw + 1) * 4096].rearrange("p (c n) -> p c n", n=512),
                    in_=wqkv_d[:, 512 * j:512 * (j + 1)].rearrange("(c p) n -> p c n", p=128)),
                    waits=wfree)
                nch = 4 if j < 2 else 8
                t0 = T0 if j < 2 else 0
                for m in range(4):
                    pair = 4 * (j % 2) + m
                    bcol = 4 * j + m
                    for n in range(nch):
                        bp = (m * nch + n) % 2
                        wfree2 = [evac_tok[bp]] if evac_tok[bp] else []
                        tok_mm = None
                        for kc in range(8):
                            tok_mm = kb.emit("pe", lambda e, bp=bp, bw=bw, kc=kc, m=m, n=n, t0=t0: e.matmul(
                                pqs[bp][:, :],
                                wq[:, bw * 4096 + 512 * kc + 128 * m: bw * 4096 + 512 * kc + 128 * (m + 1)],
                                xnT[:, 4096 * kc + t0 + 512 * n: 4096 * kc + t0 + 512 * (n + 1)],
                                start=(kc == 0), stop=(kc == 7)),
                                waits=[wq_tok[j], xnT_all, tok_vb] + wfree2)
                        wq_free[bw] = tok_mm
                        wst = [kq_dma[bp]] if kq_dma[bp] else []
                        evac_tok[bp] = kb.emit("dve", lambda e, bp=bp, bcol=bcol: e.tensor_scalar(
                            kq_st[:, bp * 512:(bp + 1) * 512], pqs[bp][:, :],
                            bqkT[:, bcol:bcol + 1], None, OP.add),
                            waits=[tok_mm, tok_bq] + wst)
                        if j < 2:
                            dst = q_d[pair, :, 512 * n: 512 * (n + 1)]
                        else:
                            dst = kt_d[pair, :, 512 * n: 512 * (n + 1)]
                        kq_dma[bp] = kb.dma("sp", f"kq{bp}", lambda e, dst=dst, bp=bp: e.dma_start(
                            out=dst, in_=kq_st[:, bp * 512:(bp + 1) * 512]),
                            waits=[evac_tok[bp]])
            for j in (4, 5):
                bw = j % 2
                wfree = [wq_free[bw]] if wq_free[bw] else []
                wq_tok[j] = kb.dma("pool", f"wq{bw}", lambda e, j=j, bw=bw: e.dma_start(
                    out=wq[:, bw * 4096:(bw + 1) * 4096].rearrange("p (c n) -> p c n", n=512),
                    in_=wqkv_d[:, 512 * j:512 * (j + 1)].rearrange("(c p) n -> p c n", p=128)),
                    waits=wfree)
            tok_vones = kb.emit("dve", lambda e: e.memset(
                vast[:, :].rearrange("p (g c) -> p g c", c=VW)[:, :, 64:65], 1.0))
            va_store = [None, None]
            for it in range(NT):
                bv = it % 2
                ev_toks = []
                for jj in (4, 5):
                    bw = jj % 2
                    bp = (it * 2 + (jj - 4)) % 2
                    wfree2 = [evac_tok[bp]] if evac_tok[bp] else []
                    tok_mm = None
                    for kc in range(8):
                        tok_mm = kb.emit("pe", lambda e, bp=bp, bw=bw, kc=kc, it=it: e.matmul(
                            pqs[bp][:, :],
                            xnT[:, 4096 * kc + 128 * it: 4096 * kc + 128 * (it + 1)],
                            wq[:, bw * 4096 + 512 * kc: bw * 4096 + 512 * (kc + 1)],
                            start=(kc == 0), stop=(kc == 7)),
                            waits=[wq_tok[jj], xnT_all] + wfree2)
                    wq_free[bw] = tok_mm
                    wst = [va_store[bv]] if va_store[bv] else []
                    g0 = 8 * (jj - 4)
                    evac_tok[bp] = kb.emit("dve", lambda e, bp=bp, bv=bv, g0=g0, jj=jj: e.tensor_tensor(
                        vast[:, bv * H * VW:(bv + 1) * H * VW].rearrange(
                            "p (g c) -> p g c", c=VW)[:, g0:g0 + 8, 0:64],
                        pqs[bp][:, :].rearrange("p (g c) -> p g c", c=64),
                        vbias[:, 512 * (jj - 4):512 * (jj - 3)].rearrange(
                            "p (g c) -> p g c", c=64),
                        OP.add),
                        waits=[tok_mm, tok_vones, tok_vb] + wst)
                    ev_toks.append(evac_tok[bp])
                va_store[bv] = kb.dma("sp", f"vas{bv}", lambda e, it=it, bv=bv: e.dma_start(
                    out=va_d[it, :, :], in_=vast[:, bv * H * VW:(bv + 1) * H * VW]),
                    waits=ev_toks)
    kb.barrier()

    msb_stack = contextlib.ExitStack()
    msb = msb_stack.enter_context(nc.sbuf_tensor("msb", [128, 4 * E * C], BF16))
    m_tok = [None] * E
    with contextlib.ExitStack() as p2s:
        _e = p2s.enter_context
        wproj_sb = _e(nc.sbuf_tensor("wproj_sb", [128, E * C], BF16))
        bprojB = _e(nc.sbuf_tensor("bprojB", [128, C], F32))
        var = _e(nc.sbuf_tensor("var", [128, 17 * H * VW], BF16))
        ktr = _e(nc.sbuf_tensor("ktr", [128, 17 * 1024], BF16))
        qtr = _e(nc.sbuf_tensor("qtr", [128, 2 * 1024], BF16))
        mkld = _e(nc.sbuf_tensor("mkld", [128, 2 * NB * 128], BF16))
        psb = _e(nc.sbuf_tensor("psb", [128, 3 * NB * 128], BF16))
        attnA = _e(nc.sbuf_tensor("attnA", [128, 2 * 64], BF16))
        attnT = _e(nc.sbuf_tensor("attnT", [128, 2 * 1024], BF16))
        zr = _e(nc.sbuf_tensor("zr", [128, 2], F32))
        xre = _e(nc.sbuf_tensor("xre", [128, 2 * C], F32))
        projsb = _e(nc.sbuf_tensor("projsb", [128, C], F32))
        t1sb = _e(nc.sbuf_tensor("t1sb", [128, C], F32))
        x1sb = _e(nc.sbuf_tensor("x1sb", [128, 2 * C], F32))
        tok_wp = kb.dma("pool", "wpj", lambda e: e.dma_start(
            out=wproj_sb[:, :].rearrange("p (c n) -> p c n", n=C),
            in_=wproj_d.ap().rearrange("(c p) n -> p c n", p=128)))
        for e_i in range(4):
            m_tok[e_i] = kb.dma("act", f"mlv{e_i}", lambda e, e_i=e_i: e.dma_start(
                out=msb[:, e_i * E * C:(e_i + 1) * E * C].rearrange(
                    "p (c n) -> p c n", n=C),
                in_=m_d[e_i, :, :].rearrange("(c p) n -> p c n", p=128)))
        va_tok = {}
        kt_tok = {}
        slot_user = {}
        ktslot_user = {}
        qt_tok = [None, None]
        qt_used = [None, None]
        sps0 = _e(nc.psum_tensor("sps0", [128, 1024], F32))
        sps1 = _e(nc.psum_tensor("sps1", [128, 1024], F32))
        pvp0 = _e(nc.psum_tensor("pvp0", [128, 512], F32))
        pvp1 = _e(nc.psum_tensor("pvp1", [128, 512], F32))
        trp2 = _e(nc.psum_tensor("trp2", [64, 1024], BF16))
        prj = _e(nc.psum_tensor("prj", [128, 512], F32))
        if True:
            tok_bpb = build_bias_row(bprojB, pvp0, pvp1, bp_row, tok_bpr)
            sps = [sps0, sps1]
            pvp = [pvp0, pvp1]
            exp_tok = [None, None]
            msk_sl = [None, None, None]
            dve_norm = [None, None]
            attnA_user = [None, None]
            norm_tok = [None, None]
            tr_ev = None
            attnT_done = [None, None]
            proj_evac_last = None
            projsb_free = None
            x1_store = [None, None]
            xre_used = [None, None]
            xre_tok = [None, None]
            mk_tok = [None, None]
            mk_used = [None, None]

            def emit_scores(i, h):
                bi, bh = i % 2, h % 2
                vi = 16 + i
                bs = [vi - d for d in (16, 8, 4, 2, 1, 0)]
                pair, pb = h >> 1, 64 * (h & 1)
                wsc = ([exp_tok[bh]] if exp_tok[bh] else []) + [qt_tok[bi]]
                tok_sc0 = None
                for idx, b in enumerate(bs):
                    slot = b % 17
                    tok_sc0 = kb.emit("pe", lambda e, bh=bh, idx=idx, slot=slot, pair=pair, pb=pb, bi=bi: e.matmul(
                        sps[bh][:, 128 * idx:128 * (idx + 1)],
                        ktr[pb:pb + 64, slot * 1024 + 128 * pair: slot * 1024 + 128 * (pair + 1)],
                        qtr[pb:pb + 64, bi * 1024 + 128 * pair: bi * 1024 + 128 * (pair + 1)],
                        start=(idx in (0, 4)), stop=(idx in (3, 5))),
                        waits=wsc + [kt_tok[b]])
                    ktslot_user[slot] = tok_sc0
                qt_used[bi] = tok_sc0
                sl = h % 3
                exp_tok[bh] = kb.emit("act", lambda e, bh=bh, sl=sl: e.activation(
                    psb[:, sl * 768: sl * 768 + 768], sps[bh][:, 0:768],
                    AF.Exp, bias=0.0, scale=SCALE),
                    waits=[tok_sc0, msk_sl[sl]])

            def emit_mask(i, h):
                bi, bh = i % 2, h % 2
                sl = h % 3
                msk_sl[sl] = kb.emit("dve", lambda e, sl=sl, bi=bi: e.tensor_mul(
                    psb[:, sl * 768: sl * 768 + 768],
                    psb[:, sl * 768: sl * 768 + 768],
                    mkld[:, bi * 768: bi * 768 + 768]),
                    waits=[exp_tok[bh], mk_tok[bi]])
                mk_used[bi] = msk_sl[sl]

            def emit_pv(i, h):
                bh = h % 2
                vi = 16 + i
                bs = [vi - d for d in (16, 8, 4, 2, 1, 0)]
                wpv = ([dve_norm[bh]] if dve_norm[bh] else []) + [msk_sl[h % 3], tok_bpb] + \
                    [va_tok[b] for b in bs]
                tok_pv = None
                sl = h % 3
                for idx, b in enumerate(bs):
                    slot = b % 17
                    tok_pv = kb.emit("pe", lambda e, bh=bh, sl=sl, idx=idx, slot=slot, h=h: e.matmul(
                        pvp[bh][:, 0:VW],
                        psb[:, sl * 768 + 128 * idx: sl * 768 + 128 * (idx + 1)],
                        var[:, slot * H * VW + VW * h: slot * H * VW + VW * (h + 1)],
                        start=(idx == 0), stop=(idx == 5)),
                        waits=wpv)
                    slot_user[slot] = tok_pv
                kb.emit("dve", lambda e, bh=bh: e.reciprocal(
                    zr[:, bh:bh + 1], pvp[bh][:, 64:65]), waits=[tok_pv])
                wn = [attnA_user[bh]] if attnA_user[bh] else []
                norm_tok[bh] = kb.emit("dve", lambda e, bh=bh: e.tensor_scalar(
                    attnA[:, bh * 64:(bh + 1) * 64], pvp[bh][:, 0:64],
                    zr[:, bh:bh + 1], None, OP.mult), waits=wn)
                dve_norm[bh] = norm_tok[bh]

            def emit_tr(i, h):
                nonlocal tr_ev
                bi, bh = i % 2, h % 2
                pair, pb = h >> 1, 64 * (h & 1)
                wtr = [tr_ev] if tr_ev else []
                tok_tr2 = kb.emit("pe", lambda e, bh=bh: e.transpose(
                    trp2[:, 0:128], attnA[:, bh * 64:(bh + 1) * 64], idb[:, :]),
                    waits=[norm_tok[bh]] + wtr)
                attnA_user[bh] = tok_tr2
                tr_ev = kb.emit("act", lambda e, bi=bi, pb=pb, pair=pair: e.copy(
                    attnT[pb:pb + 64, bi * 1024 + 128 * pair: bi * 1024 + 128 * (pair + 1)],
                    trp2[0:64, 0:128]), waits=[tok_tr2])
                attnT_done[bi] = tr_ev

            def emit_proj_half(ip, n):
                nonlocal proj_evac_last
                bip = ip % 2
                tok_pj = None
                for kc in range(8):
                    tok_pj = kb.emit("pe", lambda e, bip=bip, kc=kc, n=n: e.matmul(
                        prj[:, :],
                        attnT[:, bip * 1024 + 128 * kc: bip * 1024 + 128 * (kc + 1)],
                        wproj_sb[:, 1024 * kc + 512 * n: 1024 * kc + 512 * (n + 1)],
                        start=(kc == 0), stop=(kc == 7)),
                        waits=[attnT_done[bip], tok_wp] +
                              ([proj_evac_last] if proj_evac_last else []))
                wev = [projsb_free] if (n == 0 and projsb_free) else []
                proj_evac_last = kb.emit("act", lambda e, n=n: e.copy(
                    projsb[:, 512 * n:512 * (n + 1)], prj[:, :]),
                    waits=[tok_pj] + wev)

            def emit_x1(ip):
                nonlocal projsb_free
                bip = ip % 2
                tok_t1 = kb.emit("pool", lambda e: e.tensor_add(
                    t1sb[:, :], projsb[:, :], bprojB[:, :]),
                    waits=[proj_evac_last, tok_bpb])
                projsb_free = tok_t1
                prev_x1 = [x1_store[bip]] if x1_store[bip] else []
                xre_used[bip] = kb.emit("pool", lambda e, bip=bip: e.tensor_add(
                    x1sb[:, bip * C:(bip + 1) * C], t1sb[:, :],
                    xre[:, bip * C:(bip + 1) * C]),
                    waits=[tok_t1, xre_tok[bip]] + prev_x1)
                x1_store[bip] = kb.dma("sp", f"x1s{bip}", lambda e, ip=ip, bip=bip: e.dma_start(
                    out=x1_d[128 * ip:128 * (ip + 1), :],
                    in_=x1sb[:, bip * C:(bip + 1) * C]), waits=[xre_used[bip]])

            for i in range(OWN):
                vi = 16 + i
                bs = [vi - d for d in (16, 8, 4, 2, 1, 0)]
                bi = i % 2
                for b in bs:
                    if b not in va_tok:
                        slot = b % 17
                        wfree = [slot_user[slot]] if slot in slot_user else []
                        va_tok[b] = kb.dma("sp", f"val{slot % 4}", lambda e, b=b, slot=slot: e.dma_start(
                            out=var[:, slot * H * VW:(slot + 1) * H * VW],
                            in_=va_d[b, :, :]), waits=wfree)
                    if b not in kt_tok:
                        slot = b % 17
                        wfree = [ktslot_user[slot]] if slot in ktslot_user else []
                        kt_tok[b] = kb.dma("sp", f"ktl{slot % 4}", lambda e, b=b, slot=slot: e.dma_start(
                            out=ktr[:, slot * 1024:(slot + 1) * 1024].rearrange(
                                "p (a n) -> p a n", n=128),
                            in_=kt_d[:, :, 128 * b:128 * (b + 1)].rearrange(
                                "a p n -> p a n")), waits=wfree)
                wqt = [qt_used[bi]] if qt_used[bi] else []
                qt_tok[bi] = kb.dma("sp", f"qt{bi}", lambda e, i=i, bi=bi: e.dma_start(
                    out=qtr[:, bi * 1024:(bi + 1) * 1024].rearrange("p (a n) -> p a n", n=128),
                    in_=q_d[:, :, 128 * i:128 * (i + 1)].rearrange("a p n -> p a n")),
                    waits=wqt)
                wmk = [mk_used[bi]] if mk_used[bi] else []
                mk_tok[bi] = kb.dma("sp", f"mk{bi}", lambda e, i=i, bi=bi: e.dma_start(
                    out=mkld[:, bi * NB * 128:(bi + 1) * NB * 128], in_=mk_d[i, :, :]),
                    waits=wmk)
                wxf = [xre_used[bi]] if xre_used[bi] else []
                xre_tok[bi] = kb.dma("sp", f"xre{bi}", lambda e, i=i, bi=bi: e.dma_start(
                    out=xre[:, bi * C:(bi + 1) * C],
                    in_=x_d[T0 + 128 * i: T0 + 128 * (i + 1), :]), waits=wxf)
                for step in range(H + 3):
                    if step == 0 and i > 0:
                        emit_proj_half(i - 1, 0)
                    if step == 8 and i > 0:
                        emit_proj_half(i - 1, 1)
                        emit_x1(i - 1)
                    if step < H:
                        emit_scores(i, step)
                    if 2 <= step <= H + 1:
                        emit_pv(i, step - 2)
                    if step < H:
                        emit_mask(i, step)
                    if step >= 3:
                        emit_tr(i, step - 3)
            emit_proj_half(OWN - 1, 0)
            emit_proj_half(OWN - 1, 1)
            emit_x1(OWN - 1)
    kb.barrier()

    with nc.sbuf_tensor("xn1T", [128, OWN * C], BF16) as xn1T, \
         nc.sbuf_tensor("yacc", [128, OWN * C], F32) as yacc, \
         nc.sbuf_tensor("x1ld", [128, 2 * C], F32) as x1ld, \
         nc.sbuf_tensor("xc2", [128, 2 * C], F32) as xc2, \
         nc.sbuf_tensor("xn1bf", [128, 4 * C], BF16) as xn1bf, \
         nc.sbuf_tensor("st2", [128, 16], F32) as st2, \
         nc.sbuf_tensor("wall", [128, OWN * E], F32) as wall, \
         nc.sbuf_tensor("el", [128, 24], F32) as el, \
         nc.sbuf_tensor("outsb", [128, 2 * C], F32) as outsb:
        x1_free = [None, None]
        with nc.psum_tensor("py0", [128, 1024], F32) as py0, \
             nc.psum_tensor("py1", [128, 1024], F32) as py1:
            ev_tok = [[None, None], [None, None]]
            group_last_mm = [None] * 4
            pys = [(py0, py1), None]

            def emit_y_tile(g, i, bg):
                wmm = [t for t in ev_tok[bg] if t]
                tok_y = [None, None]
                for kc in range(8):
                    for e_loc in range(2):
                        e_g = 2 * g + e_loc
                        ms = (e_g % 4) * E * C
                        for n in range(2):
                            tok_y[e_loc] = kb.emit("pe", lambda e, bg=bg, e_loc=e_loc, i=i, kc=kc, n=n, ms=ms: e.matmul(
                                pys[bg][e_loc][:, 512 * n:512 * (n + 1)],
                                xn1T[:, 1024 * i + 128 * kc: 1024 * i + 128 * (kc + 1)],
                                msb[:, ms + 1024 * kc + 512 * n: ms + 1024 * kc + 512 * (n + 1)],
                                start=(kc == 0), stop=(kc == 7)),
                                waits=[m_tok[2 * g + e_loc]] + (wmm if kc == 0 else []))
                for e_loc in range(2):
                    e_g = 2 * g + e_loc
                    if e_g == 0:
                        ev = kb.emit("dve", lambda e, bg=bg, e_loc=e_loc, i=i, e_g=e_g: e.tensor_scalar(
                            yacc[:, C * i:C * (i + 1)], pys[bg][e_loc][:, :],
                            wall[:, E * i + e_g:E * i + e_g + 1], None, OP.mult),
                            waits=[tok_y[e_loc]])
                    else:
                        ev = kb.emit("dve", lambda e, bg=bg, e_loc=e_loc, i=i, e_g=e_g: e.scalar_tensor_tensor(
                            yacc[:, C * i:C * (i + 1)], pys[bg][e_loc][:, :],
                            wall[:, E * i + e_g:E * i + e_g + 1],
                            yacc[:, C * i:C * (i + 1)], OP.mult, OP.add),
                            waits=[tok_y[e_loc]])
                    ev_tok[bg][e_loc] = ev
                return tok_y[1]

            with nc.psum_tensor("ptr3a", [128, 1024], BF16) as ptr3a, \
                 nc.psum_tensor("ptr3b", [128, 1024], BF16) as ptr3b, \
                 nc.psum_tensor("prt", [128, 512], F32) as prt:
                ptrs3 = [ptr3a, ptr3b]
                tr_ev3 = [None, None]
                last_norm2 = [None]
                norm3_of = {}
                rt_done = None
                prev_rt_w = None
                prev_rstd2 = [None, None]
                prev_last_tr3 = [None, None, None, None]
                def p3_lad(i):
                    bi = i % 2
                    wfree = [x1_free[bi]] if x1_free[bi] else []
                    tok_x1 = kb.dma("sp", f"x1l{bi}", lambda e, i=i, bi=bi: e.dma_start(
                        out=x1ld[:, bi * C:(bi + 1) * C],
                        in_=x1_d[128 * i:128 * (i + 1), :]), waits=wfree)
                    xv = x1ld[:, bi * C:(bi + 1) * C]
                    xcv = xc2[:, bi * C:(bi + 1) * C]
                    s0, s1, s2, s3, s4 = (8 * bi + k for k in range(5))
                    tok_sum = kb.emit("act", lambda e, xv=xv, xcv=xcv, s0=s0: e.activation(
                        xcv, xv, AF.Copy, bias=0.0, scale=1.0,
                        accum_out=st2[:, s0:s0 + 1]),
                        waits=[tok_x1] +
                              ([prev_rstd2[bi]] if prev_rstd2[bi] else []) +
                              ([norm3_of[i - 2]] if i >= 2 else []))
                    kb.emit("dve", lambda e, s0=s0, s1=s1: e.tensor_scalar(
                        st2[:, s1:s1 + 1], st2[:, s0:s0 + 1], -1.0 / C, None, OP.mult),
                        waits=[tok_sum])
                    kb.emit("dve", lambda e, xv=xv, xcv=xcv, s1=s1: e.tensor_scalar(
                        xcv, xv, st2[:, s1:s1 + 1], None, OP.add))
                    tok_var = kb.emit("dve", lambda e, xv=xv, xcv=xcv, s2=s2: e.scalar_tensor_tensor(
                        xv, xcv, 1.0, xv, OP.mult, OP.mult,
                        accum_out=st2[:, s2:s2 + 1]))
                    x1_free[bi] = tok_var
                    kb.emit("act", lambda e, s2=s2, s3=s3: e.activation(
                        st2[:, s3:s3 + 1], st2[:, s2:s2 + 1], AF.Ln, bias=epsb[:, 0:1],
                        scale=1.0 / C), waits=[tok_var, tok_eps])
                    prev_rstd2[bi] = kb.emit("act", lambda e, s3=s3, s4=s4: e.activation(
                        st2[:, s4:s4 + 1], st2[:, s3:s3 + 1], AF.Exp, bias=0.0,
                        scale=-0.5))
                    bn = i % 4
                    tok_xn1 = kb.emit("dve", lambda e, bn=bn, xcv=xcv, s4=s4: e.tensor_scalar(
                        xn1bf[:, bn * C:(bn + 1) * C], xcv, st2[:, s4:s4 + 1],
                        None, OP.mult),
                        waits=[prev_rstd2[bi], tok_var] +
                              ([prev_last_tr3[bn]] if prev_last_tr3[bn] else []))
                    norm3_of[i] = tok_xn1
                def p3_work_a(i):
                    bi = i % 2
                    bn = i % 4
                    bt = i % 2
                    wfree2 = [tr_ev3[bt]] if tr_ev3[bt] else []
                    tok_tr = None
                    for c in range(8):
                        tok_tr = kb.emit("pe", lambda e, c=c, bt=bt, bn=bn: e.transpose(
                            ptrs3[bt][:, 128 * c:128 * (c + 1)],
                            xn1bf[:, bn * C + 128 * c: bn * C + 128 * (c + 1)], idb[:, :]),
                            waits=([norm3_of[i], tok_idb] + wfree2) if c == 0 else ())
                    prev_last_tr3[bn] = tok_tr
                    tr_ev3[bt] = kb.emit("act", lambda e, i=i, bt=bt: e.copy(
                        xn1T[:, 1024 * i: 1024 * (i + 1)], ptrs3[bt][:, :]),
                        waits=[tok_tr])
                    tr_done[i] = tr_ev3[bt]

                def p3_work_b(i):
                    nonlocal rt_done, prev_rt_w
                    bi = i % 2
                    tr_last = tr_done[i]
                    wrt = [rt_done] if rt_done else []
                    tok_rt = None
                    for kc in range(8):
                        tok_rt = kb.emit("pe", lambda e, i=i, kc=kc: e.matmul(
                            prt[:, 0:E],
                            xn1T[:, 1024 * i + 128 * kc: 1024 * i + 128 * (kc + 1)],
                            wr_sb[:, E * kc: E * (kc + 1)],
                            start=(kc == 0), stop=(kc == 7)),
                            waits=[tr_last, tok_wr] + wrt)
                    tok_el = kb.emit("act", lambda e: e.activation(
                        el[:, 0:8], prt[:, 0:E], AF.Exp, bias=0.0, scale=1.0),
                        waits=[tok_rt] + ([prev_rt_w] if prev_rt_w else []))
                    rt_done = tok_el
                    kb.emit("dve", lambda e: e.reduce_sum(st2[:, 5:6], el[:, 0:8], AX),
                            waits=[tok_el])
                    kb.emit("dve", lambda e: e.reduce_max(st2[:, 6:7], el[:, 0:8], AX))
                    kb.emit("dve", lambda e: e.tensor_scalar(
                        el[:, 8:16], el[:, 0:8], st2[:, 6:7], None, OP.is_equal))
                    kb.emit("dve", lambda e: e.tensor_scalar(
                        el[:, 8:16], el[:, 8:16], -1.0, 1.0, OP.mult, OP.add))
                    kb.emit("dve", lambda e: e.tensor_mul(el[:, 16:24], el[:, 0:8], el[:, 8:16]))
                    kb.emit("dve", lambda e: e.reduce_max(st2[:, 7:8], el[:, 16:24], AX))
                    kb.emit("dve", lambda e: e.tensor_scalar(
                        el[:, 8:16], el[:, 0:8], st2[:, 7:8], None, OP.is_ge))
                    kb.emit("dve", lambda e: e.tensor_mul(el[:, 16:24], el[:, 0:8], el[:, 8:16]))
                    kb.emit("dve", lambda e: e.reciprocal(st2[:, 5:6], st2[:, 5:6]))
                    prev_rt_w = kb.emit("dve", lambda e, i=i: e.tensor_scalar(
                        wall[:, E * i:E * (i + 1)], el[:, 16:24], st2[:, 5:6], None, OP.mult))

                    group_last_mm[0] = emit_y_tile(0, i, 0)

                tr_done = {}
                for i in range(OWN + 3):
                    if i < OWN:
                        p3_lad(i)
                    if 2 <= i < OWN + 2:
                        p3_work_a(i - 2)
                    if i >= 3:
                        p3_work_b(i - 3)

            kb.wait("pe", "act", kb.cnt["act"])
            kb.wait("pe", "dve", kb.cnt["dve"])
            with nc.psum_tensor("py2", [128, 1024], F32) as py2, \
                 nc.psum_tensor("py3", [128, 1024], F32) as py3:
                pys[1] = (py2, py3)
                for g in range(1, 4):
                    for e_loc in range(2):
                        e_g = 2 * g + e_loc
                        if m_tok[e_g] is None:
                            m_tok[e_g] = kb.dma("sp", f"ml{e_g % 4}", lambda e, e_g=e_g: e.dma_start(
                                out=msb[:, (e_g % 4) * E * C:((e_g % 4) + 1) * E * C].rearrange(
                                    "p (c n) -> p c n", n=C),
                                in_=m_d[e_g, :, :].rearrange("(c p) n -> p c n", p=128)),
                                waits=[group_last_mm[g - 2]])
                    out_store = [None, None]
                    x1_free2 = list(x1_free)

                    def emit_out_tile(i, last_ev):
                        bi = i % 2
                        wfree4 = [x1_free2[bi]] if x1_free2[bi] else []
                        tok_x1 = kb.dma("sp", f"x1l{bi}", lambda e, i=i, bi=bi: e.dma_start(
                            out=x1ld[:, bi * C:(bi + 1) * C],
                            in_=x1_d[128 * i:128 * (i + 1), :]), waits=wfree4)
                        wos = [out_store[bi]] if out_store[bi] else []
                        x1_free2[bi] = kb.emit("dve", lambda e, i=i, bi=bi: e.tensor_add(
                            outsb[:, bi * C:(bi + 1) * C], yacc[:, C * i:C * (i + 1)],
                            x1ld[:, bi * C:(bi + 1) * C]),
                            waits=[tok_x1, last_ev] + wos)
                        out_store[bi] = kb.dma("sp", f"out{bi}", lambda e, i=i, bi=bi: e.dma_start(
                            out=out_d[128 * i:128 * (i + 1), :],
                            in_=outsb[:, bi * C:(bi + 1) * C]), waits=[x1_free2[bi]])

                    for i in range(OWN):
                        bg = (g * OWN + i) % 2
                        group_last_mm[g] = emit_y_tile(g, i, bg)
                        if g == 3:
                            emit_out_tile(i, ev_tok[bg][1])
        kb.wait("sp", "out0", kb.cnt["out0"])
        kb.wait("sp", "out1", kb.cnt["out1"])

    msb_stack.close()
    kb.replay()
    kb.stack.close()
    return nc


_CACHE = {}


def kernel(L0_attn_x, L0_attn_W_qkv, L0_attn_b_qkv, L0_attn_W_proj, L0_attn_b_proj,
           L0_ffn_Wr, L0_ffn_Wi, L0_ffn_Wo):
    x = np.ascontiguousarray(np.asarray(L0_attn_x, dtype=np.float32))
    wqkv = np.ascontiguousarray(np.asarray(L0_attn_W_qkv, dtype=np.float32))
    bqkv = np.ascontiguousarray(np.asarray(L0_attn_b_qkv, dtype=np.float32))
    wproj = np.ascontiguousarray(np.asarray(L0_attn_W_proj, dtype=np.float32))
    bproj = np.ascontiguousarray(np.asarray(L0_attn_b_proj, dtype=np.float32))
    wr = np.ascontiguousarray(np.asarray(L0_ffn_Wr, dtype=np.float32))
    wi = np.ascontiguousarray(np.swapaxes(np.asarray(L0_ffn_Wi, dtype=np.float32), 1, 2))
    wo = np.ascontiguousarray(np.asarray(L0_ffn_Wo, dtype=np.float32))
    idn = np.eye(128, dtype=np.float32)
    masks = [build_masks(0), build_masks(1)]

    if "nc" not in _CACHE:
        _CACHE["nc"] = build_program()
    nc = _CACHE["nc"]

    in_maps = []
    for core in range(8):
        b, half = core >> 1, core & 1
        if half == 0:
            xv = np.concatenate(
                [np.zeros((OWN_TOK, C), np.float32), x[b, 0:OWN_TOK]], axis=0)
        else:
            xv = np.ascontiguousarray(x[b])
        in_maps.append({
            "x": xv, "wqkv": wqkv, "bqkv": bqkv, "wproj": wproj, "bproj": bproj,
            "wr": wr, "wi": wi, "wo": wo, "idn": idn, "mk": masks[half],
        })
    res = run_bass_kernel_spmd(nc, in_maps, core_ids=list(range(8)))
    out = np.empty((B, T, C), dtype=np.float32)
    for core in range(8):
        b, half = core >> 1, core & 1
        out[b, half * OWN_TOK:(half + 1) * OWN_TOK] = res.results[core]["out"]
    return out
